# version 07
# baseline (speedup 1.0000x reference)
import numpy as np
import ml_dtypes
import concourse.bass as bass
import concourse.mybir as mybir
from concourse.bass import IndirectOffsetOnAxis
from concourse.bass_utils import run_bass_kernel_spmd

F32 = mybir.dt.float32
BF16 = mybir.dt.bfloat16
I32 = mybir.dt.int32
AF = mybir.ActivationFunctionType
ALU = mybir.AluOpType
AX = mybir.AxisListType.X

P = 128
D = 2048
DC = 16
HD = 64
NH = 32
NKV = 8
DKV = 512
TOWN = 1024
TCTX = 2048
NQT = 8
NKB = 16
HFF = 1408
HC = 11
NE = 16
CAP = 256
NSLOT = NE * CAP
EPS_LN = 1e-5
EPS_RMS = 1e-6
NEG = -1e30
BIG = 1e9

SLOPES = [2.0 ** (-(8.0 / NH) * h) for h in range(NH)]


def _host_consts():
    c = {}
    bb = np.zeros((P, NH * NKB), np.float32)
    for h in range(NH):
        for b in range(NKB):
            bb[:, h * NKB + b] = SLOPES[h] * (128.0 * b + np.arange(P))
    c["bcolbase"] = bb
    cr = np.zeros((NH, 3, TOWN), np.float64)
    t = np.arange(TOWN, dtype=np.float64)
    for h in range(NH):
        v = -SLOPES[h] * (t + 1024.0)
        hi = v.astype(ml_dtypes.bfloat16).astype(np.float64)
        mid = (v - hi).astype(ml_dtypes.bfloat16).astype(np.float64)
        lo = (v - hi - mid).astype(ml_dtypes.bfloat16).astype(np.float64)
        cr[h, 0], cr[h, 1], cr[h, 2] = hi, mid, lo
    c["crow_all"] = cr.astype(ml_dtypes.bfloat16)
    e2 = np.zeros((NE, 1), np.float32)
    e2[:, 0] = CAP * np.arange(NE)
    c["e256"] = e2
    tk = np.zeros((P, NQT), np.float32)
    for j in range(NQT):
        tk[:, j] = 128 * j + np.arange(P)
    c["tokid8"] = tk
    e8 = np.zeros((NKV, NH), np.float32)
    for h in range(NH):
        e8[h // 4, h] = 1.0
    c["e832"] = e8
    hs = np.zeros((P, DC * NH), np.float32)
    for ch in range(DC):
        hs[0:64, ch * NH + 2 * ch] = 1.0
        hs[64:128, ch * NH + 2 * ch + 1] = 1.0
    c["hsel"] = hs.astype(ml_dtypes.bfloat16)
    ks = np.zeros((64, NKV * NKV), np.float32)
    for kv in range(NKV):
        ks[:, kv * NKV + kv] = 1.0
    c["ksel"] = ks.astype(ml_dtypes.bfloat16)
    c["oobinit"] = np.full((NSLOT + 1, 2), BIG, np.float32)
    return c


_CONSTS = _host_consts()


class Mem:

    class Lazy:
        __slots__ = ("ap",)

        def __init__(self):
            self.ap = None

        def __getitem__(self, idx):
            return self.ap[idx]

        def rearrange(self, *a, **k):
            return self.ap.rearrange(*a, **k)

        @property
        def shape(self):
            return self.ap.shape

    def __init__(self, nc):
        self.nc = nc
        probe = nc.alloc_sbuf_tensor("mem_base_probe", [P, 64], mybir.dt.uint8)
        self.base = nc.lookup_mloc(probe).addr
        self.limit = nc.sbuf_top - self.base - 256
        assert self.limit >= 204 * 1024, f"unexpected sbuf base {self.base}"
        self.specs = []

    def at(self, name, shape, dtype, ph0, ph1, align=64):
        lz = Mem.Lazy()
        size = int(np.prod(shape[1:])) if len(shape) > 1 else 1
        size *= mybir.dt.size(dtype)
        size = (size + align - 1) // align * align
        self.specs.append([lz, name, list(shape), dtype, size, ph0, ph1])
        return lz

    def commit(self):
        order = sorted(range(len(self.specs)),
                       key=lambda i: (-self.specs[i][4],
                                      -(self.specs[i][6] - self.specs[i][5])))
        placed = []
        for i in order:
            lz, name, shape, dtype, size, p0, p1 = self.specs[i]
            spans = sorted((o, o + sz) for (o, sz, q0, q1) in placed
                           if not (q1 < p0 or q0 > p1))
            off = 0
            for (a, b) in spans:
                if off + size <= a:
                    break
                off = max(off, b)
            assert off + size <= self.limit, (
                f"SBUF overflow placing {name}: off={off} size={size}")
            placed.append((off, size, p0, p1))
            lz.ap = self.nc.alloc_sbuf_tensor_at(
                name, shape, dtype, offset=self.base + off).ap()


class Prog:
    ENGS = ("sync", "gpsimd", "vector", "scalar", "tensor")
    NLANES = 8

    def __init__(self, nc):
        self.nc = nc
        self.ops = {e: [] for e in self.ENGS}
        self.hand = {}
        self.labels = {}
        self.dma_state = {}

    def _sem(self, key):
        if key not in self.hand:
            h = self.nc.semaphore("s_" + "_".join(str(x) for x in key)).__enter__()
            self.hand[key] = [h, 0]
        return self.hand[key]

    def cur(self, name):
        if name not in self.labels:
            return 0
        lb = self.labels[name]
        return lb["inck"] * len(lb["ev"])

    def wait(self, eng, name, val=None):
        if name not in self.labels:
            return
        lb = self.labels[name]
        k = len(lb["ev"]) if val is None else (
            val // lb["inck"] if lb["inck"] > 1 else val)
        if k <= 0:
            return
        assert k <= len(lb["ev"]), f"wait {name} k={k} > {len(lb['ev'])}"
        if lb["kind"] == "sig":
            s = self._sem(lb["key"])
            self.ops[eng].append(
                lambda E, h=s[0], v=lb["ev"][k - 1]: E.wait_ge(h, v))
        else:
            need = {}
            for (lane, v) in lb["ev"][:k]:
                need[lane] = max(need.get(lane, 0), v)
            q = lb["key"]
            for lane, v in sorted(need.items()):
                s = self._sem(("dma", q, lane))
                self.ops[eng].append(
                    lambda E, h=s[0], v=v: E.wait_ge(h, v))

    AUTOEDGE = ("vector", "scalar", "gpsimd")

    def add(self, eng, fn, inc=None, inck=1):
        if inc is None and (eng == "tensor" or eng == "sync"):
            self.ops[eng].append(fn)
            return None
        if inck == 16:
            st = self.dma_state.setdefault(eng, [0, [0] * self.NLANES])
            lane = st[0]
            st[0] = (lane + 1) % self.NLANES
            s = self._sem(("dma", eng, lane))
            prev = st[1][lane]
            if prev:
                self.ops[eng].append(
                    lambda E, h=s[0], v=prev: E.wait_ge(h, v))
            st[1][lane] = prev + 16
            s[1] += 16
            lb = self.labels.setdefault(
                inc, {"kind": "dma", "key": eng, "inck": 16, "ev": []})
            assert lb["kind"] == "dma" and lb["key"] == eng, f"label {inc}"
            lb["ev"].append((lane, st[1][lane]))
            self.ops[eng].append(
                lambda E, fn=fn, h=s[0]: fn(E).then_inc(h, 16))
            return None
        key = ("sig", eng)
        s = self._sem(key)
        if eng in self.AUTOEDGE and s[1] > 0:
            self.ops[eng].append(lambda E, h=s[0], v=s[1]: E.wait_ge(h, v))
        s[1] += inck
        if inc is not None:
            lb = self.labels.setdefault(
                inc, {"kind": "sig", "key": key, "inck": inck, "ev": []})
            assert lb["kind"] == "sig" and lb["key"] == key, f"label {inc}"
            lb["ev"].append(s[1])
        self.ops[eng].append(
            lambda E, fn=fn, h=s[0], v=inck: fn(E).then_inc(h, v))
        return s[1]

    def run(self):
        with self.nc.Block() as block:
            @block.sync
            def _(E):
                for fn in self.ops["sync"]:
                    fn(E)

            @block.gpsimd
            def _(E):
                for fn in self.ops["gpsimd"]:
                    fn(E)

            @block.vector
            def _(E):
                for fn in self.ops["vector"]:
                    fn(E)

            @block.scalar
            def _(E):
                for fn in self.ops["scalar"]:
                    fn(E)

            @block.tensor
            def _(E):
                for fn in self.ops["tensor"]:
                    fn(E)


def build_nc(debug=False, phase_limit=99):
    nc = bass.Bass("TRN2", target_bir_lowering=False)
    pg = Prog(nc)
    mem = Mem(nc)

    x_win = nc.dram_tensor("x_win", [TCTX, D], F32, kind="ExternalInput").ap()
    wqT = nc.dram_tensor("wqT", [D, D], F32, kind="ExternalInput").ap()
    wkT = nc.dram_tensor("wkT", [D, DKV], F32, kind="ExternalInput").ap()
    wvT = nc.dram_tensor("wvT", [D, DKV], F32, kind="ExternalInput").ap()
    woT = nc.dram_tensor("woT", [D, D], F32, kind="ExternalInput").ap()
    gateT = nc.dram_tensor("gateT", [D, NE], F32, kind="ExternalInput").ap()
    w1_d = nc.dram_tensor("w1", [NE, D, HFF], F32, kind="ExternalInput").ap()
    w2_d = nc.dram_tensor("w2", [NE, HFF, D], F32, kind="ExternalInput").ap()
    sw1_d = nc.dram_tensor("sw1", [D, HFF], F32, kind="ExternalInput").ap()
    sw2_d = nc.dram_tensor("sw2", [HFF, D], F32, kind="ExternalInput").ap()
    qnw_d = nc.dram_tensor("qnw_c", [P, DC], F32, kind="ExternalInput").ap()
    knw_d = nc.dram_tensor("knw_c", [P, 8], F32, kind="ExternalInput").ap()
    ctxv_d = nc.dram_tensor("ctxv", [P, P], F32, kind="ExternalInput").ap()
    bcb_d = nc.dram_tensor("bcolbase", [P, NH * NKB], F32,
                           kind="ExternalInput").ap()
    crow_d = nc.dram_tensor("crow_all", [NH, 3, TOWN], BF16,
                            kind="ExternalInput").ap()
    e256_d = nc.dram_tensor("e256", [NE, 1], F32, kind="ExternalInput").ap()
    e832_d = nc.dram_tensor("e832", [NKV, NH], F32, kind="ExternalInput").ap()
    hsel_d = nc.dram_tensor("hsel", [P, DC * NH], BF16,
                            kind="ExternalInput").ap()
    ksel_d = nc.dram_tensor("ksel", [64, NKV * NKV], BF16,
                            kind="ExternalInput").ap()
    tok_d = nc.dram_tensor("tokid8", [P, NQT], F32, kind="ExternalInput").ap()
    oob_d = nc.dram_tensor("oobinit", [NSLOT + 1, 2], F32,
                           kind="ExternalInput").ap()

    xn_tm_dr = nc.dram_tensor("xn_tm_dr", [TOWN, D], BF16, kind="Internal").ap()
    xn_fm_dr = nc.dram_tensor("xn_fm_dr", [DC, P, TOWN], BF16,
                              kind="Internal").ap()
    s_dr = nc.dram_tensor("s_dr", [NQT, P, D], BF16, kind="Internal").ap()
    slotmap = nc.dram_tensor("slotmap", [NSLOT + 1, 2], F32, kind="Internal").ap()
    ab_dr = nc.dram_tensor("ab_dr", [2 * TOWN + 1, D], BF16, kind="Internal").ap()
    gpre_dr = nc.dram_tensor("gpre_dr", [NQT, P, D], F32, kind="Internal").ap()

    o_tm = nc.dram_tensor("o_tm", [TOWN, D], F32, kind="ExternalOutput").ap()
    dbg = {}
    if debug:
        for nm, shp, dt in [("d_xn_tm", [TOWN, D], BF16),
                            ("d_q", [DC, P, TOWN], BF16),
                            ("d_qraw", [DC, P, TOWN], BF16),
                            ("d_rrq", [1, TOWN], F32),
                            ("d_rstb", [P, TOWN], BF16),
                            ("d_psqs", [1, TOWN], F32),
                            ("d_k", [NKV, 67, TCTX], BF16),
                            ("d_att", [DC, P, TOWN], BF16),
                            ("d_s", [NQT, P, D], BF16),
                            ("d_cw", [P, NE * NQT], F32),
                            ("d_sidx", [P, 2 * NQT], F32),
                            ("d_cab", [P, 2 * NQT], F32),
                            ("d_gpre", [NQT, P, D], F32)]:
            dbg[nm] = nc.dram_tensor(nm, shp, dt, kind="ExternalOutput").ap()

    ones_f = mem.at("ones_f", [1, P], F32, 0, 9)
    ones_bf = mem.at("ones_bf", [1, P], BF16, 0, 9)
    tokrow = mem.at("tokrow", [P, 4], F32, 0, 9)
    tokrow_i = mem.at("tokrow_i", [P, 2], I32, 0, 9)
    e256_s = mem.at("e256_s", [NE, 1], F32, 0, 9)
    sc_i32 = mem.at("sc_i32", [P, 1], I32, 0, 9)
    ones_col = mem.at("ones_col", [P, 1], BF16, 0, 9)
    zrow = mem.at("zrow", [1, 1024], BF16, 0, 9)
    cab = mem.at("cab", [P, 2 * NQT], F32, 0, 8)
    ident = mem.at("ident", [P, P], BF16, 0, 7)
    ident_f = mem.at("ident_f", [P, P], F32, 0, 6)
    scrs = mem.at("scrs", [P, NE * NQT], F32, 0, 6)
    cw16 = mem.at("cw16", [P, NE * NQT], F32, 0, 6)
    sidx = mem.at("sidx", [P, 2 * NQT], F32, 0, 6)
    tokid_s = mem.at("tokid_s", [P, NQT], F32, 0, 6)
    xnown = mem.at("xnown", [P, DC * 1024], BF16, 1, 6)
    bcolbase = mem.at("bcolb", [P, NH * NKB], F32, 0, 5)
    mdg = mem.at("mdg", [P, P], F32, 0, 5)
    m01t = mem.at("m01t", [P, P], BF16, 0, 5)
    ctxv_s = mem.at("ctxv_s", [P, P], F32, 0, 4)
    gate_sb = mem.at("gate_sb", [P, DC * NE], BF16, 6, 6)
    xnctx = mem.at("xnctx", [P, DC * 1024], BF16, 1, 4)
    qfm = mem.at("qfm", [P, DC * 1024], BF16, 2, 5)
    knw = mem.at("knw", [P, 8], F32, 0, 3)
    kaug = [mem.at(f"kaug{v}", [67, TCTX], BF16, 3, 5) for v in range(NKV)]
    qnw = mem.at("qnw", [P, DC], F32, 0, 2)
    gp_t = [mem.at(f"gp_t{i}", [P, D], F32, 7, 7) for i in range(2)]
    gp8 = mem.at("gp8", [P, D], F32, 8, 8)
    wq_sb = mem.at("wq_sb", [P, DC * 1024], BF16, 1, 2)
    attfm = mem.at("attfm", [P, DC * 1024], BF16, 5, 6)
    vaug = mem.at("vaug", [P, NKB * (NKV * 65)], BF16, 4, 5)
    wk_sb = mem.at("wk_sb", [P, DC * 512], BF16, 2, 3)
    wv_sb = mem.at("wv_sb", [P, DC * 512], BF16, 3, 4)
    wo_sb = [mem.at(f"wo_sb{i}", [P, 8 * 1024], BF16, 5, 6) for i in range(2)]
    rstb = mem.at("rstb", [P, 1024], BF16, 2, 3)
    w1_sb = mem.at("w1_sb", [P, DC * HFF], BF16, 7, 7)
    w2_sb = mem.at("w2_sb", [P, HC * D], BF16, 7, 7)
    wst = [mem.at(f"wst{i}", [P, D], F32, 7, 7) for i in range(2)]
    h1s = mem.at("h1s", [P, HC * 1024], BF16, 7, 7)
    wo_sb2 = [mem.at(f"wo_sb2_{i}", [P, 8 * 1024], BF16, 6, 6)
              for i in range(2)]
    ab_s = mem.at("ab_s", [P, 2 * D], F32, 8, 8)
    rr_k = mem.at("rr_k", [1, TCTX], F32, 3, 3)
    lnx0 = mem.at("lnx0", [P, D], F32, 1, 1)
    lnx1 = mem.at("lnx1", [P, D], F32, 1, 1)
    xngT = mem.at("xngT", [P, DC * CAP], BF16, 7, 7)
    x_rs = mem.at("x_rs", [P, D], F32, 8, 8)
    fin0 = mem.at("fin0", [P, D], F32, 8, 8)
    h1e = mem.at("h1e", [P, HC * CAP], BF16, 7, 7)
    rr_q = mem.at("rr_q", [1, TOWN], F32, 2, 2)
    lnxn0 = mem.at("lnxn0", [P, D], BF16, 1, 1)
    lnxn1 = mem.at("lnxn1", [P, D], BF16, 1, 1)
    s_tile = mem.at("s_tile", [P, D], BF16, 6, 6)
    rt_a = mem.at("rt_a", [NE, TOWN], F32, 6, 6)
    rt_b = mem.at("rt_b", [NE, TOWN], F32, 6, 6)
    rt_c = mem.at("rt_c", [NE, TOWN], F32, 6, 6)
    xng = [mem.at(f"xng{i}", [P, D], BF16, 7, 7) for i in range(2)]
    out2 = mem.at("out2", [P, D], BF16, 7, 7)
    s_rs = mem.at("s_rs", [P, D], BF16, 8, 8)
    fin1 = mem.at("fin1", [P, D], BF16, 8, 8)
    dbsnap = mem.at("dbsnap", [P, 1024], F32, 0, 9) if debug else None
    pt0 = mem.at("pt0", [P, 1152], BF16, 5, 5)
    pt1 = mem.at("pt1", [P, 1152], BF16, 5, 5)
    biascols = mem.at("biascols", [P, NH * NKB], F32, 5, 5)
    recip_row = mem.at("recip_row", [1, TOWN], BF16, 5, 5)
    sbR = mem.at("sbR", [64, 512], F32, 5, 5)
    qaug = mem.at("qaug", [67, TOWN], BF16, 5, 5)
    qaug1 = mem.at("qaug1", [67, TOWN], BF16, 5, 5)
    qsq_sc = mem.at("qsq_sc", [P, 1024], BF16, 2, 2)
    ksq_sc = mem.at("ksq_sc", [P, 1024], BF16, 3, 3)
    ksq_f = mem.at("ksq_f", [P, 1024], BF16, 3, 3)
    xnr = [mem.at(f"xnr{i}", [P, TOWN], BF16, 7, 7) for i in range(2)]
    qssm = mem.at("qssm", [NH, 1], F32, 5, 5)
    kssm = mem.at("kssm", [NKV, 1], F32, 5, 5)
    prod32 = mem.at("prod32", [NH, 1], F32, 5, 5)
    e832_s = mem.at("e832_s", [NKV, NH], F32, 0, 5)
    hsel_s = mem.at("hsel_s", [P, DC * NH], BF16, 0, 5)
    ksel_s = mem.at("ksel_s", [64, NKV * NKV], BF16, 0, 5)
    mcol = mem.at("mcol", [P, NH], F32, 5, 5)
    lnsc = mem.at("lnsc", [P, 16], F32, 1, 1)
    msk16 = mem.at("msk16", [P, NE], F32, 6, 6)
    cm16 = mem.at("cm16", [P, NE], F32, 6, 6)
    rk_a = mem.at("rk_a", [P, NE], F32, 6, 6)
    rk_b = mem.at("rk_b", [P, NE], F32, 6, 6)
    sel0 = mem.at("sel0", [P, NE], F32, 6, 6)
    sel1 = mem.at("sel1", [P, NE], F32, 6, 6)
    slot_tm = mem.at("slot_tm", [P, NE], F32, 6, 6)
    sltmp = mem.at("sltmp", [P, NE], F32, 6, 6)
    m8 = mem.at("m8", [P, 8], F32, 6, 6)
    pad8 = mem.at("pad8", [P, 8], F32, 6, 6)
    acpy = mem.at("acpy", [P, 8], F32, 6, 6)
    gsc = mem.at("gsc", [P, 4], F32, 6, 6)
    m32 = mem.at("m32", [NH, 1], BF16, 5, 5)
    mrow = mem.at("mrow", [1, NH], BF16, 5, 5)
    CVALS = [EPS_LN, EPS_RMS, -0.5, 1.0 / D, 1.0 / DKV, -1.0, 0.125,
             1.0, 0.0, 2.0, 1e-20, float(NSLOT), float(TOWN - 1),
             float(2 * TOWN), 0.5, -1e30, -1.01]
    constv_h = nc.alloc_sbuf_tensor_at("constv", [P, len(CVALS)], F32,
                                       offset=mem.base + mem.limit + 64)
    constv = constv_h.ap()
    for i, v in enumerate(CVALS):
        nc.const_aps.register(F32, float(v), constv[:, i:i + 1])

    def A(eng, fn, inc=None, inck=1):
        return pg.add(eng, fn, inc, inck)

    def W(eng, sem, val=None):
        pg.wait(eng, sem, val)

    def PHASE_GUARD(*labels):
        for e in (VE, SC, TE, GP, SY):
            for lb in labels:
                pg.wait(e, lb, None)

    SY, GP, VE, SC, TE = "sync", "gpsimd", "vector", "scalar", "tensor"

    for i, v in enumerate(CVALS):
        A(GP, lambda E, i=i, v=v: E.memset(constv[:, i:i + 1], float(v)))
    A(GP, lambda E: E.memset(ones_bf[:], 1.0))
    A(GP, lambda E: E.memset(ones_f[:], 1.0))
    A(GP, lambda E: E.memset(ones_col[:], 1.0))
    A(GP, lambda E: E.memset(zrow[:], 0.0))
    for idn in (ident, ident_f):
        A(GP, lambda E, idn=idn: E.memset(idn[:], 0.0), inc="cms")
        W(GP, "cms")
        A(GP, lambda E, idn=idn: E.affine_select(
            out=idn[:], in_=idn[:], compare_op=ALU.not_equal, fill=1.0,
            base=0, pattern=[[-1, P]], channel_multiplier=1))
    A(GP, lambda E: E.memset(m01t[:], 1.0), inc="cms")
    W(GP, "cms")
    A(GP, lambda E: E.affine_select(
        out=m01t[:], in_=m01t[:], compare_op=ALU.is_ge, fill=0.0,
        base=0, pattern=[[-1, P]], channel_multiplier=1))
    A(GP, lambda E: E.memset(mdg[:], 0.0), inc="cms")
    W(GP, "cms")
    A(GP, lambda E: E.affine_select(
        out=mdg[:], in_=mdg[:], compare_op=ALU.is_ge, fill=NEG,
        base=0, pattern=[[1, P]], channel_multiplier=-1), inc="cst")
    for src, dst in [(qnw_d, qnw), (knw_d, knw), (ctxv_d, ctxv_s),
                     (bcb_d, bcolbase), (e256_d, e256_s),
                     (e832_d, e832_s), (hsel_d, hsel_s), (ksel_d, ksel_s),
                     (tok_d, tokid_s)]:
        A(SY, lambda E, s=src, d=dst: E.dma_start(out=d[:], in_=s),
          inc="cld", inck=16)
    A(SY, lambda E: E.dma_start(out=slotmap, in_=oob_d), inc="cld", inck=16)
    for _e in (TE, VE, SC):
        W(_e, "cst", pg.cur("cst"))
        W(_e, "cld", pg.cur("cld"))

    def load_wq_half(half):
        if half == 1:
            W(GP, "qmm", 8)
        for c in range(DC):
            A(GP, lambda E, c=c, h=half: E.dma_start(
                out=wq_sb[:, c * 1024:(c + 1) * 1024],
                in_=wqT[c * P:(c + 1) * P, h * 1024:(h + 1) * 1024]),
              inc="wql", inck=16)

    load_wq_half(0)

    lnw_after = {}
    with nc.psum_tensor("ps_tp", [P, 1024], BF16) as ps_tp:
        for i in range(NKB):
            buf = lnx0 if i % 2 == 0 else lnx1
            xnb = lnxn0 if i % 2 == 0 else lnxn1
            pb = 8 * (i % 2)
            if i >= 2:
                W(SY, "lnc", i - 1)
            A(SY, lambda E, i=i, b=buf: E.dma_start(
                out=b[:], in_=x_win[i * P:(i + 1) * P, :]),
              inc="lnx", inck=16)
            W(VE, "lnx", 16 * (i + 1))
            A(VE, lambda E, b=buf, pb=pb: E.reduce_sum(
                lnsc[:, pb:pb + 1], b[:], axis=AX), inc="lns")
            W(SC, "lns", pg.cur("lns"))
            if i == 0:
                W(SC, "cst", pg.cur("cst"))
            if i >= 2:
                W(SC, "lnw", lnw_after.get(i - 2, 0))
                W(SC, "tpt", 2 * (i - 1))
            A(SC, lambda E, b=buf, xb=xnb, pb=pb: E.activation(
                xb[:], b[:], AF.Square, accum_out=lnsc[:, pb + 1:pb + 2]))
            A(SC, lambda E, pb=pb: E.mul(lnsc[:, pb + 2:pb + 3],
                                         lnsc[:, pb:pb + 1], 1.0 / D))
            A(SC, lambda E, pb=pb: E.mul(lnsc[:, pb + 3:pb + 4],
                                         lnsc[:, pb + 1:pb + 2], 1.0 / D),
              inc="lnm")
            W(VE, "lnm", pg.cur("lnm"))
            A(VE, lambda E, pb=pb: E.tensor_mul(
                lnsc[:, pb + 4:pb + 5], lnsc[:, pb + 2:pb + 3],
                lnsc[:, pb + 2:pb + 3]))
            A(VE, lambda E, pb=pb: E.tensor_sub(
                lnsc[:, pb + 5:pb + 6], lnsc[:, pb + 3:pb + 4],
                lnsc[:, pb + 4:pb + 5]), inc="lnv")
            W(SC, "lnv", pg.cur("lnv"))
            A(SC, lambda E, pb=pb: E.activation(
                lnsc[:, pb + 6:pb + 7], lnsc[:, pb + 5:pb + 6], AF.Ln,
                bias=EPS_LN))
            A(SC, lambda E, pb=pb: E.activation(
                lnsc[:, pb + 7:pb + 8], lnsc[:, pb + 6:pb + 7], AF.Exp,
                scale=-0.5), inc="lnr")
            W(VE, "lnr", pg.cur("lnr"))
            A(VE, lambda E, b=buf, xb=xnb, pb=pb: E.tensor_scalar(
                xb[:], b[:], lnsc[:, pb + 2:pb + 3], lnsc[:, pb + 7:pb + 8],
                ALU.subtract, ALU.mult), inc="lnc")
            if i >= 8:
                W(SY, "lnc", pg.cur("lnc"))
                A(SY, lambda E, i=i, xb=xnb: E.dma_start(
                    out=xn_tm_dr[(i - 8) * P:(i - 7) * P, :], in_=xb[:]),
                  inc="lnw", inck=16)
                if debug:
                    A(SY, lambda E, i=i, xb=xnb: E.dma_start(
                        out=dbg["d_xn_tm"][(i - 8) * P:(i - 7) * P, :],
                        in_=xb[:]), inc="lnw", inck=16)
            lnw_after[i] = pg.cur("lnw")
            for g in range(2):
                gidx = 2 * i + g
                W(TE, "lnc", pg.cur("lnc"))
                W(TE, "tpe", gidx)
                for c in range(8):
                    cc = g * 8 + c
                    A(TE, lambda E, xb=xnb, cc=cc, c=c: E.transpose(
                        ps_tp[:, c * 128:(c + 1) * 128],
                        xb[:, cc * 128:(cc + 1) * 128], ident[:]),
                      inc="tpt" if c == 7 else None)
                W(VE, "tpt", pg.cur("tpt"))
                dst = xnctx if i < 8 else xnown
                col = (i % 8) * 128
                A(VE, lambda E, g=g, d=dst, col=col: E.tensor_copy(
                    d[:].rearrange("p (c t) -> p c t", t=1024)
                       [:, g * 8:(g + 1) * 8, col:col + 128],
                    ps_tp[:].rearrange("p (c t) -> p c t", t=128)),
                  inc="tpe")
    W(SY, "tpe", pg.cur("tpe"))
    for c in range(DC):
        A(SY, lambda E, c=c: E.dma_start(
            out=xn_fm_dr[c], in_=xnown[:, c * 1024:(c + 1) * 1024]),
          inc="xfw", inck=16)

    if phase_limit <= 1:
        for _lbl in ("lnw", "dbq", "dbk", "dba", "dbc", "dbg1", "xfw",
                     "gwr", "ssp", "cld"):
            W(SY, _lbl, pg.cur(_lbl))
        mem.commit()
        with nc.allow_low_precision(reason="dev"):
            pg.run()
        return nc
    PHASE_GUARD("lnx", "lnw")
    with nc.psum_tensor("psQa", [P, 1024], F32) as psQa, \
         nc.psum_tensor("psQb", [P, 1024], F32) as psQb, \
         nc.psum_tensor("psQs", [1, 1024], F32) as psQs, \
         nc.psum_tensor("psQr", [P, 1024], F32) as psQr:
        W(TE, "tpe", pg.cur("tpe"))
        for half in range(2):
            if half == 1:
                load_wq_half(1)
            for oc in range(8):
                o = half * 8 + oc
                ps = psQa if o % 2 == 0 else psQb
                W(TE, "wql", 16 * DC * (half + 1))
                if o >= 2:
                    W(TE, "qsq", o - 1)
                    W(TE, "qev", o - 1)
                for c in range(DC):
                    for n2 in range(2):
                        A(TE, lambda E, c=c, oc=oc, n2=n2, ps=ps: E.matmul(
                            ps[:, n2 * 512:(n2 + 1) * 512],
                            wq_sb[:, c * 1024 + oc * 128:
                                  c * 1024 + (oc + 1) * 128],
                            xnown[:, c * 1024 + n2 * 512:
                                  c * 1024 + (n2 + 1) * 512],
                            start=(c == 0), stop=(c == DC - 1)),
                          inc="qmm" if (c == DC - 1 and n2 == 1) else None)
                W(VE, "qmm", pg.cur("qmm"))
                A(VE, lambda E, ps=ps, o=o: E.tensor_copy(
                    qfm[:, o * 1024:(o + 1) * 1024], ps[:]), inc="qev")
                A(VE, lambda E, o=o: E.tensor_mul(
                    qsq_sc[:], qfm[:, o * 1024:(o + 1) * 1024],
                    qfm[:, o * 1024:(o + 1) * 1024]), inc="qsq")
                W(TE, "qsq", pg.cur("qsq"))
                for n2 in range(2):
                    A(TE, lambda E, n2=n2, o=o: E.matmul(
                        psQs[:, n2 * 512:(n2 + 1) * 512], ones_col[:],
                        qsq_sc[:, n2 * 512:(n2 + 1) * 512],
                        start=(o == 0), stop=(o == 15),
                        skip_group_check=True),
                      inc="qss" if n2 == 1 else None)
        if debug:
            W(SY, "qev", pg.cur("qev"))
            for c in range(DC):
                A(SY, lambda E, c=c: E.dma_start(
                    out=dbg["d_qraw"][c], in_=qfm[:, c * 1024:(c + 1) * 1024]),
                  inc="dbqr", inck=16)
            W(SY, "dbqr", pg.cur("dbqr"))
        if phase_limit == 12:
            mem.commit()
            with nc.allow_low_precision(reason="dev"):
                pg.run()
            return nc
        if debug:
            W(SC, "qss", pg.cur("qss"))
            A(SC, lambda E: E.copy(dbsnap[0:1, 0:1024], psQs[:]), inc="dbsn")
            W(SY, "dbsn", pg.cur("dbsn"))
            A(SY, lambda E: E.dma_start(out=dbg["d_psqs"],
                                        in_=dbsnap[0:1, 0:1024]),
              inc="dbq", inck=16)
        W(SC, "qss", pg.cur("qss"))
        A(SC, lambda E: E.activation(rr_q[:], psQs[:], AF.Ln, scale=1.0 / D,
                                     bias=EPS_RMS))
        A(SC, lambda E: E.activation(rr_q[:], rr_q[:], AF.Exp, scale=-0.5),
          inc="qrs")
        W(TE, "qrs", pg.cur("qrs"))
        W(TE, "cst", pg.cur("cst"))
        for n2 in range(2):
            A(TE, lambda E, n2=n2: E.matmul(
                psQr[:, n2 * 512:(n2 + 1) * 512], ones_f[:],
                rr_q[:, n2 * 512:(n2 + 1) * 512], start=True, stop=True),
              inc="qrb" if n2 == 1 else None)
        W(SC, "qrb", pg.cur("qrb"))
        A(SC, lambda E: E.copy(rstb[:], psQr[:]), inc="qre")
        if phase_limit == 15:
            mem.commit()
            with nc.allow_low_precision(reason="dev"):
                pg.run()
            return nc
        if debug:
            W(SY, "qre", pg.cur("qre"))
            A(SY, lambda E: E.dma_start(out=dbg["d_rstb"], in_=rstb[:]),
              inc="dbq", inck=16)
            W(SY, "qrs", pg.cur("qrs"))
            A(SY, lambda E: E.dma_start(out=dbg["d_rrq"], in_=rr_q[:]),
              inc="dbq", inck=16)
        W(VE, "qre", pg.cur("qre"))
        W(VE, "qev", pg.cur("qev"))
        W(VE, "dbqr", pg.cur("dbqr"))
        for o in range(DC):
            A(VE, lambda E, o=o: E.tensor_mul(
                qfm[:, o * 1024:(o + 1) * 1024],
                qfm[:, o * 1024:(o + 1) * 1024], rstb[:]))
            A(VE, lambda E, o=o: E.tensor_scalar(
                qfm[:, o * 1024:(o + 1) * 1024],
                qfm[:, o * 1024:(o + 1) * 1024],
                qnw[:, o:o + 1], 0.125, ALU.mult, ALU.mult), inc="qsc")
    for c in range(DC):
        A(GP, lambda E, c=c: E.dma_start(
            out=wk_sb[:, c * 512:(c + 1) * 512],
            in_=wkT[c * P:(c + 1) * P, :]), inc="wkl", inck=16)
    if debug:
        W(SY, "qsc", pg.cur("qsc"))

        for c in range(DC):
            A(SY, lambda E, c=c: E.dma_start(
                out=dbg["d_q"][c], in_=qfm[:, c * 1024:(c + 1) * 1024]),
              inc="dbq", inck=16)

    if phase_limit <= 2:
        for _lbl in ("lnw", "dbq", "dbk", "dba", "dbc", "dbg1", "xfw",
                     "gwr", "ssp", "cld"):
            W(SY, _lbl, pg.cur(_lbl))
        mem.commit()
        with nc.allow_low_precision(reason="dev"):
            pg.run()
        return nc
    PHASE_GUARD("dbqr", "dbq")
    W(GP, "qsc", pg.cur("qsc"))
    for v in range(NKV):
        A(GP, lambda E, v=v: E.memset(kaug[v][64:67, :], 1.0),
          inc="kcst" if v == NKV - 1 else None)
    with nc.psum_tensor("psKa", [P, 1024], F32) as psKa, \
         nc.psum_tensor("psKb", [P, 1024], F32) as psKb, \
         nc.psum_tensor("psKs", [1, 1024], F32) as psKs, \
         nc.psum_tensor("psKr", [P, 1024], F32) as psKr:
        W(TE, "qev", pg.cur("qev"))
        W(TE, "qre", pg.cur("qre"))
        for th in range(2):
            xsrc = xnctx if th == 0 else xnown
            for oc in range(4):
                idx = th * 4 + oc
                ps = psKa if idx % 2 == 0 else psKb
                W(TE, "wkl", 16 * DC)
                if idx >= 2:
                    W(TE, "ksq", idx - 1)
                    W(TE, "kev", 2 * (idx - 1))
                for c in range(DC):
                    for n2 in range(2):
                        A(TE, lambda E, c=c, oc=oc, n2=n2, ps=ps, xs=xsrc:
                          E.matmul(
                            ps[:, n2 * 512:(n2 + 1) * 512],
                            wk_sb[:, c * 512 + oc * 128:
                                  c * 512 + (oc + 1) * 128],
                            xs[:, c * 1024 + n2 * 512:
                               c * 1024 + (n2 + 1) * 512],
                            start=(c == 0), stop=(c == DC - 1)),
                          inc="kmm" if (c == DC - 1 and n2 == 1) else None)
                W(VE, "kmm", pg.cur("kmm"))
                if idx == 0:
                    W(VE, "kcst", pg.cur("kcst"))
                for sub in range(2):
                    kv = oc * 2 + sub
                    A(VE, lambda E, ps=ps, kv=kv, sub=sub, th=th:
                      E.tensor_copy(
                        kaug[kv][0:64, th * 1024:(th + 1) * 1024],
                        ps[sub * 64:(sub + 1) * 64, :]), inc="kev")
                A(VE, lambda E, ps=ps: E.tensor_copy(ksq_f[:], ps[:]))
                A(VE, lambda E: E.tensor_mul(ksq_sc[:], ksq_f[:], ksq_f[:]),
                  inc="ksq")
                W(TE, "ksq", pg.cur("ksq"))
                for n2 in range(2):
                    A(TE, lambda E, n2=n2, oc=oc: E.matmul(
                        psKs[:, n2 * 512:(n2 + 1) * 512], ones_col[:],
                        ksq_sc[:, n2 * 512:(n2 + 1) * 512],
                        start=(oc == 0), stop=(oc == 3),
                        skip_group_check=True),
                      inc="kss" if n2 == 1 else None)
            W(SC, "kss", pg.cur("kss"))
            A(SC, lambda E, th=th: E.activation(
                rr_k[:, th * 1024:(th + 1) * 1024], psKs[:], AF.Ln,
                scale=1.0 / DKV, bias=EPS_RMS))
            A(SC, lambda E, th=th: E.activation(
                rr_k[:, th * 1024:(th + 1) * 1024],
                rr_k[:, th * 1024:(th + 1) * 1024], AF.Exp, scale=-0.5),
              inc="krs")
            W(TE, "krs", pg.cur("krs"))
            if th == 1:
                W(TE, "kre", 1)
            for n2 in range(2):
                A(TE, lambda E, th=th, n2=n2: E.matmul(
                    psKr[:, n2 * 512:(n2 + 1) * 512], ones_f[:],
                    rr_k[:, th * 1024 + n2 * 512:
                         th * 1024 + (n2 + 1) * 512],
                    start=True, stop=True, skip_group_check=True),
                  inc="krb" if n2 == 1 else None)
            W(SC, "krb", pg.cur("krb"))
            A(SC, lambda E: E.copy(rstb[:], psKr[:]), inc="kre")
            W(VE, "kre", pg.cur("kre"))
            W(VE, "kev", pg.cur("kev"))
            for kv in range(NKV):
                A(VE, lambda E, kv=kv, th=th: E.tensor_mul(
                    kaug[kv][0:64, th * 1024:(th + 1) * 1024],
                    kaug[kv][0:64, th * 1024:(th + 1) * 1024],
                    rstb[0:64, :]))
                A(VE, lambda E, kv=kv, th=th: E.tensor_scalar(
                    kaug[kv][0:64, th * 1024:(th + 1) * 1024],
                    kaug[kv][0:64, th * 1024:(th + 1) * 1024],
                    knw[0:64, kv:kv + 1],
                    None, ALU.mult), inc="ksc")
    W(GP, "qsc", pg.cur("qsc"))
    for c in range(DC):
        A(GP, lambda E, c=c: E.dma_start(
            out=wv_sb[:, c * 512:(c + 1) * 512],
            in_=wvT[c * P:(c + 1) * P, :]), inc="wvl", inck=16)
    if debug:
        W(SY, "ksc", pg.cur("ksc"))
        for kv in range(NKV):
            A(SY, lambda E, kv=kv: E.dma_start(
                out=dbg["d_k"][kv], in_=kaug[kv][:]), inc="dbk", inck=16)

    if phase_limit <= 3:
        for _lbl in ("lnw", "dbq", "dbk", "dba", "dbc", "dbg1", "xfw",
                     "gwr", "ssp", "cld"):
            W(SY, _lbl, pg.cur(_lbl))
        mem.commit()
        with nc.allow_low_precision(reason="dev"):
            pg.run()
        return nc
    PHASE_GUARD("dbk")
    with nc.psum_tensor("psVa", [P, 512], F32) as psVa, \
         nc.psum_tensor("psVb", [P, 512], F32) as psVb:
        W(TE, "kev", pg.cur("kev"))
        W(TE, "kre", pg.cur("kre"))
        W(VE, "cld")
        for st in range(NKB):
            ps = psVa if st % 2 == 0 else psVb
            W(TE, "wvl", 16 * DC)
            if st >= 2:
                W(TE, "vev", 2 * (st - 1))
            xsrc = xnctx if st < 8 else xnown
            col = (st % 8) * 128
            for c in range(DC):
                A(TE, lambda E, c=c, ps=ps, xs=xsrc, col=col: E.matmul(
                    ps[:], xs[:, c * 1024 + col: c * 1024 + col + 128],
                    wv_sb[:, c * 512:(c + 1) * 512],
                    start=(c == 0), stop=(c == DC - 1)),
                  inc="vmm" if c == DC - 1 else None)
            W(VE, "vmm", pg.cur("vmm"))
            A(VE, lambda E, ps=ps, st=st: E.tensor_copy(
                vaug[:, st * 520:(st + 1) * 520].rearrange(
                    "p (k c) -> p k c", c=65)[:, :, 0:64],
                ps[:].rearrange("p (k c) -> p k c", c=64)), inc="vev")
            A(VE, lambda E, st=st: E.tensor_copy(
                vaug[:, st * 520:(st + 1) * 520].rearrange(
                    "p (k c) -> p k c", c=65)[:, :, 64:65],
                ctxv_s[:, st * 8:(st + 1) * 8].rearrange(
                    "p (k c) -> p k c", c=1)), inc="vev")

    if phase_limit <= 4:
        for _lbl in ("lnw", "dbq", "dbk", "dba", "dbc", "dbg1", "xfw",
                     "gwr", "ssp", "cld"):
            W(SY, _lbl, pg.cur(_lbl))
        mem.commit()
        with nc.allow_low_precision(reason="dev"):
            pg.run()
        return nc
    with nc.psum_tensor("psQall", [NH, 1024], F32) as psQall, \
         nc.psum_tensor("psKall", [NKV, 2048], F32) as psKall:
        W(TE, "vev", pg.cur("vev"))
        W(TE, "qsc", pg.cur("qsc"))
        W(TE, "ksc", pg.cur("ksc"))
        W(VE, "ksc", pg.cur("ksc"))
        W(VE, "qsc", pg.cur("qsc"))
        W(VE, "vev", pg.cur("vev"))
        for kv in range(NKV):
            if kv >= 1:
                W(VE, "kbm", kv)
            A(VE, lambda E, kv=kv: E.tensor_mul(
                attfm[0:64, 0:TCTX], kaug[kv][0:64, :], kaug[kv][0:64, :]),
              inc="ks2")
            W(TE, "ks2", kv + 1)
            for i in range(4):
                A(TE, lambda E, kv=kv, i=i: E.matmul(
                    psKall[:, i * 512:(i + 1) * 512],
                    ksel_s[:, kv * NKV:(kv + 1) * NKV],
                    attfm[0:64, i * 512:(i + 1) * 512],
                    start=(kv == 0), stop=(kv == NKV - 1),
                    skip_group_check=True),
                  inc="kbm" if i == 3 else None)
        W(VE, "kbm", pg.cur("kbm"))
        A(VE, lambda E: E.reduce_max(kssm[:], psKall[:], axis=AX),
          inc="kmx")
        for c in range(DC):
            sq = pt0 if c % 2 == 0 else pt1
            if c >= 2:
                W(VE, "qbm", c - 1)
            A(VE, lambda E, c=c, sq=sq: E.tensor_mul(
                sq[:, 0:1024], qfm[:, c * 1024:(c + 1) * 1024],
                qfm[:, c * 1024:(c + 1) * 1024]), inc="qs2")
            W(TE, "qs2", c + 1)
            for n2 in range(2):
                A(TE, lambda E, c=c, n2=n2, sq=sq: E.matmul(
                    psQall[:, n2 * 512:(n2 + 1) * 512],
                    hsel_s[:, c * NH:(c + 1) * NH],
                    sq[:, n2 * 512:(n2 + 1) * 512],
                    start=(c == 0), stop=(c == DC - 1),
                    skip_group_check=True),
                  inc="qbm" if n2 == 1 else None)
        W(VE, "qbm", pg.cur("qbm"))
        A(VE, lambda E: E.reduce_max(qssm[:], psQall[:], axis=AX),
          inc="qmx")
    with nc.psum_tensor("psMB", [P, NH], F32) as psMB, \
         nc.psum_tensor("psMr", [1, NH], BF16) as psMr, \
         nc.psum_tensor("psE", [NH, 1], F32) as psE:
        W(TE, "kmx", 1)
        W(TE, "qmx", 1)
        A(TE, lambda E: E.matmul(psE[:], e832_s[:], kssm[:],
                                 start=True, stop=True,
                                 skip_group_check=True), inc="bex")
        W(VE, "bex", pg.cur("bex"))
        A(VE, lambda E: E.tensor_mul(prod32[:], qssm[:], psE[:]), inc="bpr")
        W(SC, "bpr", pg.cur("bpr"))
        A(SC, lambda E: E.activation(prod32[:], prod32[:], AF.Ln,
                                     bias=1e-20))
        A(SC, lambda E: E.activation(prod32[:], prod32[:], AF.Exp,
                                     scale=0.5))
        A(SC, lambda E: E.activation(m32[:], prod32[:], AF.Copy,
                                     scale=-1.01, bias=-0.5), inc="am3")
        W(TE, "am3", pg.cur("am3"))
        A(TE, lambda E: E.transpose(psMr[:], m32[:], ident[0:NH, 0:NH]),
          inc="amr")
        W(SC, "amr", pg.cur("amr"))
        A(SC, lambda E: E.copy(mrow[:], psMr[:]), inc="amw")
        W(TE, "amw", pg.cur("amw"))
        A(TE, lambda E: E.matmul(psMB[:], ones_bf[:], mrow[:],
                                 start=True, stop=True,
                                 skip_group_check=True), inc="am4")
        W(SC, "am4", pg.cur("am4"))
        A(SC, lambda E: E.copy(mcol[:], psMB[:]), inc="am5")
        W(VE, "am5", pg.cur("am5"))
        for h in range(NH):
            A(VE, lambda E, h=h: E.tensor_scalar(
                biascols[:, h * NKB:(h + 1) * NKB],
                bcolbase[:, h * NKB:(h + 1) * NKB],
                mcol[:, h:h + 1], None, ALU.add), inc="am6")

    W(GP, "vmm", pg.cur("vmm"))
    W(GP, "ksc", pg.cur("ksc"))
    for c in range(8):
        for i in range(2):
            A(GP, lambda E, c=c, i=i: E.dma_start(
                out=wo_sb[i][:, c * 1024:(c + 1) * 1024],
                in_=woT[c * P:(c + 1) * P, i * 1024:(i + 1) * 1024]),
              inc="wol", inck=16)

    pieces = []
    for b in range(NKB):
        j0, j1 = max(0, b - 8), min(NQT - 1, b)
        wdt = (j1 - j0 + 1) * 128
        cuts = list(range(0, wdt, 512)) + [wdt]
        pieces.append((j0, j1,
                       [(cuts[i], cuts[i + 1])
                        for i in range(len(cuts) - 1)]))
    with nc.psum_tensor("psB0", [P, 512], F32) as psB0, \
         nc.psum_tensor("psB1", [P, 512], F32) as psB1, \
         nc.psum_tensor("psB2", [P, 512], F32) as psB2, \
         nc.psum_tensor("psO0", [65, 1024], F32) as psO0, \
         nc.psum_tensor("psO1", [65, 1024], F32) as psO1, \
         nc.psum_tensor("psRc", [64, 512], F32) as psRc:
        W(TE, "amx", pg.cur("amx"))
        W(TE, "am6", pg.cur("am6"))
        W(SC, "am6", pg.cur("am6"))
        pcount = 0
        qauX = (qaug, qaug1)
        bmm_after = {}

        def stage_qaug(hh):
            qa = qauX[hh % 2]
            g = bmm_after.get(hh - 2)
            if g:
                W(VE, "bmm", g)
                W(SY, "bmm", g)
            A(VE, lambda E, hh=hh, qa=qa: E.tensor_copy(
                qa[0:64, :], qfm[64 * (hh % 2):64 * (hh % 2) + 64,
                                 (hh // 2) * 1024:(hh // 2) * 1024 + 1024]),
              inc="qau")
            A(SY, lambda E, hh=hh, qa=qa: E.dma_start(out=qa[64:67, :],
                                                      in_=crow_d[hh]),
              inc="qauD", inck=16)

        def emit_tail(hh):
            psOt = psO0 if hh % 2 == 0 else psO1
            qrowt = 64 * (hh % 2)
            qcolt = (hh // 2) * 1024
            W(TE, "pvr", hh + 1)
            for n2 in range(2):
                if 2 * hh + n2 >= 1:
                    W(TE, "pvc", 2 * hh + n2)
                A(TE, lambda E, n2=n2: E.matmul(
                    psRc[:], ones_bf[:, 0:64],
                    recip_row[:, n2 * 512:(n2 + 1) * 512],
                    start=True, stop=True, skip_group_check=True),
                  inc="pvb")
                W(SC, "pvb", pg.cur("pvb"))
                if 2 * hh + n2 >= 1:
                    W(SC, "pvd2", 2 * hh + n2)
                A(SC, lambda E, n2=n2: E.copy(sbR[:], psRc[:]), inc="pvc")
                W(VE, "pvc", pg.cur("pvc"))
                A(VE, lambda E, qrowt=qrowt, qcolt=qcolt, n2=n2, psOt=psOt:
                  E.tensor_mul(
                    attfm[qrowt:qrowt + 64,
                          qcolt + n2 * 512: qcolt + (n2 + 1) * 512],
                    psOt[0:64, n2 * 512:(n2 + 1) * 512],
                    sbR[:]), inc="pvd2")

        W(SY, "vev", pg.cur("vev"))
        W(SY, "vmm", pg.cur("vmm"))
        W(VE, "qsc", pg.cur("qsc"))
        stage_qaug(0)
        stage_qaug(1)
        psBs = (psB0, psB1, psB2)
        for h in range(NH):
            kv = h // 4
            qrow = 64 * (h % 2)
            qcol = (h // 2) * 1024
            qaug_h = qauX[h % 2]
            psO = psO0 if h % 2 == 0 else psO1
            W(TE, "qau", h + 1)
            W(TE, "qauD", 16 * (h + 1))
            if h >= 2:
                W(TE, "pvd2", 2 * (h - 1))
            for n2 in range(2):
                A(TE, lambda E, n2=n2, psO=psO: E.matmul(
                    psO[:, n2 * 512:(n2 + 1) * 512], ones_bf[0:1, 0:65],
                    zrow[0:1, n2 * 512:(n2 + 1) * 512],
                    start=True, stop=False, skip_group_check=True))
            pending_pv = None
            for b in range(NKB):
                if b == 6 and h >= 1:
                    emit_tail(h - 1)
                j0, j1, pcs = pieces[b]
                pt = pt0 if b % 2 == 0 else pt1
                for (p0, p1) in pcs:
                    ps = psBs[pcount % 3]
                    if pcount >= 3:
                        W(TE, "exb", pcount - 2)
                    A(TE, lambda E, kv=kv, b=b, j0=j0, p0=p0, p1=p1, ps=ps,
                      qa=qaug_h:
                      E.matmul(ps[:, 0:p1 - p0],
                               kaug[kv][:, b * 128:(b + 1) * 128],
                               qa[:, j0 * 128 + p0: j0 * 128 + p1],
                               start=True, stop=True,
                               skip_group_check=True), inc="bmm")
                    has_diag = (b >= 8 and p0 == 0)
                    if has_diag:
                        W(VE, "bmm", pg.cur("bmm"))
                        A(VE, lambda E, ps=ps: E.tensor_add(
                            ps[:, 0:128], ps[:, 0:128], mdg[:]), inc="bmk")
                    W(SC, "bmm", pg.cur("bmm"))
                    if has_diag:
                        W(SC, "bmk", pg.cur("bmk"))
                    A(SC, lambda E, ps=ps, pt=pt, h=h, b=b, p0=p0, p1=p1:
                      E.activation(pt[:, p0:p1], ps[:, 0:p1 - p0], AF.Exp,
                                   bias=biascols[:, h * NKB + b:
                                                 h * NKB + b + 1]),
                      inc="exb")
                    pcount += 1
                if b <= 7:
                    off = (b - j0) * 128
                    W(VE, "exb", pg.cur("exb"))
                    A(VE, lambda E, pt=pt, off=off: E.tensor_mul(
                        pt[:, off:off + 128], pt[:, off:off + 128],
                        m01t[:]), inc="bmt")

                def make_pv(b=b, j0=j0, j1=j1, kv=kv, pt=pt, psO=psO,
                            exbv=None, bmtv=None):
                    exbv = pg.cur("exb")
                    bmtv = pg.cur("bmt")

                    def emit_pv():
                        W(TE, "exb", exbv)
                        if b <= 7:
                            W(TE, "bmt", bmtv)
                        vs0 = b * 520 + kv * 65
                        wdt = (j1 - j0 + 1) * 128
                        if b <= 7:
                            ow = (b - j0) * 128
                            for q0 in range(0, ow, 512):
                                qw = min(512, ow - q0)
                                A(TE, lambda E, vs0=vs0, j0=j0, q0=q0, qw=qw,
                                  pt=pt, psO=psO:
                                  E.matmul(
                                    psO[:, j0 * 128 + q0:
                                        j0 * 128 + q0 + qw],
                                    vaug[:, vs0:vs0 + 65], pt[:, q0:q0 + qw],
                                    start=False, stop=False,
                                    skip_group_check=True))
                            A(TE, lambda E, vs0=vs0, b=b, pt=pt, ow=ow,
                              psO=psO:
                              E.matmul(
                                psO[:, b * 128:(b + 1) * 128],
                                vaug[:, vs0:vs0 + 65], pt[:, ow:ow + 128],
                                start=False, stop=False,
                                skip_group_check=True),
                              inc="pvm")
                        else:
                            a0, a1 = j0 * 128, j0 * 128 + wdt
                            cuts = ([a0] + [c for c in range(512, 2048, 512)
                                            if a0 < c < a1] + [a1])
                            for ci in range(len(cuts) - 1):
                                c0, c1 = cuts[ci], cuts[ci + 1]
                                last = (c1 == a1)
                                A(TE, lambda E, vs0=vs0, c0=c0, c1=c1, a0=a0,
                                  pt=pt, psO=psO,
                                  st_=(b == NKB - 1 and last): E.matmul(
                                    psO[:, c0:c1],
                                    vaug[:, vs0:vs0 + 65],
                                    pt[:, c0 - a0:c1 - a0],
                                    start=False, stop=st_,
                                    skip_group_check=True),
                                  inc="pvm" if last else None)
                    return emit_pv
                if pending_pv is not None:
                    pending_pv()
                pending_pv = make_pv()
            pending_pv()
            bmm_after[h] = pg.cur("bmm")
            if h + 2 < NH:
                stage_qaug(h + 2)
            W(VE, "pvm", pg.cur("pvm"))
            if h >= 1:
                W(VE, "pvb", 2 * h)
            A(VE, lambda E, psO=psO: E.reciprocal(recip_row[:],
                                                  psO[64:65, :]),
              inc="pvr")
        emit_tail(NH - 1)

    if debug:
        W(SY, "pvd2", pg.cur("pvd2"))
        for c in range(DC):
            A(SY, lambda E, c=c: E.dma_start(
                out=dbg["d_att"][c], in_=attfm[:, c * 1024:(c + 1) * 1024]),
              inc="dba", inck=16)

    if phase_limit <= 5:
        for _lbl in ("lnw", "dbq", "dbk", "dba", "dbc", "dbg1", "xfw",
                     "gwr", "ssp", "cld"):
            W(SY, _lbl, pg.cur(_lbl))
        mem.commit()
        with nc.allow_low_precision(reason="dev"):
            pg.run()
        return nc
    PHASE_GUARD("dba")
    W(GP, "pvd2", 2 * NH)
    W(GP, "exb", pg.cur("exb"))
    for c in range(8):
        for i in range(2):
            A(GP, lambda E, c=c, i=i: E.dma_start(
                out=wo_sb2[i][:, c * 1024:(c + 1) * 1024],
                in_=woT[(8 + c) * P:(9 + c) * P, i * 1024:(i + 1) * 1024]),
              inc="wol2", inck=16)
    for c in range(DC):
        A(GP, lambda E, c=c: E.dma_start(
            out=gate_sb[:, c * NE:(c + 1) * NE],
            in_=gateT[c * P:(c + 1) * P, :]), inc="wgl", inck=16)
    with nc.psum_tensor("psW0", [P, 512], F32) as psW0, \
         nc.psum_tensor("psW1", [P, 512], F32) as psW1, \
         nc.psum_tensor("psW2", [P, 512], F32) as psW2, \
         nc.psum_tensor("psW3", [P, 512], F32) as psW3:
        psWs = [psW0, psW1, psW2, psW3]
        W(TE, "pvd2", pg.cur("pvd2"))
        W(TE, "pvc", pg.cur("pvc"))
        W(TE, "exb", pg.cur("exb"))
        for tt in range(NQT):
            for og in range(4):
                ps = psWs[og]
                W(TE, "wol", 16 * 16)
                if tt == 0:
                    W(TE, "wol2", 16 * 16)
                if tt >= 1:
                    W(TE, "wev", (tt - 1) * 4 + og + 1)
                for c in range(DC):
                    wsb = (wo_sb if c < 8 else wo_sb2)[og // 2]
                    cc = c % 8
                    A(TE, lambda E, tt=tt, og=og, c=c, cc=cc, ps=ps, wsb=wsb:
                      E.matmul(
                        ps[:],
                        attfm[:, c * 1024 + tt * 128:
                              c * 1024 + (tt + 1) * 128],
                        wsb[:, cc * 1024 + (og % 2) * 512:
                            cc * 1024 + (og % 2) * 512 + 512],
                        start=(c == 0), stop=(c == DC - 1)),
                      inc="wmm" if c == DC - 1 else None)
                W(VE, "wmm", pg.cur("wmm"))
                if og == 0 and tt >= 1:
                    W(VE, "ssp", pg.cur("ssp"))
                A(VE, lambda E, ps=ps, og=og: E.tensor_copy(
                    s_tile[:, og * 512:(og + 1) * 512], ps[:]), inc="wev")
            W(SY, "wev", pg.cur("wev"))
            A(SY, lambda E, tt=tt: E.dma_start(out=s_dr[tt], in_=s_tile[:]),
              inc="ssp", inck=16)
            if debug:
                A(SY, lambda E, tt=tt: E.dma_start(
                    out=dbg["d_s"][tt], in_=s_tile[:]), inc="ssp", inck=16)

    with nc.psum_tensor("psG", [P, NE], F32) as psG, \
         nc.psum_tensor("psT1", [NE, P], F32) as psT1, \
         nc.psum_tensor("psT2", [P, NE], F32) as psT2:
        W(TE, "wev", pg.cur("wev"))
        W(TE, "wgl", 16 * DC)
        A(VE, lambda E: E.memset(pad8[:, 4:8], -1e30), inc="gpad")
        for tt in range(NQT):
            if tt >= 1:
                W(TE, "gsg", tt)
            for c in range(DC):
                A(TE, lambda E, c=c, tt=tt: E.matmul(
                    psG[:],
                    xnown[:, c * 1024 + tt * 128: c * 1024 + (tt + 1) * 128],
                    gate_sb[:, c * NE:(c + 1) * NE],
                    start=(c == 0), stop=(c == DC - 1)),
                  inc="gmm" if c == DC - 1 else None)
            W(SC, "gmm", pg.cur("gmm"))
            A(SC, lambda E, tt=tt: E.activation(
                scrs[:, tt * NE:(tt + 1) * NE], psG[:], AF.Sigmoid),
              inc="gsg")
            W(VE, "gsg", pg.cur("gsg"))
            for g in range(4):
                A(VE, lambda E, g=g, tt=tt: E.tensor_copy(
                    pad8[:, 0:4],
                    scrs[:, tt * NE + 4 * g: tt * NE + 4 * g + 4]))
                A(VE, lambda E: E.max(out=m8[:], in_=pad8[:]))
                A(VE, lambda E, g=g: E.tensor_add(
                    gsc[:, g:g + 1], m8[:, 0:1], m8[:, 1:2]))
            A(VE, lambda E: E.tensor_copy(pad8[:, 0:4], gsc[:]))
            A(VE, lambda E: E.max(out=m8[:], in_=pad8[:]), inc="gm8")
            W(SC, "gm8", pg.cur("gm8"))
            A(SC, lambda E: E.copy(acpy[:, 0:1], m8[:, 1:2]), inc="gth")
            W(VE, "gth", pg.cur("gth"))
            A(VE, lambda E: E.tensor_scalar(
                sltmp[:, 0:4] if False else gsc[:], gsc[:], acpy[:, 0:1],
                None, ALU.is_ge), inc="gmk")
            W(SC, "gmk", pg.cur("gmk"))
            A(SC, lambda E: E.copy(acpy[:, 4:8], gsc[:]), inc="gmkA")
            W(VE, "gmkA", pg.cur("gmkA"))
            for g in range(4):
                A(VE, lambda E, g=g, tt=tt: E.tensor_scalar(
                    msk16[:, 4 * g:4 * g + 4],
                    scrs[:, tt * NE + 4 * g: tt * NE + 4 * g + 4],
                    acpy[:, 4 + g:5 + g], None, ALU.mult))
            A(VE, lambda E: E.max(out=m8[:], in_=msk16[:]), inc="gt2")
            W(SC, "gt2", pg.cur("gt2"))
            A(SC, lambda E: E.copy(acpy[:, 1:3], m8[:, 0:2]), inc="gt2A")
            W(VE, "gt2A", pg.cur("gt2A"))
            A(VE, lambda E: E.tensor_add(sltmp[:, 4:5], m8[:, 0:1],
                                         m8[:, 1:2]))
            A(VE, lambda E: E.tensor_scalar(
                sltmp[:, 4:5], sltmp[:, 4:5], 1e-20, None, ALU.add))
            A(VE, lambda E: E.reciprocal(sltmp[:, 5:6], sltmp[:, 4:5]),
              inc="grc")
            W(SC, "grc", pg.cur("grc"))
            A(SC, lambda E: E.copy(acpy[:, 3:4], sltmp[:, 5:6]), inc="grcA")
            W(VE, "grcA", pg.cur("grcA"))
            A(VE, lambda E: E.tensor_scalar(
                cm16[:], msk16[:], acpy[:, 2:3], None, ALU.is_ge))
            A(VE, lambda E, tt=tt: E.tensor_mul(
                cw16[:, tt * NE:(tt + 1) * NE], msk16[:], cm16[:]))
            A(VE, lambda E, tt=tt: E.tensor_scalar(
                cw16[:, tt * NE:(tt + 1) * NE],
                cw16[:, tt * NE:(tt + 1) * NE],
                acpy[:, 3:4], None, ALU.mult), inc="gcw")
            W(TE, "gcw", pg.cur("gcw"))
            if tt >= 1:
                W(TE, "gte", tt)
            A(TE, lambda E: E.transpose(psT1[:], cm16[:], ident_f[:]),
              inc="gtp")
            W(VE, "gtp", pg.cur("gtp"))
            A(VE, lambda E, tt=tt: E.tensor_copy(
                rt_a[:, tt * 128:(tt + 1) * 128], psT1[:]), inc="gte")
        W(VE, "gte", pg.cur("gte"))
        A(VE, lambda E: E.tensor_copy(rt_c[:], rt_a[:]))
        src, dst = rt_a, rt_b
        for sh in [1, 2, 4, 8, 16, 32, 64, 128, 256, 512]:
            A(VE, lambda E, s=src, d=dst, sh=sh: E.tensor_copy(
                d[:, 0:sh], s[:, 0:sh]))
            A(VE, lambda E, s=src, d=dst, sh=sh: E.tensor_add(
                d[:, sh:TOWN], s[:, sh:TOWN], s[:, 0:TOWN - sh]))
            src, dst = dst, src
        A(VE, lambda E, s=src: E.tensor_sub(s[:], s[:], rt_c[:]))
        A(VE, lambda E, s=src, d=dst: E.tensor_scalar(
            d[:], s[:], CAP - 0.5, None, ALU.is_ge))
        A(VE, lambda E, s=src: E.tensor_scalar(
            s[:], s[:], e256_s[:, 0:1], None, ALU.add))
        A(VE, lambda E, s=src, d=dst: E.scalar_tensor_tensor(
            s[:], d[:], BIG, s[:], ALU.mult, ALU.add), inc="gps")
        slotT = src
        for tt in range(NQT):
            W(TE, "gps", pg.cur("gps"))
            if tt >= 1:
                W(TE, "gsl", tt)
            A(TE, lambda E, tt=tt, s=slotT: E.transpose(
                psT2[:], s[:, tt * 128:(tt + 1) * 128],
                ident_f[0:NE, 0:NE]), inc="gtq")
            W(VE, "gtq", pg.cur("gtq"))
            A(VE, lambda E: E.tensor_copy(slot_tm[:], psT2[:]), inc="gsl")
            A(VE, lambda E, tt=tt: E.tensor_scalar(
                cm16[:], cw16[:, tt * NE:(tt + 1) * NE], 0.0, None,
                ALU.is_gt))
            A(VE, lambda E: E.tensor_copy(rk_a[:], cm16[:]))
            ra, rb = rk_a, rk_b
            for sh in (1, 2, 4, 8):
                A(VE, lambda E, a=ra, b2=rb, sh=sh: E.tensor_copy(
                    b2[:, 0:sh], a[:, 0:sh]))
                A(VE, lambda E, a=ra, b2=rb, sh=sh: E.tensor_add(
                    b2[:, sh:NE], a[:, sh:NE], a[:, 0:NE - sh]))
                ra, rb = rb, ra
            A(VE, lambda E, r=ra: E.tensor_sub(r[:], r[:], cm16[:]))
            A(VE, lambda E, r=ra: E.tensor_mul(sel1[:], cm16[:], r[:]))
            A(VE, lambda E: E.tensor_sub(sel0[:], cm16[:], sel1[:]))
            for k, sl in ((0, sel0), (1, sel1)):
                A(VE, lambda E, sl=sl: E.tensor_mul(sltmp[:], slot_tm[:],
                                                    sl[:]))
                A(VE, lambda E, tt=tt, k=k: E.reduce_sum(
                    sidx[:, 2 * tt + k:2 * tt + k + 1], sltmp[:], axis=AX))
                A(VE, lambda E, sl=sl, tt=tt: E.tensor_mul(
                    sltmp[:], cw16[:, tt * NE:(tt + 1) * NE], sl[:]))
                A(VE, lambda E, tt=tt, k=k: E.reduce_sum(
                    cab[:, 2 * tt + k:2 * tt + k + 1], sltmp[:], axis=AX),
                  inc="gsx")
        for tt in range(NQT):
            for k in range(2):
                n = tt * 2 + k
                if n >= 1:
                    W(VE, "gsct", 16 * n)
                A(VE, lambda E, tt=tt: E.tensor_copy(
                    tokrow[:, 0:1], tokid_s[:, tt:tt + 1]))
                A(VE, lambda E, tt=tt, k=k: E.tensor_scalar(
                    tokrow[:, 1:2], tokid_s[:, tt:tt + 1], 2.0, float(k),
                    ALU.mult, ALU.add))
                A(VE, lambda E, tt=tt, k=k: E.tensor_scalar(
                    sc_i32[:], sidx[:, 2 * tt + k:2 * tt + k + 1],
                    float(NSLOT), None, ALU.min), inc="gsc1")
                W(GP, "gsc1", pg.cur("gsc1"))
                W(GP, "cld")
                A(GP, lambda E: E.indirect_dma_start(
                    out=slotmap,
                    out_offset=IndirectOffsetOnAxis(sc_i32[:], 0),
                    in_=tokrow[:, 0:2], in_offset=None),
                  inc="gsct", inck=16)
    if debug:
        W(SY, "gsx", pg.cur("gsx"))
        A(SY, lambda E: E.dma_start(out=dbg["d_cw"], in_=cw16[:]),
          inc="dbc", inck=16)
        A(SY, lambda E: E.dma_start(out=dbg["d_sidx"], in_=sidx[:]),
          inc="dbc", inck=16)
        A(SY, lambda E: E.dma_start(out=dbg["d_cab"], in_=cab[:]),
          inc="dbc", inck=16)

    if phase_limit <= 6:
        for _lbl in ("lnw", "dbq", "dbk", "dba", "dbc", "dbg1", "xfw",
                     "gwr", "ssp", "cld"):
            W(SY, _lbl, pg.cur(_lbl))
        mem.commit()
        with nc.allow_low_precision(reason="dev"):
            pg.run()
        return nc
    PHASE_GUARD("ssp", "dbc")
    def load_w1(e, wait_sem, wait_val):
        src = sw1_d if e < 0 else w1_d[e]
        if wait_val:
            W(GP, wait_sem, wait_val)
        for c in range(DC):
            A(GP, lambda E, c=c, src=src: E.dma_start(
                out=w1_sb[:, c * HFF:(c + 1) * HFF],
                in_=src[c * P:(c + 1) * P, :]), inc="mw1", inck=16)

    w2ldg = [0]

    def load_w2(e, wait_sem, wait_val):
        src = sw2_d if e < 0 else w2_d[e]
        first = True
        for c in range(HC):
            g = w2ldg[0]
            if g >= 2:
                W(SY, "w2c", g - 1)
            A(SY, lambda E, c=c, src=src, g=g: E.dma_start(
                out=wst[g % 2][:], in_=src[c * P:(c + 1) * P, :]),
              inc="w2l", inck=16)
            W(VE, "w2l", 16 * (g + 1))
            if first and wait_val:
                W(VE, wait_sem, wait_val)
                first = False
            A(VE, lambda E, c=c, g=g: E.tensor_copy(
                w2_sb[:, c * D:(c + 1) * D], wst[g % 2][:]), inc="w2c")
            w2ldg[0] += 1

    W(GP, "wev", pg.cur("wev"))
    W(GP, "gsx", pg.cur("gsx"))
    W(GP, "ssp", pg.cur("ssp"))
    load_w1(-1, None, 0)
    with nc.psum_tensor("psH0", [P, 1024], F32) as psH0, \
         nc.psum_tensor("psH1", [P, 1024], F32) as psH1, \
         nc.psum_tensor("psP0", [P, 512], F32) as psP0, \
         nc.psum_tensor("psP1", [P, 512], F32) as psP1:
        W(SY, "xfw", pg.cur("xfw"))
        W(SY, "gsx", pg.cur("gsx"))
        W(SY, "gsct", pg.cur("gsct"))
        W(TE, "gsl", NQT)
        W(TE, "gte", NQT)
        W(TE, "gsg", NQT)
        for hc in range(HC):
            ps = psH0 if hc % 2 == 0 else psH1
            W(TE, "mw1", 16 * DC)
            if hc >= 2:
                W(TE, "shev", hc - 1)
            for c in range(DC):
                n = hc * DC + c
                if n >= 2:
                    W(SY, "shmm", n - 1)
                A(SY, lambda E, c=c: E.dma_start(
                    out=xnr[c % 2][:], in_=xn_fm_dr[c]), inc="xnl", inck=16)
                W(TE, "xnl", 16 * (n + 1))
                for n2 in range(2):
                    A(TE, lambda E, c=c, hc=hc, n2=n2, ps=ps: E.matmul(
                        ps[:, n2 * 512:(n2 + 1) * 512],
                        w1_sb[:, c * HFF + hc * 128:
                              c * HFF + (hc + 1) * 128],
                        xnr[c % 2][:, n2 * 512:(n2 + 1) * 512],
                        start=(c == 0), stop=(c == DC - 1)),
                      inc="shmm" if n2 == 1 else None)
            W(SC, "shmm", pg.cur("shmm"))
            A(SC, lambda E, ps=ps, hc=hc: E.activation(
                h1s[:, hc * 1024:(hc + 1) * 1024], ps[:], AF.Silu),
              inc="shev")
        load_w2(-1, None, 0)
        W(TE, "shev", pg.cur("shev"))
        for tt in range(NQT):
            for og in range(4):
                idx = tt * 4 + og
                ps = psP0 if idx % 2 == 0 else psP1
                W(TE, "w2c", HC)
                if idx >= 2:
                    W(TE, "shp2", idx - 1)
                for c in range(HC):
                    A(TE, lambda E, c=c, tt=tt, og=og, ps=ps: E.matmul(
                        ps[:],
                        h1s[:, c * 1024 + tt * 128:
                            c * 1024 + (tt + 1) * 128],
                        w2_sb[:, c * D + og * 512: c * D + (og + 1) * 512],
                        start=(c == 0), stop=(c == HC - 1)),
                      inc="shm2" if c == HC - 1 else None)
                W(SC, "shm2", pg.cur("shm2"))
                if tt >= 2 and og == 0:
                    W(SC, "gwr", 16 * (tt - 1))
                A(SC, lambda E, ps=ps, tt=tt, og=og: E.copy(
                    gp_t[tt % 2][:, og * 512:(og + 1) * 512],
                    ps[:]), inc="shp2")
            W(SY, "shp2", pg.cur("shp2"))
            A(SY, lambda E, tt=tt: E.dma_start(
                out=gpre_dr[tt], in_=gp_t[tt % 2][:]), inc="gwr", inck=16)
        load_w1(0, "shmm", pg.cur("shmm"))

        for e in range(NE):
            for st in range(2):
                n = 2 * e + st
                W(SY, "gsct", pg.cur("gsct"))
                if n >= 1:
                    W(SY, "mti", n)
                W(GP, "lnw" if e == 0 and st == 0 else "mti",
                  16 * 8 if e == 0 and st == 0 else 0)
                A(SY, lambda E, e=e, st=st: E.dma_start(
                    out=tokrow[:, 0:2],
                    in_=slotmap[e * CAP + st * 128:
                                e * CAP + (st + 1) * 128, :]),
                  inc="mtr", inck=16)
                W(VE, "mtr", 16 * (n + 1))
                if n >= 1:
                    W(VE, "mgth", 16 * n)
                A(VE, lambda E: E.tensor_scalar(
                    tokrow_i[:, 0:1], tokrow[:, 0:1], float(TOWN - 1),
                    None, ALU.min), inc="mti")
                W(GP, "mti", pg.cur("mti"))
                if e >= 1:
                    W(GP, "mxt", 4 * e)
                A(GP, lambda E, st=st: E.indirect_dma_start(
                    out=xng[st][:], out_offset=None, in_=xn_tm_dr,
                    in_offset=IndirectOffsetOnAxis(tokrow_i[:, 0:1], 0)),
                  inc="mgth", inck=16)
            with nc.psum_tensor(f"psX{e}", [P, 1024], BF16) as psX:
                for st in range(2):
                    for g in range(2):
                        gi = 4 * e + 2 * st + g
                        W(TE, "mgth", 16 * (2 * e + st + 1))
                        W(TE, "mxe", gi)
                        for c in range(8):
                            cc = g * 8 + c
                            A(TE, lambda E, st=st, cc=cc, c=c, ps=psX:
                              E.transpose(
                                ps[:, c * 128:(c + 1) * 128],
                                xng[st][:, cc * 128:(cc + 1) * 128],
                                ident[:]), inc="mxt" if c == 7 else None)
                        W(VE, "mxt", pg.cur("mxt"))
                        A(VE, lambda E, st=st, g=g, ps=psX: E.tensor_copy(
                            xngT[:].rearrange("p (c s) -> p c s", s=CAP)
                                [:, g * 8:(g + 1) * 8,
                                 st * 128:(st + 1) * 128],
                            ps[:].rearrange("p (c s) -> p c s", s=128)),
                          inc="mxe")
            for hc in range(HC):
                ps = psP0 if hc % 2 == 0 else psP1
                W(TE, "mw1", 16 * DC * (e + 2))
                W(TE, "mxe", 4 * (e + 1))
                if hc == 0:
                    if e == 0:
                        W(TE, "shp2", pg.cur("shp2"))
                    else:
                        W(TE, "meo2", 8 * e)
                if e * HC + hc >= 2:
                    W(TE, "meev", e * HC + hc - 1)
                for c in range(DC):
                    A(TE, lambda E, c=c, hc=hc, ps=ps: E.matmul(
                        ps[:, 0:CAP],
                        w1_sb[:, c * HFF + hc * 128:
                              c * HFF + (hc + 1) * 128],
                        xngT[:, c * CAP:(c + 1) * CAP],
                        start=(c == 0), stop=(c == DC - 1)),
                      inc="memm" if c == DC - 1 else None)
                W(SC, "memm", pg.cur("memm"))
                A(SC, lambda E, ps=ps, hc=hc: E.activation(
                    h1e[:, hc * CAP:(hc + 1) * CAP], ps[:, 0:CAP],
                    AF.Silu), inc="meev")
            load_w2(e, "shm2" if e == 0 else "mem2",
                    pg.cur("shm2") if e == 0 else pg.cur("mem2"))
            for st in range(2):
                for og in range(4):
                    idx = st * 4 + og
                    ps = psP0 if idx % 2 == 0 else psP1
                    W(TE, "w2c", HC * (e + 2))
                    W(TE, "meev", (e + 1) * HC)
                    if 8 * e + idx >= 2:
                        W(TE, "meo2", 8 * e + idx - 1)
                    for c in range(HC):
                        A(TE, lambda E, c=c, st=st, og=og, ps=ps: E.matmul(
                            ps[:],
                            h1e[:, c * CAP + st * 128:
                                c * CAP + st * 128 + 128],
                            w2_sb[:, c * D + og * 512:
                                  c * D + (og + 1) * 512],
                            start=(c == 0), stop=(c == HC - 1)),
                          inc="mem2" if c == HC - 1 else None)
                    W(VE, "mem2", pg.cur("mem2"))
                    if 2 * e + st >= 1:
                        W(VE, "msct", 16 * (2 * e + st))
                    A(VE, lambda E, ps=ps, og=og: E.tensor_copy(
                        out2[:, og * 512:(og + 1) * 512], ps[:]),
                      inc="meo2")
                n2v = 2 * e + st
                W(SY, "mti2", n2v)
                A(SY, lambda E, e=e, st=st: E.dma_start(
                    out=tokrow[:, 2:4],
                    in_=slotmap[e * CAP + st * 128:
                                e * CAP + (st + 1) * 128, :]),
                  inc="mtr2", inck=16)
                W(VE, "mtr2", 16 * (n2v + 1))
                A(VE, lambda E: E.tensor_scalar(
                    tokrow_i[:, 1:2], tokrow[:, 3:4], float(2 * TOWN),
                    None, ALU.min), inc="mti2")
                W(GP, "mti2", pg.cur("mti2"))
                W(GP, "meo2", pg.cur("meo2"))
                A(GP, lambda E: E.indirect_dma_start(
                    out=ab_dr,
                    out_offset=IndirectOffsetOnAxis(tokrow_i[:, 1:2], 0),
                    in_=out2[:], in_offset=None),
                  inc="msct", inck=16)
            if e + 1 < NE:
                load_w1(e + 1, "memm", pg.cur("memm"))
    if debug:
        W(SY, "gwr", pg.cur("gwr"))
        for tt in range(NQT):
            A(SY, lambda E, tt=tt: E.dma_start(
                out=dbg["d_gpre"][tt], in_=gpre_dr[tt]),
              inc="dbg1", inck=16)

    if phase_limit <= 7:
        for _lbl in ("lnw", "dbq", "dbk", "dba", "dbc", "dbg1", "xfw",
                     "gwr", "ssp", "cld"):
            W(SY, _lbl, pg.cur(_lbl))
        mem.commit()
        with nc.allow_low_precision(reason="dev"):
            pg.run()
        return nc
    PHASE_GUARD("msct", "gwr", "dbg1", "mtr", "mtr2", "mgth")
    W(SY, "msct", pg.cur("msct"))
    W(SY, "gwr", pg.cur("gwr"))
    W(SY, "ssp", pg.cur("ssp"))
    W(GP, "msct", pg.cur("msct"))
    for tt in range(NQT):
        if tt >= 1:
            W(SY, "fo", tt)
            W(GP, "fo", tt)
        A(GP, lambda E, tt=tt: E.dma_start(
            out=ab_s[:, 0:D],
            in_=ab_dr[0:2 * TOWN, :].rearrange("(t k) d -> t k d", k=2)
                [tt * 128:(tt + 1) * 128, 0, :]), inc="flg", inck=16)
        A(GP, lambda E, tt=tt: E.dma_start(
            out=ab_s[:, D:2 * D],
            in_=ab_dr[0:2 * TOWN, :].rearrange("(t k) d -> t k d", k=2)
                [tt * 128:(tt + 1) * 128, 1, :]), inc="flg", inck=16)
        A(SY, lambda E, tt=tt: E.dma_start(out=s_rs[:], in_=s_dr[tt]),
          inc="fl", inck=16)
        A(SY, lambda E, tt=tt: E.dma_start(
            out=x_rs[:], in_=x_win[(8 + tt) * P:(9 + tt) * P, :]),
          inc="fl", inck=16)
        A(SY, lambda E, tt=tt: E.dma_start(out=gp8[:], in_=gpre_dr[tt]),
          inc="fl", inck=16)
        W(VE, "fl", pg.cur("fl"))
        W(VE, "flg", pg.cur("flg"))
        if tt >= 1:
            W(VE, "fst", 16 * tt)
        A(VE, lambda E, tt=tt: E.scalar_tensor_tensor(
            fin0[:], ab_s[:, 0:D], cab[:, 2 * tt:2 * tt + 1],
            gp8[:], ALU.mult, ALU.add))
        A(VE, lambda E, tt=tt: E.scalar_tensor_tensor(
            fin0[:], ab_s[:, D:2 * D], cab[:, 2 * tt + 1:2 * tt + 2],
            fin0[:], ALU.mult, ALU.add), inc="fg")
        W(SC, "fg", pg.cur("fg"))
        A(SC, lambda E: E.activation(fin1[:], fin0[:], AF.Silu), inc="fs")
        W(VE, "fs", pg.cur("fs"))
        A(VE, lambda E: E.tensor_mul(fin0[:], s_rs[:], fin1[:]))
        A(VE, lambda E: E.tensor_add(fin0[:], fin0[:], x_rs[:]), inc="fo")
        W(SY, "fo", pg.cur("fo"))
        A(SY, lambda E, tt=tt: E.dma_start(
            out=o_tm[tt * 128:(tt + 1) * 128, :], in_=fin0[:]),
          inc="fst", inck=16)
    W(SY, "fst", pg.cur("fst"))
    if debug:
        for s in ("dbq", "dbk", "dba", "dbc", "dbg1", "lnw"):
            W(SY, s, pg.cur(s))

    mem.commit()
    with nc.allow_low_precision(reason="bf16 matmul operands with fp32 accum; "
                                "tolerance spec allows bf16 matmuls"):
        pg.run()
    return nc


_NC_CACHE = {}


def kernel(L0_attn_x, L0_attn_wq, L0_attn_wk, L0_attn_wv, L0_attn_wo,
           L0_attn_q_norm_w, L0_attn_k_norm_w, L0_ffn_gate_w, L0_ffn_w1,
           L0_ffn_w2, L0_ffn_sw1, L0_ffn_sw2, _debug=False):
    x = np.asarray(L0_attn_x, np.float32)
    B, T, _ = x.shape
    xr = np.ascontiguousarray(x.reshape(B * T, D))

    wqT = np.ascontiguousarray(np.asarray(L0_attn_wq, np.float32).T)
    wkT = np.ascontiguousarray(np.asarray(L0_attn_wk, np.float32).T)
    wvT = np.ascontiguousarray(np.asarray(L0_attn_wv, np.float32).T)
    woT = np.ascontiguousarray(np.asarray(L0_attn_wo, np.float32).T)
    gateT = np.ascontiguousarray(np.asarray(L0_ffn_gate_w, np.float32).T)
    w1 = np.ascontiguousarray(np.asarray(L0_ffn_w1, np.float32))
    w2 = np.ascontiguousarray(np.asarray(L0_ffn_w2, np.float32))
    sw1 = np.ascontiguousarray(np.asarray(L0_ffn_sw1, np.float32)[0])
    sw2 = np.ascontiguousarray(np.asarray(L0_ffn_sw2, np.float32)[0])
    qnw_c = np.ascontiguousarray(
        np.asarray(L0_attn_q_norm_w, np.float32).reshape(DC, P).T)
    knw_c = np.zeros((P, 8), np.float32)
    knw_c[0:64, :] = np.asarray(L0_attn_k_norm_w, np.float32).reshape(8, 64).T

    key = "dbg" if _debug else "main"
    if key not in _NC_CACHE:
        _NC_CACHE[key] = build_nc(debug=_debug)
    nc = _NC_CACHE[key]

    in_maps = []
    for c in range(8):
        t0 = c * TOWN
        xw = np.zeros((TCTX, D), np.float32)
        if c % 4 == 0:
            xw[1024:] = xr[t0:t0 + TOWN]
        else:
            xw[:] = xr[t0 - 1024:t0 + TOWN]
        valid = np.zeros((P, P), np.float32)
        for st in range(NKB):
            v = 1.0 if (st >= 8 or c % 4 != 0) else 0.0
            valid[:, st * 8:(st + 1) * 8] = v
        in_maps.append({
            "x_win": xw, "wqT": wqT, "wkT": wkT, "wvT": wvT, "woT": woT,
            "gateT": gateT, "w1": w1, "w2": w2, "sw1": sw1, "sw2": sw2,
            "qnw_c": qnw_c, "knw_c": knw_c, "ctxv": valid,
            "bcolbase": _CONSTS["bcolbase"], "crow_all": _CONSTS["crow_all"],
            "e256": _CONSTS["e256"], "e832": _CONSTS["e832"],
            "hsel": _CONSTS["hsel"], "ksel": _CONSTS["ksel"],
            "tokid8": _CONSTS["tokid8"], "oobinit": _CONSTS["oobinit"],
        })

    res = run_bass_kernel_spmd(nc, in_maps, core_ids=list(range(8)))
    out = np.concatenate([np.asarray(res.results[c]["o_tm"])
                          for c in range(8)], axis=0)
    if _debug:
        return out.reshape(B, T, D), res
    return out.reshape(B, T, D)


# version 08
# speedup vs baseline: 1.0383x; 1.0383x over previous
import numpy as np
import ml_dtypes
import concourse.bass as bass
import concourse.mybir as mybir
from concourse.bass import IndirectOffsetOnAxis
from concourse.bass_utils import run_bass_kernel_spmd

F32 = mybir.dt.float32
BF16 = mybir.dt.bfloat16
I32 = mybir.dt.int32
AF = mybir.ActivationFunctionType
ALU = mybir.AluOpType
AX = mybir.AxisListType.X

P = 128
D = 2048
DC = 16
HD = 64
NH = 32
NKV = 8
DKV = 512
TOWN = 1024
TCTX = 2048
NQT = 8
NKB = 16
HFF = 1408
HC = 11
NE = 16
CAP = 256
NSLOT = NE * CAP
EPS_LN = 1e-5
EPS_RMS = 1e-6
NEG = -1e30
BIG = 1e9

SLOPES = [2.0 ** (-(8.0 / NH) * h) for h in range(NH)]


def _host_consts():
    c = {}
    bb = np.zeros((P, NH * NKB), np.float32)
    for h in range(NH):
        for b in range(NKB):
            bb[:, h * NKB + b] = SLOPES[h] * (128.0 * b + np.arange(P))
    c["bcolbase"] = bb
    cr = np.zeros((NH, 3, TOWN), np.float64)
    t = np.arange(TOWN, dtype=np.float64)
    for h in range(NH):
        v = -SLOPES[h] * (t + 1024.0)
        hi = v.astype(ml_dtypes.bfloat16).astype(np.float64)
        mid = (v - hi).astype(ml_dtypes.bfloat16).astype(np.float64)
        lo = (v - hi - mid).astype(ml_dtypes.bfloat16).astype(np.float64)
        cr[h, 0], cr[h, 1], cr[h, 2] = hi, mid, lo
    c["crow_all"] = cr.astype(ml_dtypes.bfloat16)
    e2 = np.zeros((NE, 1), np.float32)
    e2[:, 0] = CAP * np.arange(NE)
    c["e256"] = e2
    tk = np.zeros((P, NQT), np.float32)
    for j in range(NQT):
        tk[:, j] = 128 * j + np.arange(P)
    c["tokid8"] = tk
    e8 = np.zeros((NKV, NH), np.float32)
    for h in range(NH):
        e8[h // 4, h] = 1.0
    c["e832"] = e8
    hs = np.zeros((P, DC * NH), np.float32)
    for ch in range(DC):
        hs[0:64, ch * NH + 2 * ch] = 1.0
        hs[64:128, ch * NH + 2 * ch + 1] = 1.0
    c["hsel"] = hs.astype(ml_dtypes.bfloat16)
    ks = np.zeros((64, NKV * NKV), np.float32)
    for kv in range(NKV):
        ks[:, kv * NKV + kv] = 1.0
    c["ksel"] = ks.astype(ml_dtypes.bfloat16)
    c["oobinit"] = np.full((NSLOT + 1, 2), BIG, np.float32)
    return c


_CONSTS = _host_consts()


class Mem:

    class Lazy:
        __slots__ = ("ap",)

        def __init__(self):
            self.ap = None

        def __getitem__(self, idx):
            return self.ap[idx]

        def rearrange(self, *a, **k):
            return self.ap.rearrange(*a, **k)

        @property
        def shape(self):
            return self.ap.shape

    def __init__(self, nc):
        self.nc = nc
        probe = nc.alloc_sbuf_tensor("mem_base_probe", [P, 64], mybir.dt.uint8)
        self.base = nc.lookup_mloc(probe).addr
        self.limit = nc.sbuf_top - self.base - 256
        assert self.limit >= 204 * 1024, f"unexpected sbuf base {self.base}"
        self.specs = []

    def at(self, name, shape, dtype, ph0, ph1, align=64):
        lz = Mem.Lazy()
        size = int(np.prod(shape[1:])) if len(shape) > 1 else 1
        size *= mybir.dt.size(dtype)
        size = (size + align - 1) // align * align
        self.specs.append([lz, name, list(shape), dtype, size, ph0, ph1])
        return lz

    def commit(self):
        order = sorted(range(len(self.specs)),
                       key=lambda i: (-self.specs[i][4],
                                      -(self.specs[i][6] - self.specs[i][5])))
        placed = []
        for i in order:
            lz, name, shape, dtype, size, p0, p1 = self.specs[i]
            spans = sorted((o, o + sz) for (o, sz, q0, q1) in placed
                           if not (q1 < p0 or q0 > p1))
            off = 0
            for (a, b) in spans:
                if off + size <= a:
                    break
                off = max(off, b)
            assert off + size <= self.limit, (
                f"SBUF overflow placing {name}: off={off} size={size}")
            placed.append((off, size, p0, p1))
            lz.ap = self.nc.alloc_sbuf_tensor_at(
                name, shape, dtype, offset=self.base + off).ap()


class Prog:
    ENGS = ("sync", "gpsimd", "vector", "scalar", "tensor")
    NLANES = 8

    def __init__(self, nc):
        self.nc = nc
        self.ops = {e: [] for e in self.ENGS}
        self.hand = {}
        self.labels = {}
        self.dma_state = {}

    def _sem(self, key):
        if key not in self.hand:
            h = self.nc.semaphore("s_" + "_".join(str(x) for x in key)).__enter__()
            self.hand[key] = [h, 0]
        return self.hand[key]

    def cur(self, name):
        if name not in self.labels:
            return 0
        lb = self.labels[name]
        return lb["inck"] * len(lb["ev"])

    def wait(self, eng, name, val=None):
        if name not in self.labels:
            return
        lb = self.labels[name]
        k = len(lb["ev"]) if val is None else (
            val // lb["inck"] if lb["inck"] > 1 else val)
        if k <= 0:
            return
        assert k <= len(lb["ev"]), f"wait {name} k={k} > {len(lb['ev'])}"
        if lb["kind"] == "sig":
            s = self._sem(lb["key"])
            self.ops[eng].append(
                lambda E, h=s[0], v=lb["ev"][k - 1]: E.wait_ge(h, v))
        else:
            need = {}
            for (lane, v) in lb["ev"][:k]:
                need[lane] = max(need.get(lane, 0), v)
            q = lb["key"]
            for lane, v in sorted(need.items()):
                s = self._sem(("dma", q, lane))
                self.ops[eng].append(
                    lambda E, h=s[0], v=v: E.wait_ge(h, v))

    AUTOEDGE = ("vector", "scalar", "gpsimd")

    def add(self, eng, fn, inc=None, inck=1):
        if inc is None and (eng == "tensor" or eng == "sync"):
            self.ops[eng].append(fn)
            return None
        if inck == 16:
            st = self.dma_state.setdefault(eng, [0, [0] * self.NLANES])
            lane = st[0]
            st[0] = (lane + 1) % self.NLANES
            s = self._sem(("dma", eng, lane))
            prev = st[1][lane]
            if prev:
                self.ops[eng].append(
                    lambda E, h=s[0], v=prev: E.wait_ge(h, v))
            st[1][lane] = prev + 16
            s[1] += 16
            lb = self.labels.setdefault(
                inc, {"kind": "dma", "key": eng, "inck": 16, "ev": []})
            assert lb["kind"] == "dma" and lb["key"] == eng, f"label {inc}"
            lb["ev"].append((lane, st[1][lane]))
            self.ops[eng].append(
                lambda E, fn=fn, h=s[0]: fn(E).then_inc(h, 16))
            return None
        key = ("sig", eng)
        s = self._sem(key)
        if eng in self.AUTOEDGE and s[1] > 0:
            self.ops[eng].append(lambda E, h=s[0], v=s[1]: E.wait_ge(h, v))
        s[1] += inck
        if inc is not None:
            lb = self.labels.setdefault(
                inc, {"kind": "sig", "key": key, "inck": inck, "ev": []})
            assert lb["kind"] == "sig" and lb["key"] == key, f"label {inc}"
            lb["ev"].append(s[1])
        self.ops[eng].append(
            lambda E, fn=fn, h=s[0], v=inck: fn(E).then_inc(h, v))
        return s[1]

    def run(self):
        with self.nc.Block() as block:
            @block.sync
            def _(E):
                for fn in self.ops["sync"]:
                    fn(E)

            @block.gpsimd
            def _(E):
                for fn in self.ops["gpsimd"]:
                    fn(E)

            @block.vector
            def _(E):
                for fn in self.ops["vector"]:
                    fn(E)

            @block.scalar
            def _(E):
                for fn in self.ops["scalar"]:
                    fn(E)

            @block.tensor
            def _(E):
                for fn in self.ops["tensor"]:
                    fn(E)


def build_nc(debug=False, phase_limit=99):
    nc = bass.Bass("TRN2", target_bir_lowering=False)
    pg = Prog(nc)
    mem = Mem(nc)

    x_win = nc.dram_tensor("x_win", [TCTX, D], F32, kind="ExternalInput").ap()
    wqT = nc.dram_tensor("wqT", [D, D], F32, kind="ExternalInput").ap()
    wkT = nc.dram_tensor("wkT", [D, DKV], F32, kind="ExternalInput").ap()
    wvT = nc.dram_tensor("wvT", [D, DKV], F32, kind="ExternalInput").ap()
    woT = nc.dram_tensor("woT", [D, D], F32, kind="ExternalInput").ap()
    gateT = nc.dram_tensor("gateT", [D, NE], F32, kind="ExternalInput").ap()
    w1_d = nc.dram_tensor("w1", [NE, D, HFF], F32, kind="ExternalInput").ap()
    w2_d = nc.dram_tensor("w2", [NE, HFF, D], F32, kind="ExternalInput").ap()
    sw1_d = nc.dram_tensor("sw1", [D, HFF], F32, kind="ExternalInput").ap()
    sw2_d = nc.dram_tensor("sw2", [HFF, D], F32, kind="ExternalInput").ap()
    qnw_d = nc.dram_tensor("qnw_c", [P, DC], F32, kind="ExternalInput").ap()
    knw_d = nc.dram_tensor("knw_c", [P, 8], F32, kind="ExternalInput").ap()
    ctxv_d = nc.dram_tensor("ctxv", [P, P], F32, kind="ExternalInput").ap()
    bcb_d = nc.dram_tensor("bcolbase", [P, NH * NKB], F32,
                           kind="ExternalInput").ap()
    crow_d = nc.dram_tensor("crow_all", [NH, 3, TOWN], BF16,
                            kind="ExternalInput").ap()
    e256_d = nc.dram_tensor("e256", [NE, 1], F32, kind="ExternalInput").ap()
    e832_d = nc.dram_tensor("e832", [NKV, NH], F32, kind="ExternalInput").ap()
    hsel_d = nc.dram_tensor("hsel", [P, DC * NH], BF16,
                            kind="ExternalInput").ap()
    ksel_d = nc.dram_tensor("ksel", [64, NKV * NKV], BF16,
                            kind="ExternalInput").ap()
    tok_d = nc.dram_tensor("tokid8", [P, NQT], F32, kind="ExternalInput").ap()
    oob_d = nc.dram_tensor("oobinit", [NSLOT + 1, 2], F32,
                           kind="ExternalInput").ap()

    xn_tm_dr = nc.dram_tensor("xn_tm_dr", [TOWN, D], BF16, kind="Internal").ap()
    xn_fm_dr = nc.dram_tensor("xn_fm_dr", [DC, P, TOWN], BF16,
                              kind="Internal").ap()
    s_dr = nc.dram_tensor("s_dr", [NQT, P, D], BF16, kind="Internal").ap()
    slotmap = nc.dram_tensor("slotmap", [NSLOT + 1, 2], F32, kind="Internal").ap()
    ab_dr = nc.dram_tensor("ab_dr", [2 * TOWN + 1, D], BF16, kind="Internal").ap()
    gpre_dr = nc.dram_tensor("gpre_dr", [NQT, P, D], F32, kind="Internal").ap()

    o_tm = nc.dram_tensor("o_tm", [TOWN, D], F32, kind="ExternalOutput").ap()
    dbg = {}
    if debug:
        for nm, shp, dt in [("d_xn_tm", [TOWN, D], BF16),
                            ("d_q", [DC, P, TOWN], BF16),
                            ("d_qraw", [DC, P, TOWN], BF16),
                            ("d_rrq", [1, TOWN], F32),
                            ("d_rstb", [P, TOWN], BF16),
                            ("d_psqs", [1, TOWN], F32),
                            ("d_k", [NKV, 67, TCTX], BF16),
                            ("d_att", [DC, P, TOWN], BF16),
                            ("d_s", [NQT, P, D], BF16),
                            ("d_cw", [P, NE * NQT], F32),
                            ("d_sidx", [P, 2 * NQT], F32),
                            ("d_cab", [P, 2 * NQT], F32),
                            ("d_gpre", [NQT, P, D], F32)]:
            dbg[nm] = nc.dram_tensor(nm, shp, dt, kind="ExternalOutput").ap()

    ones_f = mem.at("ones_f", [1, P], F32, 0, 9)
    ones_bf = mem.at("ones_bf", [1, P], BF16, 0, 9)
    tokrow = mem.at("tokrow", [P, 4], F32, 0, 9)
    tokrow_i = mem.at("tokrow_i", [P, 2], I32, 0, 9)
    e256_s = mem.at("e256_s", [NE, 1], F32, 0, 9)
    sc_i32 = mem.at("sc_i32", [P, 1], I32, 0, 9)
    ones_col = mem.at("ones_col", [P, 1], BF16, 0, 9)
    zrow = mem.at("zrow", [1, 1024], BF16, 0, 9)
    cab = mem.at("cab", [P, 2 * NQT], F32, 0, 8)
    ident = mem.at("ident", [P, P], BF16, 0, 7)
    ident_f = mem.at("ident_f", [P, P], F32, 0, 6)
    scrs = mem.at("scrs", [P, NE * NQT], F32, 0, 6)
    cw16 = mem.at("cw16", [P, NE * NQT], F32, 0, 6)
    sidx = mem.at("sidx", [P, 2 * NQT], F32, 0, 6)
    tokid_s = mem.at("tokid_s", [P, NQT], F32, 0, 6)
    xnown = mem.at("xnown", [P, DC * 1024], BF16, 1, 6)
    bcolbase = mem.at("bcolb", [P, NH * NKB], F32, 0, 5)
    mdg = mem.at("mdg", [P, P], F32, 0, 5)
    m01t = mem.at("m01t", [P, P], BF16, 0, 5)
    ctxv_s = mem.at("ctxv_s", [P, P], F32, 0, 4)
    gate_sb = mem.at("gate_sb", [P, DC * NE], BF16, 6, 6)
    xnctx = mem.at("xnctx", [P, DC * 1024], BF16, 1, 4)
    qfm = mem.at("qfm", [P, DC * 1024], BF16, 2, 5)
    knw = mem.at("knw", [P, 8], F32, 0, 3)
    kaug = [mem.at(f"kaug{v}", [67, TCTX], BF16, 3, 5) for v in range(NKV)]
    qnw = mem.at("qnw", [P, DC], F32, 0, 2)
    gp_t = [mem.at(f"gp_t{i}", [P, D], F32, 7, 7) for i in range(2)]
    gp8 = mem.at("gp8", [P, D], F32, 8, 8)
    wq_sb = mem.at("wq_sb", [P, DC * 1024], BF16, 1, 2)
    attfm = mem.at("attfm", [P, DC * 1024], BF16, 5, 6)
    vaug = mem.at("vaug", [P, NKB * (NKV * 65)], BF16, 4, 5)
    wk_sb = mem.at("wk_sb", [P, DC * 512], BF16, 2, 3)
    wv_sb = mem.at("wv_sb", [P, DC * 512], BF16, 3, 4)
    wo_sb = [mem.at(f"wo_sb{i}", [P, 8 * 1024], BF16, 5, 6) for i in range(2)]
    rstb = mem.at("rstb", [P, 1024], BF16, 2, 3)
    w1_sb = mem.at("w1_sb", [P, DC * HFF], BF16, 7, 7)
    w2_sb = mem.at("w2_sb", [P, HC * D], BF16, 7, 7)
    h1s = mem.at("h1s", [P, HC * 1024], BF16, 7, 7)
    wo_sb2 = [mem.at(f"wo_sb2_{i}", [P, 8 * 1024], BF16, 6, 6)
              for i in range(2)]
    ab_s = mem.at("ab_s", [P, 2 * D], F32, 8, 8)
    rr_k = mem.at("rr_k", [1, TCTX], F32, 3, 3)
    lnx0 = mem.at("lnx0", [P, D], F32, 1, 1)
    lnx1 = mem.at("lnx1", [P, D], F32, 1, 1)
    xngT = mem.at("xngT", [P, DC * CAP], BF16, 7, 7)
    xngT1 = mem.at("xngT1", [P, DC * CAP], BF16, 7, 7)
    x_rs = mem.at("x_rs", [P, D], F32, 8, 8)
    fin0 = mem.at("fin0", [P, D], F32, 8, 8)
    h1e = mem.at("h1e", [P, HC * CAP], BF16, 7, 7)
    rr_q = mem.at("rr_q", [1, TOWN], F32, 2, 2)
    lnxn0 = mem.at("lnxn0", [P, D], BF16, 1, 1)
    lnxn1 = mem.at("lnxn1", [P, D], BF16, 1, 1)
    s_tile = mem.at("s_tile", [P, D], BF16, 6, 6)
    rt_a = mem.at("rt_a", [NE, TOWN], F32, 6, 6)
    rt_b = mem.at("rt_b", [NE, TOWN], F32, 6, 6)
    rt_c = mem.at("rt_c", [NE, TOWN], F32, 6, 6)
    xng = [mem.at(f"xng{i}", [P, D], BF16, 7, 7) for i in range(2)]
    out2 = mem.at("out2", [P, D], BF16, 7, 7)
    s_rs = mem.at("s_rs", [P, D], BF16, 8, 8)
    fin1 = mem.at("fin1", [P, D], BF16, 8, 8)
    dbsnap = mem.at("dbsnap", [P, 1024], F32, 0, 9) if debug else None
    pt0 = mem.at("pt0", [P, 1152], BF16, 5, 5)
    pt1 = mem.at("pt1", [P, 1152], BF16, 5, 5)
    biascols = mem.at("biascols", [P, NH * NKB], F32, 5, 5)
    recip_row = mem.at("recip_row", [1, TOWN], BF16, 5, 5)
    sbR = mem.at("sbR", [64, 512], F32, 5, 5)
    qaug = mem.at("qaug", [67, TOWN], BF16, 5, 5)
    qaug1 = mem.at("qaug1", [67, TOWN], BF16, 5, 5)
    qsq_sc = mem.at("qsq_sc", [P, 1024], BF16, 2, 2)
    ksq_sc = mem.at("ksq_sc", [P, 1024], BF16, 3, 3)
    ksq_f = mem.at("ksq_f", [P, 1024], BF16, 3, 3)
    xnr = [mem.at(f"xnr{i}", [P, TOWN], BF16, 7, 7) for i in range(2)]
    qssm = mem.at("qssm", [NH, 1], F32, 5, 5)
    kssm = mem.at("kssm", [NKV, 1], F32, 5, 5)
    prod32 = mem.at("prod32", [NH, 1], F32, 5, 5)
    e832_s = mem.at("e832_s", [NKV, NH], F32, 0, 5)
    hsel_s = mem.at("hsel_s", [P, DC * NH], BF16, 0, 5)
    ksel_s = mem.at("ksel_s", [64, NKV * NKV], BF16, 0, 5)
    mcol = mem.at("mcol", [P, NH], F32, 5, 5)
    lnsc = mem.at("lnsc", [P, 16], F32, 1, 1)
    msk16 = mem.at("msk16", [P, NE], F32, 6, 6)
    cm16 = mem.at("cm16", [P, NE], F32, 6, 6)
    rk_a = mem.at("rk_a", [P, NE], F32, 6, 6)
    rk_b = mem.at("rk_b", [P, NE], F32, 6, 6)
    sel0 = mem.at("sel0", [P, NE], F32, 6, 6)
    sel1 = mem.at("sel1", [P, NE], F32, 6, 6)
    slot_tm = mem.at("slot_tm", [P, NE], F32, 6, 6)
    sltmp = mem.at("sltmp", [P, NE], F32, 6, 6)
    m8 = mem.at("m8", [P, 8], F32, 6, 6)
    pad8 = mem.at("pad8", [P, 8], F32, 6, 6)
    acpy = mem.at("acpy", [P, 8], F32, 6, 6)
    gsc = mem.at("gsc", [P, 4], F32, 6, 6)
    m32 = mem.at("m32", [NH, 1], BF16, 5, 5)
    mrow = mem.at("mrow", [1, NH], BF16, 5, 5)
    CVALS = [EPS_LN, EPS_RMS, -0.5, 1.0 / D, 1.0 / DKV, -1.0, 0.125,
             1.0, 0.0, 2.0, 1e-20, float(NSLOT), float(TOWN - 1),
             float(2 * TOWN), 0.5, -1e30, -1.01]
    constv_h = nc.alloc_sbuf_tensor_at("constv", [P, len(CVALS)], F32,
                                       offset=mem.base + mem.limit + 64)
    constv = constv_h.ap()
    for i, v in enumerate(CVALS):
        nc.const_aps.register(F32, float(v), constv[:, i:i + 1])

    def A(eng, fn, inc=None, inck=1):
        return pg.add(eng, fn, inc, inck)

    def W(eng, sem, val=None):
        pg.wait(eng, sem, val)

    def PHASE_GUARD(*labels):
        for e in (VE, SC, TE, GP, SY):
            for lb in labels:
                pg.wait(e, lb, None)

    SY, GP, VE, SC, TE = "sync", "gpsimd", "vector", "scalar", "tensor"

    for i, v in enumerate(CVALS):
        A(GP, lambda E, i=i, v=v: E.memset(constv[:, i:i + 1], float(v)))
    A(GP, lambda E: E.memset(ones_bf[:], 1.0))
    A(GP, lambda E: E.memset(ones_f[:], 1.0))
    A(GP, lambda E: E.memset(ones_col[:], 1.0))
    A(GP, lambda E: E.memset(zrow[:], 0.0))
    for idn in (ident, ident_f):
        A(GP, lambda E, idn=idn: E.memset(idn[:], 0.0), inc="cms")
        W(GP, "cms")
        A(GP, lambda E, idn=idn: E.affine_select(
            out=idn[:], in_=idn[:], compare_op=ALU.not_equal, fill=1.0,
            base=0, pattern=[[-1, P]], channel_multiplier=1))
    A(GP, lambda E: E.memset(m01t[:], 1.0), inc="cms")
    W(GP, "cms")
    A(GP, lambda E: E.affine_select(
        out=m01t[:], in_=m01t[:], compare_op=ALU.is_ge, fill=0.0,
        base=0, pattern=[[-1, P]], channel_multiplier=1))
    A(GP, lambda E: E.memset(mdg[:], 0.0), inc="cms")
    W(GP, "cms")
    A(GP, lambda E: E.affine_select(
        out=mdg[:], in_=mdg[:], compare_op=ALU.is_ge, fill=NEG,
        base=0, pattern=[[1, P]], channel_multiplier=-1), inc="cst")
    for src, dst in [(qnw_d, qnw), (knw_d, knw), (ctxv_d, ctxv_s),
                     (bcb_d, bcolbase), (e256_d, e256_s),
                     (e832_d, e832_s), (hsel_d, hsel_s), (ksel_d, ksel_s),
                     (tok_d, tokid_s)]:
        A(SY, lambda E, s=src, d=dst: E.dma_start(out=d[:], in_=s),
          inc="cld", inck=16)
    A(SY, lambda E: E.dma_start(out=slotmap, in_=oob_d), inc="cld", inck=16)
    for _e in (TE, VE, SC):
        W(_e, "cst", pg.cur("cst"))
        W(_e, "cld", pg.cur("cld"))

    def load_wq_half(half):
        if half == 1:
            W(GP, "qmm", 8)
        for c in range(DC):
            A(GP, lambda E, c=c, h=half: E.dma_start(
                out=wq_sb[:, c * 1024:(c + 1) * 1024],
                in_=wqT[c * P:(c + 1) * P, h * 1024:(h + 1) * 1024]),
              inc="wql", inck=16)

    load_wq_half(0)

    lnw_after = {}
    with nc.psum_tensor("ps_tp", [P, 1024], BF16) as ps_tp:
        for i in range(NKB):
            buf = lnx0 if i % 2 == 0 else lnx1
            xnb = lnxn0 if i % 2 == 0 else lnxn1
            pb = 8 * (i % 2)
            if i >= 2:
                W(SY, "lnc", i - 1)
            A(SY, lambda E, i=i, b=buf: E.dma_start(
                out=b[:], in_=x_win[i * P:(i + 1) * P, :]),
              inc="lnx", inck=16)
            W(VE, "lnx", 16 * (i + 1))
            A(VE, lambda E, b=buf, pb=pb: E.reduce_sum(
                lnsc[:, pb:pb + 1], b[:], axis=AX), inc="lns")
            W(SC, "lns", pg.cur("lns"))
            if i == 0:
                W(SC, "cst", pg.cur("cst"))
            if i >= 2:
                W(SC, "lnw", lnw_after.get(i - 2, 0))
                W(SC, "tpt", 2 * (i - 1))
            A(SC, lambda E, b=buf, xb=xnb, pb=pb: E.activation(
                xb[:], b[:], AF.Square, accum_out=lnsc[:, pb + 1:pb + 2]))
            A(SC, lambda E, pb=pb: E.mul(lnsc[:, pb + 2:pb + 3],
                                         lnsc[:, pb:pb + 1], 1.0 / D))
            A(SC, lambda E, pb=pb: E.mul(lnsc[:, pb + 3:pb + 4],
                                         lnsc[:, pb + 1:pb + 2], 1.0 / D),
              inc="lnm")
            W(VE, "lnm", pg.cur("lnm"))
            A(VE, lambda E, pb=pb: E.tensor_mul(
                lnsc[:, pb + 4:pb + 5], lnsc[:, pb + 2:pb + 3],
                lnsc[:, pb + 2:pb + 3]))
            A(VE, lambda E, pb=pb: E.tensor_sub(
                lnsc[:, pb + 5:pb + 6], lnsc[:, pb + 3:pb + 4],
                lnsc[:, pb + 4:pb + 5]), inc="lnv")
            W(SC, "lnv", pg.cur("lnv"))
            A(SC, lambda E, pb=pb: E.activation(
                lnsc[:, pb + 6:pb + 7], lnsc[:, pb + 5:pb + 6], AF.Ln,
                bias=EPS_LN))
            A(SC, lambda E, pb=pb: E.activation(
                lnsc[:, pb + 7:pb + 8], lnsc[:, pb + 6:pb + 7], AF.Exp,
                scale=-0.5), inc="lnr")
            W(VE, "lnr", pg.cur("lnr"))
            A(VE, lambda E, b=buf, xb=xnb, pb=pb: E.tensor_scalar(
                xb[:], b[:], lnsc[:, pb + 2:pb + 3], lnsc[:, pb + 7:pb + 8],
                ALU.subtract, ALU.mult), inc="lnc")
            if i >= 8:
                W(SY, "lnc", pg.cur("lnc"))
                A(SY, lambda E, i=i, xb=xnb: E.dma_start(
                    out=xn_tm_dr[(i - 8) * P:(i - 7) * P, :], in_=xb[:]),
                  inc="lnw", inck=16)
                if debug:
                    A(SY, lambda E, i=i, xb=xnb: E.dma_start(
                        out=dbg["d_xn_tm"][(i - 8) * P:(i - 7) * P, :],
                        in_=xb[:]), inc="lnw", inck=16)
            lnw_after[i] = pg.cur("lnw")
            for g in range(2):
                gidx = 2 * i + g
                W(TE, "lnc", pg.cur("lnc"))
                W(TE, "tpe", gidx)
                for c in range(8):
                    cc = g * 8 + c
                    A(TE, lambda E, xb=xnb, cc=cc, c=c: E.transpose(
                        ps_tp[:, c * 128:(c + 1) * 128],
                        xb[:, cc * 128:(cc + 1) * 128], ident[:]),
                      inc="tpt" if c == 7 else None)
                W(VE, "tpt", pg.cur("tpt"))
                dst = xnctx if i < 8 else xnown
                col = (i % 8) * 128
                A(VE, lambda E, g=g, d=dst, col=col: E.tensor_copy(
                    d[:].rearrange("p (c t) -> p c t", t=1024)
                       [:, g * 8:(g + 1) * 8, col:col + 128],
                    ps_tp[:].rearrange("p (c t) -> p c t", t=128)),
                  inc="tpe")
    W(SY, "tpe", pg.cur("tpe"))
    for c in range(DC):
        A(SY, lambda E, c=c: E.dma_start(
            out=xn_fm_dr[c], in_=xnown[:, c * 1024:(c + 1) * 1024]),
          inc="xfw", inck=16)

    if phase_limit <= 1:
        for _lbl in ("lnw", "dbq", "dbk", "dba", "dbc", "dbg1", "xfw",
                     "gwr", "ssp", "cld"):
            W(SY, _lbl, pg.cur(_lbl))
        mem.commit()
        with nc.allow_low_precision(reason="dev"):
            pg.run()
        return nc
    PHASE_GUARD("lnx", "lnw")
    with nc.psum_tensor("psQa", [P, 1024], F32) as psQa, \
         nc.psum_tensor("psQb", [P, 1024], F32) as psQb, \
         nc.psum_tensor("psQs", [1, 1024], F32) as psQs, \
         nc.psum_tensor("psQr", [P, 1024], F32) as psQr:
        W(TE, "tpe", pg.cur("tpe"))
        for half in range(2):
            if half == 1:
                load_wq_half(1)
            for oc in range(8):
                o = half * 8 + oc
                ps = psQa if o % 2 == 0 else psQb
                W(TE, "wql", 16 * DC * (half + 1))
                if o >= 2:
                    W(TE, "qsq", o - 1)
                    W(TE, "qev", o - 1)
                for c in range(DC):
                    for n2 in range(2):
                        A(TE, lambda E, c=c, oc=oc, n2=n2, ps=ps: E.matmul(
                            ps[:, n2 * 512:(n2 + 1) * 512],
                            wq_sb[:, c * 1024 + oc * 128:
                                  c * 1024 + (oc + 1) * 128],
                            xnown[:, c * 1024 + n2 * 512:
                                  c * 1024 + (n2 + 1) * 512],
                            start=(c == 0), stop=(c == DC - 1)),
                          inc="qmm" if (c == DC - 1 and n2 == 1) else None)
                W(VE, "qmm", pg.cur("qmm"))
                A(VE, lambda E, ps=ps, o=o: E.tensor_copy(
                    qfm[:, o * 1024:(o + 1) * 1024], ps[:]), inc="qev")
                A(VE, lambda E, o=o: E.tensor_mul(
                    qsq_sc[:], qfm[:, o * 1024:(o + 1) * 1024],
                    qfm[:, o * 1024:(o + 1) * 1024]), inc="qsq")
                W(TE, "qsq", pg.cur("qsq"))
                for n2 in range(2):
                    A(TE, lambda E, n2=n2, o=o: E.matmul(
                        psQs[:, n2 * 512:(n2 + 1) * 512], ones_col[:],
                        qsq_sc[:, n2 * 512:(n2 + 1) * 512],
                        start=(o == 0), stop=(o == 15),
                        skip_group_check=True),
                      inc="qss" if n2 == 1 else None)
        if debug:
            W(SY, "qev", pg.cur("qev"))
            for c in range(DC):
                A(SY, lambda E, c=c: E.dma_start(
                    out=dbg["d_qraw"][c], in_=qfm[:, c * 1024:(c + 1) * 1024]),
                  inc="dbqr", inck=16)
            W(SY, "dbqr", pg.cur("dbqr"))
        if phase_limit == 12:
            mem.commit()
            with nc.allow_low_precision(reason="dev"):
                pg.run()
            return nc
        if debug:
            W(SC, "qss", pg.cur("qss"))
            A(SC, lambda E: E.copy(dbsnap[0:1, 0:1024], psQs[:]), inc="dbsn")
            W(SY, "dbsn", pg.cur("dbsn"))
            A(SY, lambda E: E.dma_start(out=dbg["d_psqs"],
                                        in_=dbsnap[0:1, 0:1024]),
              inc="dbq", inck=16)
        W(SC, "qss", pg.cur("qss"))
        A(SC, lambda E: E.activation(rr_q[:], psQs[:], AF.Ln, scale=1.0 / D,
                                     bias=EPS_RMS))
        A(SC, lambda E: E.activation(rr_q[:], rr_q[:], AF.Exp, scale=-0.5),
          inc="qrs")
        W(TE, "qrs", pg.cur("qrs"))
        W(TE, "cst", pg.cur("cst"))
        for n2 in range(2):
            A(TE, lambda E, n2=n2: E.matmul(
                psQr[:, n2 * 512:(n2 + 1) * 512], ones_f[:],
                rr_q[:, n2 * 512:(n2 + 1) * 512], start=True, stop=True),
              inc="qrb" if n2 == 1 else None)
        W(SC, "qrb", pg.cur("qrb"))
        A(SC, lambda E: E.copy(rstb[:], psQr[:]), inc="qre")
        if phase_limit == 15:
            mem.commit()
            with nc.allow_low_precision(reason="dev"):
                pg.run()
            return nc
        if debug:
            W(SY, "qre", pg.cur("qre"))
            A(SY, lambda E: E.dma_start(out=dbg["d_rstb"], in_=rstb[:]),
              inc="dbq", inck=16)
            W(SY, "qrs", pg.cur("qrs"))
            A(SY, lambda E: E.dma_start(out=dbg["d_rrq"], in_=rr_q[:]),
              inc="dbq", inck=16)
        W(VE, "qre", pg.cur("qre"))
        W(VE, "qev", pg.cur("qev"))
        W(VE, "dbqr", pg.cur("dbqr"))
        for o in range(DC):
            A(VE, lambda E, o=o: E.tensor_mul(
                qfm[:, o * 1024:(o + 1) * 1024],
                qfm[:, o * 1024:(o + 1) * 1024], rstb[:]))
            A(VE, lambda E, o=o: E.tensor_scalar(
                qfm[:, o * 1024:(o + 1) * 1024],
                qfm[:, o * 1024:(o + 1) * 1024],
                qnw[:, o:o + 1], 0.125, ALU.mult, ALU.mult), inc="qsc")
    for c in range(DC):
        A(GP, lambda E, c=c: E.dma_start(
            out=wk_sb[:, c * 512:(c + 1) * 512],
            in_=wkT[c * P:(c + 1) * P, :]), inc="wkl", inck=16)
    if debug:
        W(SY, "qsc", pg.cur("qsc"))

        for c in range(DC):
            A(SY, lambda E, c=c: E.dma_start(
                out=dbg["d_q"][c], in_=qfm[:, c * 1024:(c + 1) * 1024]),
              inc="dbq", inck=16)

    if phase_limit <= 2:
        for _lbl in ("lnw", "dbq", "dbk", "dba", "dbc", "dbg1", "xfw",
                     "gwr", "ssp", "cld"):
            W(SY, _lbl, pg.cur(_lbl))
        mem.commit()
        with nc.allow_low_precision(reason="dev"):
            pg.run()
        return nc
    PHASE_GUARD("dbqr", "dbq")
    W(GP, "qsc", pg.cur("qsc"))
    for v in range(NKV):
        A(GP, lambda E, v=v: E.memset(kaug[v][64:67, :], 1.0),
          inc="kcst" if v == NKV - 1 else None)
    with nc.psum_tensor("psKa", [P, 1024], F32) as psKa, \
         nc.psum_tensor("psKb", [P, 1024], F32) as psKb, \
         nc.psum_tensor("psKs", [1, 1024], F32) as psKs, \
         nc.psum_tensor("psKr", [P, 1024], F32) as psKr:
        W(TE, "qev", pg.cur("qev"))
        W(TE, "qre", pg.cur("qre"))
        for th in range(2):
            xsrc = xnctx if th == 0 else xnown
            for oc in range(4):
                idx = th * 4 + oc
                ps = psKa if idx % 2 == 0 else psKb
                W(TE, "wkl", 16 * DC)
                if idx >= 2:
                    W(TE, "ksq", idx - 1)
                    W(TE, "kev", 2 * (idx - 1))
                for c in range(DC):
                    for n2 in range(2):
                        A(TE, lambda E, c=c, oc=oc, n2=n2, ps=ps, xs=xsrc:
                          E.matmul(
                            ps[:, n2 * 512:(n2 + 1) * 512],
                            wk_sb[:, c * 512 + oc * 128:
                                  c * 512 + (oc + 1) * 128],
                            xs[:, c * 1024 + n2 * 512:
                               c * 1024 + (n2 + 1) * 512],
                            start=(c == 0), stop=(c == DC - 1)),
                          inc="kmm" if (c == DC - 1 and n2 == 1) else None)
                W(VE, "kmm", pg.cur("kmm"))
                if idx == 0:
                    W(VE, "kcst", pg.cur("kcst"))
                for sub in range(2):
                    kv = oc * 2 + sub
                    A(VE, lambda E, ps=ps, kv=kv, sub=sub, th=th:
                      E.tensor_copy(
                        kaug[kv][0:64, th * 1024:(th + 1) * 1024],
                        ps[sub * 64:(sub + 1) * 64, :]), inc="kev")
                A(VE, lambda E, ps=ps: E.tensor_copy(ksq_f[:], ps[:]))
                A(VE, lambda E: E.tensor_mul(ksq_sc[:], ksq_f[:], ksq_f[:]),
                  inc="ksq")
                W(TE, "ksq", pg.cur("ksq"))
                for n2 in range(2):
                    A(TE, lambda E, n2=n2, oc=oc: E.matmul(
                        psKs[:, n2 * 512:(n2 + 1) * 512], ones_col[:],
                        ksq_sc[:, n2 * 512:(n2 + 1) * 512],
                        start=(oc == 0), stop=(oc == 3),
                        skip_group_check=True),
                      inc="kss" if n2 == 1 else None)
            W(SC, "kss", pg.cur("kss"))
            A(SC, lambda E, th=th: E.activation(
                rr_k[:, th * 1024:(th + 1) * 1024], psKs[:], AF.Ln,
                scale=1.0 / DKV, bias=EPS_RMS))
            A(SC, lambda E, th=th: E.activation(
                rr_k[:, th * 1024:(th + 1) * 1024],
                rr_k[:, th * 1024:(th + 1) * 1024], AF.Exp, scale=-0.5),
              inc="krs")
            W(TE, "krs", pg.cur("krs"))
            if th == 1:
                W(TE, "kre", 1)
            for n2 in range(2):
                A(TE, lambda E, th=th, n2=n2: E.matmul(
                    psKr[:, n2 * 512:(n2 + 1) * 512], ones_f[:],
                    rr_k[:, th * 1024 + n2 * 512:
                         th * 1024 + (n2 + 1) * 512],
                    start=True, stop=True, skip_group_check=True),
                  inc="krb" if n2 == 1 else None)
            W(SC, "krb", pg.cur("krb"))
            A(SC, lambda E: E.copy(rstb[:], psKr[:]), inc="kre")
            W(VE, "kre", pg.cur("kre"))
            W(VE, "kev", pg.cur("kev"))
            for kv in range(NKV):
                A(VE, lambda E, kv=kv, th=th: E.tensor_mul(
                    kaug[kv][0:64, th * 1024:(th + 1) * 1024],
                    kaug[kv][0:64, th * 1024:(th + 1) * 1024],
                    rstb[0:64, :]))
                A(VE, lambda E, kv=kv, th=th: E.tensor_scalar(
                    kaug[kv][0:64, th * 1024:(th + 1) * 1024],
                    kaug[kv][0:64, th * 1024:(th + 1) * 1024],
                    knw[0:64, kv:kv + 1],
                    None, ALU.mult), inc="ksc")
    W(GP, "qsc", pg.cur("qsc"))
    for c in range(DC):
        A(GP, lambda E, c=c: E.dma_start(
            out=wv_sb[:, c * 512:(c + 1) * 512],
            in_=wvT[c * P:(c + 1) * P, :]), inc="wvl", inck=16)
    if debug:
        W(SY, "ksc", pg.cur("ksc"))
        for kv in range(NKV):
            A(SY, lambda E, kv=kv: E.dma_start(
                out=dbg["d_k"][kv], in_=kaug[kv][:]), inc="dbk", inck=16)

    if phase_limit <= 3:
        for _lbl in ("lnw", "dbq", "dbk", "dba", "dbc", "dbg1", "xfw",
                     "gwr", "ssp", "cld"):
            W(SY, _lbl, pg.cur(_lbl))
        mem.commit()
        with nc.allow_low_precision(reason="dev"):
            pg.run()
        return nc
    PHASE_GUARD("dbk")
    with nc.psum_tensor("psVa", [P, 512], F32) as psVa, \
         nc.psum_tensor("psVb", [P, 512], F32) as psVb:
        W(TE, "kev", pg.cur("kev"))
        W(TE, "kre", pg.cur("kre"))
        W(VE, "cld")
        for st in range(NKB):
            ps = psVa if st % 2 == 0 else psVb
            W(TE, "wvl", 16 * DC)
            if st >= 2:
                W(TE, "vev", 2 * (st - 1))
            xsrc = xnctx if st < 8 else xnown
            col = (st % 8) * 128
            for c in range(DC):
                A(TE, lambda E, c=c, ps=ps, xs=xsrc, col=col: E.matmul(
                    ps[:], xs[:, c * 1024 + col: c * 1024 + col + 128],
                    wv_sb[:, c * 512:(c + 1) * 512],
                    start=(c == 0), stop=(c == DC - 1)),
                  inc="vmm" if c == DC - 1 else None)
            W(VE, "vmm", pg.cur("vmm"))
            A(VE, lambda E, ps=ps, st=st: E.tensor_copy(
                vaug[:, st * 520:(st + 1) * 520].rearrange(
                    "p (k c) -> p k c", c=65)[:, :, 0:64],
                ps[:].rearrange("p (k c) -> p k c", c=64)), inc="vev")
            A(VE, lambda E, st=st: E.tensor_copy(
                vaug[:, st * 520:(st + 1) * 520].rearrange(
                    "p (k c) -> p k c", c=65)[:, :, 64:65],
                ctxv_s[:, st * 8:(st + 1) * 8].rearrange(
                    "p (k c) -> p k c", c=1)), inc="vev")

    if phase_limit <= 4:
        for _lbl in ("lnw", "dbq", "dbk", "dba", "dbc", "dbg1", "xfw",
                     "gwr", "ssp", "cld"):
            W(SY, _lbl, pg.cur(_lbl))
        mem.commit()
        with nc.allow_low_precision(reason="dev"):
            pg.run()
        return nc
    with nc.psum_tensor("psQall", [NH, 1024], F32) as psQall, \
         nc.psum_tensor("psKall", [NKV, 2048], F32) as psKall:
        W(TE, "vev", pg.cur("vev"))
        W(TE, "qsc", pg.cur("qsc"))
        W(TE, "ksc", pg.cur("ksc"))
        W(VE, "ksc", pg.cur("ksc"))
        W(VE, "qsc", pg.cur("qsc"))
        W(VE, "vev", pg.cur("vev"))
        for kv in range(NKV):
            if kv >= 1:
                W(VE, "kbm", kv)
            A(VE, lambda E, kv=kv: E.tensor_mul(
                attfm[0:64, 0:TCTX], kaug[kv][0:64, :], kaug[kv][0:64, :]),
              inc="ks2")
            W(TE, "ks2", kv + 1)
            for i in range(4):
                A(TE, lambda E, kv=kv, i=i: E.matmul(
                    psKall[:, i * 512:(i + 1) * 512],
                    ksel_s[:, kv * NKV:(kv + 1) * NKV],
                    attfm[0:64, i * 512:(i + 1) * 512],
                    start=(kv == 0), stop=(kv == NKV - 1),
                    skip_group_check=True),
                  inc="kbm" if i == 3 else None)
        W(VE, "kbm", pg.cur("kbm"))
        A(VE, lambda E: E.reduce_max(kssm[:], psKall[:], axis=AX),
          inc="kmx")
        for c in range(DC):
            sq = pt0 if c % 2 == 0 else pt1
            if c >= 2:
                W(VE, "qbm", c - 1)
            A(VE, lambda E, c=c, sq=sq: E.tensor_mul(
                sq[:, 0:1024], qfm[:, c * 1024:(c + 1) * 1024],
                qfm[:, c * 1024:(c + 1) * 1024]), inc="qs2")
            W(TE, "qs2", c + 1)
            for n2 in range(2):
                A(TE, lambda E, c=c, n2=n2, sq=sq: E.matmul(
                    psQall[:, n2 * 512:(n2 + 1) * 512],
                    hsel_s[:, c * NH:(c + 1) * NH],
                    sq[:, n2 * 512:(n2 + 1) * 512],
                    start=(c == 0), stop=(c == DC - 1),
                    skip_group_check=True),
                  inc="qbm" if n2 == 1 else None)
        W(VE, "qbm", pg.cur("qbm"))
        A(VE, lambda E: E.reduce_max(qssm[:], psQall[:], axis=AX),
          inc="qmx")
    with nc.psum_tensor("psMB", [P, NH], F32) as psMB, \
         nc.psum_tensor("psMr", [1, NH], BF16) as psMr, \
         nc.psum_tensor("psE", [NH, 1], F32) as psE:
        W(TE, "kmx", 1)
        W(TE, "qmx", 1)
        A(TE, lambda E: E.matmul(psE[:], e832_s[:], kssm[:],
                                 start=True, stop=True,
                                 skip_group_check=True), inc="bex")
        W(VE, "bex", pg.cur("bex"))
        A(VE, lambda E: E.tensor_mul(prod32[:], qssm[:], psE[:]), inc="bpr")
        W(SC, "bpr", pg.cur("bpr"))
        A(SC, lambda E: E.activation(prod32[:], prod32[:], AF.Ln,
                                     bias=1e-20))
        A(SC, lambda E: E.activation(prod32[:], prod32[:], AF.Exp,
                                     scale=0.5))
        A(SC, lambda E: E.activation(m32[:], prod32[:], AF.Copy,
                                     scale=-1.01, bias=-0.5), inc="am3")
        W(TE, "am3", pg.cur("am3"))
        A(TE, lambda E: E.transpose(psMr[:], m32[:], ident[0:NH, 0:NH]),
          inc="amr")
        W(SC, "amr", pg.cur("amr"))
        A(SC, lambda E: E.copy(mrow[:], psMr[:]), inc="amw")
        W(TE, "amw", pg.cur("amw"))
        A(TE, lambda E: E.matmul(psMB[:], ones_bf[:], mrow[:],
                                 start=True, stop=True,
                                 skip_group_check=True), inc="am4")
        W(SC, "am4", pg.cur("am4"))
        A(SC, lambda E: E.copy(mcol[:], psMB[:]), inc="am5")
        W(VE, "am5", pg.cur("am5"))
        for h in range(NH):
            A(VE, lambda E, h=h: E.tensor_scalar(
                biascols[:, h * NKB:(h + 1) * NKB],
                bcolbase[:, h * NKB:(h + 1) * NKB],
                mcol[:, h:h + 1], None, ALU.add), inc="am6")

    W(GP, "vmm", pg.cur("vmm"))
    W(GP, "ksc", pg.cur("ksc"))
    for c in range(8):
        for i in range(2):
            A(GP, lambda E, c=c, i=i: E.dma_start(
                out=wo_sb[i][:, c * 1024:(c + 1) * 1024],
                in_=woT[c * P:(c + 1) * P, i * 1024:(i + 1) * 1024]),
              inc="wol", inck=16)

    pieces = []
    for b in range(NKB):
        j0, j1 = max(0, b - 8), min(NQT - 1, b)
        wdt = (j1 - j0 + 1) * 128
        cuts = list(range(0, wdt, 512)) + [wdt]
        pieces.append((j0, j1,
                       [(cuts[i], cuts[i + 1])
                        for i in range(len(cuts) - 1)]))
    with nc.psum_tensor("psB0", [P, 512], F32) as psB0, \
         nc.psum_tensor("psB1", [P, 512], F32) as psB1, \
         nc.psum_tensor("psB2", [P, 512], F32) as psB2, \
         nc.psum_tensor("psO0", [65, 1024], F32) as psO0, \
         nc.psum_tensor("psO1", [65, 1024], F32) as psO1, \
         nc.psum_tensor("psRc", [64, 512], F32) as psRc:
        W(TE, "amx", pg.cur("amx"))
        W(TE, "am6", pg.cur("am6"))
        W(SC, "am6", pg.cur("am6"))
        pcount = 0
        qauX = (qaug, qaug1)
        bmm_after = {}

        def stage_qaug(hh):
            qa = qauX[hh % 2]
            g = bmm_after.get(hh - 2)
            if g:
                W(VE, "bmm", g)
                W(SY, "bmm", g)
            A(VE, lambda E, hh=hh, qa=qa: E.tensor_copy(
                qa[0:64, :], qfm[64 * (hh % 2):64 * (hh % 2) + 64,
                                 (hh // 2) * 1024:(hh // 2) * 1024 + 1024]),
              inc="qau")
            A(SY, lambda E, hh=hh, qa=qa: E.dma_start(out=qa[64:67, :],
                                                      in_=crow_d[hh]),
              inc="qauD", inck=16)

        def emit_tail(hh):
            psOt = psO0 if hh % 2 == 0 else psO1
            qrowt = 64 * (hh % 2)
            qcolt = (hh // 2) * 1024
            W(TE, "pvr", hh + 1)
            for n2 in range(2):
                if 2 * hh + n2 >= 1:
                    W(TE, "pvc", 2 * hh + n2)
                A(TE, lambda E, n2=n2: E.matmul(
                    psRc[:], ones_bf[:, 0:64],
                    recip_row[:, n2 * 512:(n2 + 1) * 512],
                    start=True, stop=True, skip_group_check=True),
                  inc="pvb")
                W(SC, "pvb", pg.cur("pvb"))
                if 2 * hh + n2 >= 1:
                    W(SC, "pvd2", 2 * hh + n2)
                A(SC, lambda E, n2=n2: E.copy(sbR[:], psRc[:]), inc="pvc")
                W(VE, "pvc", pg.cur("pvc"))
                A(VE, lambda E, qrowt=qrowt, qcolt=qcolt, n2=n2, psOt=psOt:
                  E.tensor_mul(
                    attfm[qrowt:qrowt + 64,
                          qcolt + n2 * 512: qcolt + (n2 + 1) * 512],
                    psOt[0:64, n2 * 512:(n2 + 1) * 512],
                    sbR[:]), inc="pvd2")

        W(SY, "vev", pg.cur("vev"))
        W(SY, "vmm", pg.cur("vmm"))
        W(VE, "qsc", pg.cur("qsc"))
        stage_qaug(0)
        stage_qaug(1)
        psBs = (psB0, psB1, psB2)
        for h in range(NH):
            kv = h // 4
            qrow = 64 * (h % 2)
            qcol = (h // 2) * 1024
            qaug_h = qauX[h % 2]
            psO = psO0 if h % 2 == 0 else psO1
            W(TE, "qau", h + 1)
            W(TE, "qauD", 16 * (h + 1))
            if h >= 2:
                W(TE, "pvd2", 2 * (h - 1))
            for n2 in range(2):
                A(TE, lambda E, n2=n2, psO=psO: E.matmul(
                    psO[:, n2 * 512:(n2 + 1) * 512], ones_bf[0:1, 0:65],
                    zrow[0:1, n2 * 512:(n2 + 1) * 512],
                    start=True, stop=False, skip_group_check=True))
            pending_pv = None
            for b in range(NKB):
                if b == 6 and h >= 1:
                    emit_tail(h - 1)
                j0, j1, pcs = pieces[b]
                pt = pt0 if b % 2 == 0 else pt1
                for (p0, p1) in pcs:
                    ps = psBs[pcount % 3]
                    if pcount >= 3:
                        W(TE, "exb", pcount - 2)
                    A(TE, lambda E, kv=kv, b=b, j0=j0, p0=p0, p1=p1, ps=ps,
                      qa=qaug_h:
                      E.matmul(ps[:, 0:p1 - p0],
                               kaug[kv][:, b * 128:(b + 1) * 128],
                               qa[:, j0 * 128 + p0: j0 * 128 + p1],
                               start=True, stop=True,
                               skip_group_check=True), inc="bmm")
                    has_diag = (b >= 8 and p0 == 0)
                    if has_diag:
                        W(VE, "bmm", pg.cur("bmm"))
                        A(VE, lambda E, ps=ps: E.tensor_add(
                            ps[:, 0:128], ps[:, 0:128], mdg[:]), inc="bmk")
                    W(SC, "bmm", pg.cur("bmm"))
                    if has_diag:
                        W(SC, "bmk", pg.cur("bmk"))
                    A(SC, lambda E, ps=ps, pt=pt, h=h, b=b, p0=p0, p1=p1:
                      E.activation(pt[:, p0:p1], ps[:, 0:p1 - p0], AF.Exp,
                                   bias=biascols[:, h * NKB + b:
                                                 h * NKB + b + 1]),
                      inc="exb")
                    pcount += 1
                if b <= 7:
                    off = (b - j0) * 128
                    W(VE, "exb", pg.cur("exb"))
                    A(VE, lambda E, pt=pt, off=off: E.tensor_mul(
                        pt[:, off:off + 128], pt[:, off:off + 128],
                        m01t[:]), inc="bmt")

                def make_pv(b=b, j0=j0, j1=j1, kv=kv, pt=pt, psO=psO,
                            exbv=None, bmtv=None):
                    exbv = pg.cur("exb")
                    bmtv = pg.cur("bmt")

                    def emit_pv():
                        W(TE, "exb", exbv)
                        if b <= 7:
                            W(TE, "bmt", bmtv)
                        vs0 = b * 520 + kv * 65
                        wdt = (j1 - j0 + 1) * 128
                        if b <= 7:
                            ow = (b - j0) * 128
                            for q0 in range(0, ow, 512):
                                qw = min(512, ow - q0)
                                A(TE, lambda E, vs0=vs0, j0=j0, q0=q0, qw=qw,
                                  pt=pt, psO=psO:
                                  E.matmul(
                                    psO[:, j0 * 128 + q0:
                                        j0 * 128 + q0 + qw],
                                    vaug[:, vs0:vs0 + 65], pt[:, q0:q0 + qw],
                                    start=False, stop=False,
                                    skip_group_check=True))
                            A(TE, lambda E, vs0=vs0, b=b, pt=pt, ow=ow,
                              psO=psO:
                              E.matmul(
                                psO[:, b * 128:(b + 1) * 128],
                                vaug[:, vs0:vs0 + 65], pt[:, ow:ow + 128],
                                start=False, stop=False,
                                skip_group_check=True),
                              inc="pvm")
                        else:
                            a0, a1 = j0 * 128, j0 * 128 + wdt
                            cuts = ([a0] + [c for c in range(512, 2048, 512)
                                            if a0 < c < a1] + [a1])
                            for ci in range(len(cuts) - 1):
                                c0, c1 = cuts[ci], cuts[ci + 1]
                                last = (c1 == a1)
                                A(TE, lambda E, vs0=vs0, c0=c0, c1=c1, a0=a0,
                                  pt=pt, psO=psO,
                                  st_=(b == NKB - 1 and last): E.matmul(
                                    psO[:, c0:c1],
                                    vaug[:, vs0:vs0 + 65],
                                    pt[:, c0 - a0:c1 - a0],
                                    start=False, stop=st_,
                                    skip_group_check=True),
                                  inc="pvm" if last else None)
                    return emit_pv
                if pending_pv is not None:
                    pending_pv()
                pending_pv = make_pv()
            pending_pv()
            bmm_after[h] = pg.cur("bmm")
            if h + 2 < NH:
                stage_qaug(h + 2)
            W(VE, "pvm", pg.cur("pvm"))
            if h >= 1:
                W(VE, "pvb", 2 * h)
            A(VE, lambda E, psO=psO: E.reciprocal(recip_row[:],
                                                  psO[64:65, :]),
              inc="pvr")
        emit_tail(NH - 1)

    if debug:
        W(SY, "pvd2", pg.cur("pvd2"))
        for c in range(DC):
            A(SY, lambda E, c=c: E.dma_start(
                out=dbg["d_att"][c], in_=attfm[:, c * 1024:(c + 1) * 1024]),
              inc="dba", inck=16)

    if phase_limit <= 5:
        for _lbl in ("lnw", "dbq", "dbk", "dba", "dbc", "dbg1", "xfw",
                     "gwr", "ssp", "cld"):
            W(SY, _lbl, pg.cur(_lbl))
        mem.commit()
        with nc.allow_low_precision(reason="dev"):
            pg.run()
        return nc
    PHASE_GUARD("dba")
    W(GP, "pvd2", 2 * NH)
    W(GP, "exb", pg.cur("exb"))
    for c in range(8):
        for i in range(2):
            A(GP, lambda E, c=c, i=i: E.dma_start(
                out=wo_sb2[i][:, c * 1024:(c + 1) * 1024],
                in_=woT[(8 + c) * P:(9 + c) * P, i * 1024:(i + 1) * 1024]),
              inc="wol2", inck=16)
    for c in range(DC):
        A(GP, lambda E, c=c: E.dma_start(
            out=gate_sb[:, c * NE:(c + 1) * NE],
            in_=gateT[c * P:(c + 1) * P, :]), inc="wgl", inck=16)
    with nc.psum_tensor("psW0", [P, 512], F32) as psW0, \
         nc.psum_tensor("psW1", [P, 512], F32) as psW1, \
         nc.psum_tensor("psW2", [P, 512], F32) as psW2, \
         nc.psum_tensor("psW3", [P, 512], F32) as psW3:
        psWs = [psW0, psW1, psW2, psW3]
        W(TE, "pvd2", pg.cur("pvd2"))
        W(TE, "pvc", pg.cur("pvc"))
        W(TE, "exb", pg.cur("exb"))
        for tt in range(NQT):
            for og in range(4):
                ps = psWs[og]
                W(TE, "wol", 16 * 16)
                if tt == 0:
                    W(TE, "wol2", 16 * 16)
                if tt >= 1:
                    W(TE, "wev", (tt - 1) * 4 + og + 1)
                for c in range(DC):
                    wsb = (wo_sb if c < 8 else wo_sb2)[og // 2]
                    cc = c % 8
                    A(TE, lambda E, tt=tt, og=og, c=c, cc=cc, ps=ps, wsb=wsb:
                      E.matmul(
                        ps[:],
                        attfm[:, c * 1024 + tt * 128:
                              c * 1024 + (tt + 1) * 128],
                        wsb[:, cc * 1024 + (og % 2) * 512:
                            cc * 1024 + (og % 2) * 512 + 512],
                        start=(c == 0), stop=(c == DC - 1)),
                      inc="wmm" if c == DC - 1 else None)
                W(VE, "wmm", pg.cur("wmm"))
                if og == 0 and tt >= 1:
                    W(VE, "ssp", pg.cur("ssp"))
                A(VE, lambda E, ps=ps, og=og: E.tensor_copy(
                    s_tile[:, og * 512:(og + 1) * 512], ps[:]), inc="wev")
            W(SY, "wev", pg.cur("wev"))
            A(SY, lambda E, tt=tt: E.dma_start(out=s_dr[tt], in_=s_tile[:]),
              inc="ssp", inck=16)
            if debug:
                A(SY, lambda E, tt=tt: E.dma_start(
                    out=dbg["d_s"][tt], in_=s_tile[:]), inc="ssp", inck=16)

    with nc.psum_tensor("psG", [P, NE], F32) as psG, \
         nc.psum_tensor("psT1", [NE, P], F32) as psT1, \
         nc.psum_tensor("psT2", [P, NE], F32) as psT2:
        W(TE, "wev", pg.cur("wev"))
        W(TE, "wgl", 16 * DC)
        A(VE, lambda E: E.memset(pad8[:, 4:8], -1e30), inc="gpad")
        for tt in range(NQT):
            if tt >= 1:
                W(TE, "gsg", tt)
            for c in range(DC):
                A(TE, lambda E, c=c, tt=tt: E.matmul(
                    psG[:],
                    xnown[:, c * 1024 + tt * 128: c * 1024 + (tt + 1) * 128],
                    gate_sb[:, c * NE:(c + 1) * NE],
                    start=(c == 0), stop=(c == DC - 1)),
                  inc="gmm" if c == DC - 1 else None)
            W(SC, "gmm", pg.cur("gmm"))
            A(SC, lambda E, tt=tt: E.activation(
                scrs[:, tt * NE:(tt + 1) * NE], psG[:], AF.Sigmoid),
              inc="gsg")
            W(VE, "gsg", pg.cur("gsg"))
            for g in range(4):
                A(VE, lambda E, g=g, tt=tt: E.tensor_copy(
                    pad8[:, 0:4],
                    scrs[:, tt * NE + 4 * g: tt * NE + 4 * g + 4]))
                A(VE, lambda E: E.max(out=m8[:], in_=pad8[:]))
                A(VE, lambda E, g=g: E.tensor_add(
                    gsc[:, g:g + 1], m8[:, 0:1], m8[:, 1:2]))
            A(VE, lambda E: E.tensor_copy(pad8[:, 0:4], gsc[:]))
            A(VE, lambda E: E.max(out=m8[:], in_=pad8[:]), inc="gm8")
            W(SC, "gm8", pg.cur("gm8"))
            A(SC, lambda E: E.copy(acpy[:, 0:1], m8[:, 1:2]), inc="gth")
            W(VE, "gth", pg.cur("gth"))
            A(VE, lambda E: E.tensor_scalar(
                sltmp[:, 0:4] if False else gsc[:], gsc[:], acpy[:, 0:1],
                None, ALU.is_ge), inc="gmk")
            W(SC, "gmk", pg.cur("gmk"))
            A(SC, lambda E: E.copy(acpy[:, 4:8], gsc[:]), inc="gmkA")
            W(VE, "gmkA", pg.cur("gmkA"))
            for g in range(4):
                A(VE, lambda E, g=g, tt=tt: E.tensor_scalar(
                    msk16[:, 4 * g:4 * g + 4],
                    scrs[:, tt * NE + 4 * g: tt * NE + 4 * g + 4],
                    acpy[:, 4 + g:5 + g], None, ALU.mult))
            A(VE, lambda E: E.max(out=m8[:], in_=msk16[:]), inc="gt2")
            W(SC, "gt2", pg.cur("gt2"))
            A(SC, lambda E: E.copy(acpy[:, 1:3], m8[:, 0:2]), inc="gt2A")
            W(VE, "gt2A", pg.cur("gt2A"))
            A(VE, lambda E: E.tensor_add(sltmp[:, 4:5], m8[:, 0:1],
                                         m8[:, 1:2]))
            A(VE, lambda E: E.tensor_scalar(
                sltmp[:, 4:5], sltmp[:, 4:5], 1e-20, None, ALU.add))
            A(VE, lambda E: E.reciprocal(sltmp[:, 5:6], sltmp[:, 4:5]),
              inc="grc")
            W(SC, "grc", pg.cur("grc"))
            A(SC, lambda E: E.copy(acpy[:, 3:4], sltmp[:, 5:6]), inc="grcA")
            W(VE, "grcA", pg.cur("grcA"))
            A(VE, lambda E: E.tensor_scalar(
                cm16[:], msk16[:], acpy[:, 2:3], None, ALU.is_ge))
            A(VE, lambda E, tt=tt: E.tensor_mul(
                cw16[:, tt * NE:(tt + 1) * NE], msk16[:], cm16[:]))
            A(VE, lambda E, tt=tt: E.tensor_scalar(
                cw16[:, tt * NE:(tt + 1) * NE],
                cw16[:, tt * NE:(tt + 1) * NE],
                acpy[:, 3:4], None, ALU.mult), inc="gcw")
            W(TE, "gcw", pg.cur("gcw"))
            if tt >= 1:
                W(TE, "gte", tt)
            A(TE, lambda E: E.transpose(psT1[:], cm16[:], ident_f[:]),
              inc="gtp")
            W(VE, "gtp", pg.cur("gtp"))
            A(VE, lambda E, tt=tt: E.tensor_copy(
                rt_a[:, tt * 128:(tt + 1) * 128], psT1[:]), inc="gte")
        W(VE, "gte", pg.cur("gte"))
        A(VE, lambda E: E.tensor_copy(rt_c[:], rt_a[:]))
        src, dst = rt_a, rt_b
        for sh in [1, 2, 4, 8, 16, 32, 64, 128, 256, 512]:
            A(VE, lambda E, s=src, d=dst, sh=sh: E.tensor_copy(
                d[:, 0:sh], s[:, 0:sh]))
            A(VE, lambda E, s=src, d=dst, sh=sh: E.tensor_add(
                d[:, sh:TOWN], s[:, sh:TOWN], s[:, 0:TOWN - sh]))
            src, dst = dst, src
        A(VE, lambda E, s=src: E.tensor_sub(s[:], s[:], rt_c[:]))
        A(VE, lambda E, s=src, d=dst: E.tensor_scalar(
            d[:], s[:], CAP - 0.5, None, ALU.is_ge))
        A(VE, lambda E, s=src: E.tensor_scalar(
            s[:], s[:], e256_s[:, 0:1], None, ALU.add))
        A(VE, lambda E, s=src, d=dst: E.scalar_tensor_tensor(
            s[:], d[:], BIG, s[:], ALU.mult, ALU.add), inc="gps")
        slotT = src
        for tt in range(NQT):
            W(TE, "gps", pg.cur("gps"))
            if tt >= 1:
                W(TE, "gsl", tt)
            A(TE, lambda E, tt=tt, s=slotT: E.transpose(
                psT2[:], s[:, tt * 128:(tt + 1) * 128],
                ident_f[0:NE, 0:NE]), inc="gtq")
            W(VE, "gtq", pg.cur("gtq"))
            A(VE, lambda E: E.tensor_copy(slot_tm[:], psT2[:]), inc="gsl")
            A(VE, lambda E, tt=tt: E.tensor_scalar(
                cm16[:], cw16[:, tt * NE:(tt + 1) * NE], 0.0, None,
                ALU.is_gt))
            A(VE, lambda E: E.tensor_copy(rk_a[:], cm16[:]))
            ra, rb = rk_a, rk_b
            for sh in (1, 2, 4, 8):
                A(VE, lambda E, a=ra, b2=rb, sh=sh: E.tensor_copy(
                    b2[:, 0:sh], a[:, 0:sh]))
                A(VE, lambda E, a=ra, b2=rb, sh=sh: E.tensor_add(
                    b2[:, sh:NE], a[:, sh:NE], a[:, 0:NE - sh]))
                ra, rb = rb, ra
            A(VE, lambda E, r=ra: E.tensor_sub(r[:], r[:], cm16[:]))
            A(VE, lambda E, r=ra: E.tensor_mul(sel1[:], cm16[:], r[:]))
            A(VE, lambda E: E.tensor_sub(sel0[:], cm16[:], sel1[:]))
            for k, sl in ((0, sel0), (1, sel1)):
                A(VE, lambda E, sl=sl: E.tensor_mul(sltmp[:], slot_tm[:],
                                                    sl[:]))
                A(VE, lambda E, tt=tt, k=k: E.reduce_sum(
                    sidx[:, 2 * tt + k:2 * tt + k + 1], sltmp[:], axis=AX))
                A(VE, lambda E, sl=sl, tt=tt: E.tensor_mul(
                    sltmp[:], cw16[:, tt * NE:(tt + 1) * NE], sl[:]))
                A(VE, lambda E, tt=tt, k=k: E.reduce_sum(
                    cab[:, 2 * tt + k:2 * tt + k + 1], sltmp[:], axis=AX),
                  inc="gsx")
        for tt in range(NQT):
            for k in range(2):
                n = tt * 2 + k
                if n >= 1:
                    W(VE, "gsct", 16 * n)
                A(VE, lambda E, tt=tt: E.tensor_copy(
                    tokrow[:, 0:1], tokid_s[:, tt:tt + 1]))
                A(VE, lambda E, tt=tt, k=k: E.tensor_scalar(
                    tokrow[:, 1:2], tokid_s[:, tt:tt + 1], 2.0, float(k),
                    ALU.mult, ALU.add))
                A(VE, lambda E, tt=tt, k=k: E.tensor_scalar(
                    sc_i32[:], sidx[:, 2 * tt + k:2 * tt + k + 1],
                    float(NSLOT), None, ALU.min), inc="gsc1")
                W(GP, "gsc1", pg.cur("gsc1"))
                W(GP, "cld")
                A(GP, lambda E: E.indirect_dma_start(
                    out=slotmap,
                    out_offset=IndirectOffsetOnAxis(sc_i32[:], 0),
                    in_=tokrow[:, 0:2], in_offset=None),
                  inc="gsct", inck=16)
    if debug:
        W(SY, "gsx", pg.cur("gsx"))
        A(SY, lambda E: E.dma_start(out=dbg["d_cw"], in_=cw16[:]),
          inc="dbc", inck=16)
        A(SY, lambda E: E.dma_start(out=dbg["d_sidx"], in_=sidx[:]),
          inc="dbc", inck=16)
        A(SY, lambda E: E.dma_start(out=dbg["d_cab"], in_=cab[:]),
          inc="dbc", inck=16)

    if phase_limit <= 6:
        for _lbl in ("lnw", "dbq", "dbk", "dba", "dbc", "dbg1", "xfw",
                     "gwr", "ssp", "cld"):
            W(SY, _lbl, pg.cur(_lbl))
        mem.commit()
        with nc.allow_low_precision(reason="dev"):
            pg.run()
        return nc
    PHASE_GUARD("ssp", "dbc")
    def load_w1(e, wait_sem, wait_val):
        src = sw1_d if e < 0 else w1_d[e]
        if wait_val:
            W(GP, wait_sem, wait_val)
        for c in range(DC):
            A(GP, lambda E, c=c, src=src: E.dma_start(
                out=w1_sb[:, c * HFF:(c + 1) * HFF],
                in_=src[c * P:(c + 1) * P, :]), inc="mw1", inck=16)

    def load_w2(e, wait_sem, wait_val):
        src = sw2_d if e < 0 else w2_d[e]
        if wait_val:
            W(GP, wait_sem, wait_val)
        for c in range(HC):
            A(GP, lambda E, c=c, src=src: E.dma_start(
                out=w2_sb[:, c * D:(c + 1) * D],
                in_=src[c * P:(c + 1) * P, :]), inc="w2c", inck=16)

    W(GP, "wev", pg.cur("wev"))
    W(GP, "gsx", pg.cur("gsx"))
    W(GP, "ssp", pg.cur("ssp"))
    load_w1(-1, None, 0)
    with nc.psum_tensor("psH0", [P, 1024], F32) as psH0, \
         nc.psum_tensor("psH1", [P, 1024], F32) as psH1, \
         nc.psum_tensor("psP0", [P, 512], F32) as psP0, \
         nc.psum_tensor("psP1", [P, 512], F32) as psP1, \
         nc.psum_tensor("psXa", [P, 1024], BF16) as psXa, \
         nc.psum_tensor("psXb", [P, 1024], BF16) as psXb:
        W(SY, "xfw", pg.cur("xfw"))
        W(SY, "gsx", pg.cur("gsx"))
        W(SY, "gsct", pg.cur("gsct"))
        W(TE, "gsl", NQT)
        W(TE, "gte", NQT)
        W(TE, "gsg", NQT)
        for hc in range(HC):
            ps = psH0 if hc % 2 == 0 else psH1
            W(TE, "mw1", 16 * DC)
            if hc >= 2:
                W(TE, "shev", hc - 1)
            for c in range(DC):
                n = hc * DC + c
                if n >= 2:
                    W(SY, "shmm", n - 1)
                A(SY, lambda E, c=c: E.dma_start(
                    out=xnr[c % 2][:], in_=xn_fm_dr[c]), inc="xnl", inck=16)
                W(TE, "xnl", 16 * (n + 1))
                for n2 in range(2):
                    A(TE, lambda E, c=c, hc=hc, n2=n2, ps=ps: E.matmul(
                        ps[:, n2 * 512:(n2 + 1) * 512],
                        w1_sb[:, c * HFF + hc * 128:
                              c * HFF + (hc + 1) * 128],
                        xnr[c % 2][:, n2 * 512:(n2 + 1) * 512],
                        start=(c == 0), stop=(c == DC - 1)),
                      inc="shmm" if n2 == 1 else None)
            W(SC, "shmm", pg.cur("shmm"))
            A(SC, lambda E, ps=ps, hc=hc: E.activation(
                h1s[:, hc * 1024:(hc + 1) * 1024], ps[:], AF.Silu),
              inc="shev")
        load_w2(-1, None, 0)
        W(TE, "shev", pg.cur("shev"))
        for tt in range(NQT):
            for og in range(4):
                idx = tt * 4 + og
                ps = psP0 if idx % 2 == 0 else psP1
                W(TE, "w2c", 16 * HC)
                if idx >= 2:
                    W(TE, "shp2", idx - 1)
                for c in range(HC):
                    A(TE, lambda E, c=c, tt=tt, og=og, ps=ps: E.matmul(
                        ps[:],
                        h1s[:, c * 1024 + tt * 128:
                            c * 1024 + (tt + 1) * 128],
                        w2_sb[:, c * D + og * 512: c * D + (og + 1) * 512],
                        start=(c == 0), stop=(c == HC - 1)),
                      inc="shm2" if c == HC - 1 else None)
                W(SC, "shm2", pg.cur("shm2"))
                if tt >= 2 and og == 0:
                    W(SC, "gwr", 16 * (tt - 1))
                A(SC, lambda E, ps=ps, tt=tt, og=og: E.copy(
                    gp_t[tt % 2][:, og * 512:(og + 1) * 512],
                    ps[:]), inc="shp2")
            W(SY, "shp2", pg.cur("shp2"))
            A(SY, lambda E, tt=tt: E.dma_start(
                out=gpre_dr[tt], in_=gp_t[tt % 2][:]), inc="gwr", inck=16)
        load_w1(0, "shmm", pg.cur("shmm"))

        xngTs = (xngT, xngT1)
        psXs2 = (psXa, psXb)

        def fetch(e):
            xT = xngTs[e % 2]
            psX = psXs2[e % 2]
            for st in range(2):
                n = 2 * e + st
                W(SY, "gsct", pg.cur("gsct"))
                if n >= 1:
                    W(SY, "mti", n)
                W(GP, "lnw" if n == 0 else "mti", 16 * 8 if n == 0 else 0)
                A(SY, lambda E, e=e, st=st: E.dma_start(
                    out=tokrow[:, 0:2],
                    in_=slotmap[e * CAP + st * 128:
                                e * CAP + (st + 1) * 128, :]),
                  inc="mtr", inck=16)
                W(VE, "mtr", 16 * (n + 1))
                if n >= 1:
                    W(VE, "mgth", 16 * n)
                A(VE, lambda E: E.tensor_scalar(
                    tokrow_i[:, 0:1], tokrow[:, 0:1], float(TOWN - 1),
                    None, ALU.min), inc="mti")
                W(GP, "mti", pg.cur("mti"))
                if e >= 1:
                    W(GP, "mxt", 4 * e)
                A(GP, lambda E, st=st: E.indirect_dma_start(
                    out=xng[st][:], out_offset=None, in_=xn_tm_dr,
                    in_offset=IndirectOffsetOnAxis(tokrow_i[:, 0:1], 0)),
                  inc="mgth", inck=16)
            first_evac = True
            for st in range(2):
                for g in range(2):
                    gi = 4 * e + 2 * st + g
                    W(TE, "mgth", 16 * (2 * e + st + 1))
                    if 2 * st + g == 0:
                        if e >= 2:
                            W(TE, "mxe", 4 * (e - 1))
                    else:
                        W(TE, "mxe", gi)
                    for c in range(8):
                        cc = g * 8 + c
                        A(TE, lambda E, st=st, cc=cc, c=c, ps=psX:
                          E.transpose(
                            ps[:, c * 128:(c + 1) * 128],
                            xng[st][:, cc * 128:(cc + 1) * 128],
                            ident[:]), inc="mxt" if c == 7 else None)
                    W(VE, "mxt", pg.cur("mxt"))
                    if first_evac and e >= 2:
                        W(VE, "memm", HC * (e - 1))
                        first_evac = False
                    A(VE, lambda E, st=st, g=g, ps=psX, xT=xT:
                      E.tensor_copy(
                        xT[:].rearrange("p (c s) -> p c s", s=CAP)
                            [:, g * 8:(g + 1) * 8,
                             st * 128:(st + 1) * 128],
                        ps[:].rearrange("p (c s) -> p c s", s=128)),
                      inc="mxe")

        fetch(0)
        for e in range(NE):
            xngT_e = xngTs[e % 2]
            for hc in range(HC):
                ps = psP0 if hc % 2 == 0 else psP1
                W(TE, "mw1", 16 * DC * (e + 2))
                W(TE, "mxe", 4 * (e + 1))
                if hc == 0:
                    if e == 0:
                        W(TE, "shp2", pg.cur("shp2"))
                    else:
                        W(TE, "meo2", 8 * e)
                if e * HC + hc >= 2:
                    W(TE, "meev", e * HC + hc - 1)
                for c in range(DC):
                    A(TE, lambda E, c=c, hc=hc, ps=ps, xT=xngT_e: E.matmul(
                        ps[:, 0:CAP],
                        w1_sb[:, c * HFF + hc * 128:
                              c * HFF + (hc + 1) * 128],
                        xT[:, c * CAP:(c + 1) * CAP],
                        start=(c == 0), stop=(c == DC - 1)),
                      inc="memm" if c == DC - 1 else None)
                W(SC, "memm", pg.cur("memm"))
                A(SC, lambda E, ps=ps, hc=hc: E.activation(
                    h1e[:, hc * CAP:(hc + 1) * CAP], ps[:, 0:CAP],
                    AF.Silu), inc="meev")
            if e + 1 < NE:
                fetch(e + 1)
            load_w2(e, "shm2" if e == 0 else "mem2",
                    pg.cur("shm2") if e == 0 else pg.cur("mem2"))
            for st in range(2):
                for og in range(4):
                    idx = st * 4 + og
                    ps = psP0 if idx % 2 == 0 else psP1
                    W(TE, "w2c", 16 * HC * (e + 2))
                    W(TE, "meev", (e + 1) * HC)
                    if 8 * e + idx >= 2:
                        W(TE, "meo2", 8 * e + idx - 1)
                    for c in range(HC):
                        A(TE, lambda E, c=c, st=st, og=og, ps=ps: E.matmul(
                            ps[:],
                            h1e[:, c * CAP + st * 128:
                                c * CAP + st * 128 + 128],
                            w2_sb[:, c * D + og * 512:
                                  c * D + (og + 1) * 512],
                            start=(c == 0), stop=(c == HC - 1)),
                          inc="mem2" if c == HC - 1 else None)
                    W(VE, "mem2", pg.cur("mem2"))
                    if 2 * e + st >= 1:
                        W(VE, "msct", 16 * (2 * e + st))
                    A(VE, lambda E, ps=ps, og=og: E.tensor_copy(
                        out2[:, og * 512:(og + 1) * 512], ps[:]),
                      inc="meo2")
                n2v = 2 * e + st
                W(SY, "mti2", n2v)
                A(SY, lambda E, e=e, st=st: E.dma_start(
                    out=tokrow[:, 2:4],
                    in_=slotmap[e * CAP + st * 128:
                                e * CAP + (st + 1) * 128, :]),
                  inc="mtr2", inck=16)
                W(VE, "mtr2", 16 * (n2v + 1))
                A(VE, lambda E: E.tensor_scalar(
                    tokrow_i[:, 1:2], tokrow[:, 3:4], float(2 * TOWN),
                    None, ALU.min), inc="mti2")
                W(GP, "mti2", pg.cur("mti2"))
                W(GP, "meo2", pg.cur("meo2"))
                A(GP, lambda E: E.indirect_dma_start(
                    out=ab_dr,
                    out_offset=IndirectOffsetOnAxis(tokrow_i[:, 1:2], 0),
                    in_=out2[:], in_offset=None),
                  inc="msct", inck=16)
            if e + 1 < NE:
                load_w1(e + 1, "memm", pg.cur("memm"))
    if debug:
        W(SY, "gwr", pg.cur("gwr"))
        for tt in range(NQT):
            A(SY, lambda E, tt=tt: E.dma_start(
                out=dbg["d_gpre"][tt], in_=gpre_dr[tt]),
              inc="dbg1", inck=16)

    if phase_limit <= 7:
        for _lbl in ("lnw", "dbq", "dbk", "dba", "dbc", "dbg1", "xfw",
                     "gwr", "ssp", "cld"):
            W(SY, _lbl, pg.cur(_lbl))
        mem.commit()
        with nc.allow_low_precision(reason="dev"):
            pg.run()
        return nc
    PHASE_GUARD("msct", "gwr", "dbg1", "mtr", "mtr2", "mgth")
    W(SY, "msct", pg.cur("msct"))
    W(SY, "gwr", pg.cur("gwr"))
    W(SY, "ssp", pg.cur("ssp"))
    W(GP, "msct", pg.cur("msct"))
    for tt in range(NQT):
        if tt >= 1:
            W(SY, "fo", tt)
            W(GP, "fo", tt)
        A(GP, lambda E, tt=tt: E.dma_start(
            out=ab_s[:, 0:D],
            in_=ab_dr[0:2 * TOWN, :].rearrange("(t k) d -> t k d", k=2)
                [tt * 128:(tt + 1) * 128, 0, :]), inc="flg", inck=16)
        A(GP, lambda E, tt=tt: E.dma_start(
            out=ab_s[:, D:2 * D],
            in_=ab_dr[0:2 * TOWN, :].rearrange("(t k) d -> t k d", k=2)
                [tt * 128:(tt + 1) * 128, 1, :]), inc="flg", inck=16)
        A(SY, lambda E, tt=tt: E.dma_start(out=s_rs[:], in_=s_dr[tt]),
          inc="fl", inck=16)
        A(SY, lambda E, tt=tt: E.dma_start(
            out=x_rs[:], in_=x_win[(8 + tt) * P:(9 + tt) * P, :]),
          inc="fl", inck=16)
        A(SY, lambda E, tt=tt: E.dma_start(out=gp8[:], in_=gpre_dr[tt]),
          inc="fl", inck=16)
        W(VE, "fl", pg.cur("fl"))
        W(VE, "flg", pg.cur("flg"))
        if tt >= 1:
            W(VE, "fst", 16 * tt)
        A(VE, lambda E, tt=tt: E.scalar_tensor_tensor(
            fin0[:], ab_s[:, 0:D], cab[:, 2 * tt:2 * tt + 1],
            gp8[:], ALU.mult, ALU.add))
        A(VE, lambda E, tt=tt: E.scalar_tensor_tensor(
            fin0[:], ab_s[:, D:2 * D], cab[:, 2 * tt + 1:2 * tt + 2],
            fin0[:], ALU.mult, ALU.add), inc="fg")
        W(SC, "fg", pg.cur("fg"))
        A(SC, lambda E: E.activation(fin1[:], fin0[:], AF.Silu), inc="fs")
        W(VE, "fs", pg.cur("fs"))
        A(VE, lambda E: E.tensor_mul(fin0[:], s_rs[:], fin1[:]))
        A(VE, lambda E: E.tensor_add(fin0[:], fin0[:], x_rs[:]), inc="fo")
        W(SY, "fo", pg.cur("fo"))
        A(SY, lambda E, tt=tt: E.dma_start(
            out=o_tm[tt * 128:(tt + 1) * 128, :], in_=fin0[:]),
          inc="fst", inck=16)
    W(SY, "fst", pg.cur("fst"))
    if debug:
        for s in ("dbq", "dbk", "dba", "dbc", "dbg1", "lnw"):
            W(SY, s, pg.cur(s))

    mem.commit()
    with nc.allow_low_precision(reason="bf16 matmul operands with fp32 accum; "
                                "tolerance spec allows bf16 matmuls"):
        pg.run()
    return nc


_NC_CACHE = {}


def kernel(L0_attn_x, L0_attn_wq, L0_attn_wk, L0_attn_wv, L0_attn_wo,
           L0_attn_q_norm_w, L0_attn_k_norm_w, L0_ffn_gate_w, L0_ffn_w1,
           L0_ffn_w2, L0_ffn_sw1, L0_ffn_sw2, _debug=False):
    x = np.asarray(L0_attn_x, np.float32)
    B, T, _ = x.shape
    xr = np.ascontiguousarray(x.reshape(B * T, D))

    wqT = np.ascontiguousarray(np.asarray(L0_attn_wq, np.float32).T)
    wkT = np.ascontiguousarray(np.asarray(L0_attn_wk, np.float32).T)
    wvT = np.ascontiguousarray(np.asarray(L0_attn_wv, np.float32).T)
    woT = np.ascontiguousarray(np.asarray(L0_attn_wo, np.float32).T)
    gateT = np.ascontiguousarray(np.asarray(L0_ffn_gate_w, np.float32).T)
    w1 = np.ascontiguousarray(np.asarray(L0_ffn_w1, np.float32))
    w2 = np.ascontiguousarray(np.asarray(L0_ffn_w2, np.float32))
    sw1 = np.ascontiguousarray(np.asarray(L0_ffn_sw1, np.float32)[0])
    sw2 = np.ascontiguousarray(np.asarray(L0_ffn_sw2, np.float32)[0])
    qnw_c = np.ascontiguousarray(
        np.asarray(L0_attn_q_norm_w, np.float32).reshape(DC, P).T)
    knw_c = np.zeros((P, 8), np.float32)
    knw_c[0:64, :] = np.asarray(L0_attn_k_norm_w, np.float32).reshape(8, 64).T

    key = "dbg" if _debug else "main"
    if key not in _NC_CACHE:
        _NC_CACHE[key] = build_nc(debug=_debug)
    nc = _NC_CACHE[key]

    in_maps = []
    for c in range(8):
        t0 = c * TOWN
        xw = np.zeros((TCTX, D), np.float32)
        if c % 4 == 0:
            xw[1024:] = xr[t0:t0 + TOWN]
        else:
            xw[:] = xr[t0 - 1024:t0 + TOWN]
        valid = np.zeros((P, P), np.float32)
        for st in range(NKB):
            v = 1.0 if (st >= 8 or c % 4 != 0) else 0.0
            valid[:, st * 8:(st + 1) * 8] = v
        in_maps.append({
            "x_win": xw, "wqT": wqT, "wkT": wkT, "wvT": wvT, "woT": woT,
            "gateT": gateT, "w1": w1, "w2": w2, "sw1": sw1, "sw2": sw2,
            "qnw_c": qnw_c, "knw_c": knw_c, "ctxv": valid,
            "bcolbase": _CONSTS["bcolbase"], "crow_all": _CONSTS["crow_all"],
            "e256": _CONSTS["e256"], "e832": _CONSTS["e832"],
            "hsel": _CONSTS["hsel"], "ksel": _CONSTS["ksel"],
            "tokid8": _CONSTS["tokid8"], "oobinit": _CONSTS["oobinit"],
        })

    res = run_bass_kernel_spmd(nc, in_maps, core_ids=list(range(8)))
    out = np.concatenate([np.asarray(res.results[c]["o_tm"])
                          for c in range(8)], axis=0)
    if _debug:
        return out.reshape(B, T, D), res
    return out.reshape(B, T, D)


# version 09
# speedup vs baseline: 1.1449x; 1.1027x over previous
import numpy as np
import ml_dtypes
import concourse.bass as bass
import concourse.mybir as mybir
from concourse.bass import IndirectOffsetOnAxis
from concourse.bass_utils import run_bass_kernel_spmd

F32 = mybir.dt.float32
BF16 = mybir.dt.bfloat16
I32 = mybir.dt.int32
AF = mybir.ActivationFunctionType
ALU = mybir.AluOpType
AX = mybir.AxisListType.X

P = 128
D = 2048
DC = 16
HD = 64
NH = 32
NKV = 8
DKV = 512
TOWN = 1024
TCTX = 2048
NQT = 8
NKB = 16
HFF = 1408
HC = 11
NE = 16
CAP = 256
NSLOT = NE * CAP
EPS_LN = 1e-5
EPS_RMS = 1e-6
NEG = -1e30
BIG = 1e9

SLOPES = [2.0 ** (-(8.0 / NH) * h) for h in range(NH)]


def _host_consts():
    c = {}
    bb = np.zeros((P, NH * NKB), np.float32)
    for h in range(NH):
        for b in range(NKB):
            bb[:, h * NKB + b] = SLOPES[h] * (128.0 * b + np.arange(P))
    c["bcolbase"] = bb
    cr = np.zeros((NH, 3, TOWN), np.float64)
    t = np.arange(TOWN, dtype=np.float64)
    for h in range(NH):
        v = -SLOPES[h] * (t + 1024.0)
        hi = v.astype(ml_dtypes.bfloat16).astype(np.float64)
        mid = (v - hi).astype(ml_dtypes.bfloat16).astype(np.float64)
        lo = (v - hi - mid).astype(ml_dtypes.bfloat16).astype(np.float64)
        cr[h, 0], cr[h, 1], cr[h, 2] = hi, mid, lo
    c["crow_all"] = cr.astype(ml_dtypes.bfloat16)
    e2 = np.zeros((NE, 1), np.float32)
    e2[:, 0] = CAP * np.arange(NE)
    c["e256"] = e2
    tk = np.zeros((P, NQT), np.float32)
    for j in range(NQT):
        tk[:, j] = 128 * j + np.arange(P)
    c["tokid8"] = tk
    e8 = np.zeros((NKV, NH), np.float32)
    for h in range(NH):
        e8[h // 4, h] = 1.0
    c["e832"] = e8
    hs = np.zeros((P, DC * NH), np.float32)
    for ch in range(DC):
        hs[0:64, ch * NH + 2 * ch] = 1.0
        hs[64:128, ch * NH + 2 * ch + 1] = 1.0
    c["hsel"] = hs.astype(ml_dtypes.bfloat16)
    ks = np.zeros((64, NKV * NKV), np.float32)
    for kv in range(NKV):
        ks[:, kv * NKV + kv] = 1.0
    c["ksel"] = ks.astype(ml_dtypes.bfloat16)
    c["oobinit"] = np.full((NSLOT + 1, 2), BIG, np.float32)
    return c


_CONSTS = _host_consts()


class Mem:

    class Lazy:
        __slots__ = ("ap",)

        def __init__(self):
            self.ap = None

        def __getitem__(self, idx):
            return self.ap[idx]

        def rearrange(self, *a, **k):
            return self.ap.rearrange(*a, **k)

        @property
        def shape(self):
            return self.ap.shape

    def __init__(self, nc):
        self.nc = nc
        probe = nc.alloc_sbuf_tensor("mem_base_probe", [P, 64], mybir.dt.uint8)
        self.base = nc.lookup_mloc(probe).addr
        self.limit = nc.sbuf_top - self.base - 256
        assert self.limit >= 204 * 1024, f"unexpected sbuf base {self.base}"
        self.specs = []

    def at(self, name, shape, dtype, ph0, ph1, align=64):
        lz = Mem.Lazy()
        size = int(np.prod(shape[1:])) if len(shape) > 1 else 1
        size *= mybir.dt.size(dtype)
        size = (size + align - 1) // align * align
        self.specs.append([lz, name, list(shape), dtype, size, ph0, ph1])
        return lz

    def commit(self):
        order = sorted(range(len(self.specs)),
                       key=lambda i: (-self.specs[i][4],
                                      -(self.specs[i][6] - self.specs[i][5])))
        placed = []
        for i in order:
            lz, name, shape, dtype, size, p0, p1 = self.specs[i]
            spans = sorted((o, o + sz) for (o, sz, q0, q1) in placed
                           if not (q1 < p0 or q0 > p1))
            off = 0
            for (a, b) in spans:
                if off + size <= a:
                    break
                off = max(off, b)
            assert off + size <= self.limit, (
                f"SBUF overflow placing {name}: off={off} size={size}")
            placed.append((off, size, p0, p1))
            lz.ap = self.nc.alloc_sbuf_tensor_at(
                name, shape, dtype, offset=self.base + off).ap()


class Prog:
    ENGS = ("sync", "gpsimd", "vector", "scalar", "tensor")
    NLANES = 8

    def __init__(self, nc):
        self.nc = nc
        self.ops = {e: [] for e in self.ENGS}
        self.hand = {}
        self.labels = {}
        self.dma_state = {}

    def _sem(self, key):
        if key not in self.hand:
            h = self.nc.semaphore("s_" + "_".join(str(x) for x in key)).__enter__()
            self.hand[key] = [h, 0]
        return self.hand[key]

    def cur(self, name):
        if name not in self.labels:
            return 0
        lb = self.labels[name]
        return lb["inck"] * len(lb["ev"])

    def wait(self, eng, name, val=None):
        if name not in self.labels:
            return
        lb = self.labels[name]
        k = len(lb["ev"]) if val is None else (
            val // lb["inck"] if lb["inck"] > 1 else val)
        if k <= 0:
            return
        assert k <= len(lb["ev"]), f"wait {name} k={k} > {len(lb['ev'])}"
        if lb["kind"] == "sig":
            s = self._sem(lb["key"])
            self.ops[eng].append(
                lambda E, h=s[0], v=lb["ev"][k - 1]: E.wait_ge(h, v))
        else:
            need = {}
            for (lane, v) in lb["ev"][:k]:
                need[lane] = max(need.get(lane, 0), v)
            q = lb["key"]
            for lane, v in sorted(need.items()):
                s = self._sem(("dma", q, lane))
                self.ops[eng].append(
                    lambda E, h=s[0], v=v: E.wait_ge(h, v))

    AUTOEDGE = ("vector", "scalar", "gpsimd")

    def add(self, eng, fn, inc=None, inck=1):
        if inc is None and (eng == "tensor" or eng == "sync"):
            self.ops[eng].append(fn)
            return None
        if inck == 16:
            st = self.dma_state.setdefault(eng, [0, [0] * self.NLANES])
            lane = st[0]
            st[0] = (lane + 1) % self.NLANES
            s = self._sem(("dma", eng, lane))
            prev = st[1][lane]
            if prev:
                self.ops[eng].append(
                    lambda E, h=s[0], v=prev: E.wait_ge(h, v))
            st[1][lane] = prev + 16
            s[1] += 16
            lb = self.labels.setdefault(
                inc, {"kind": "dma", "key": eng, "inck": 16, "ev": []})
            assert lb["kind"] == "dma" and lb["key"] == eng, f"label {inc}"
            lb["ev"].append((lane, st[1][lane]))
            self.ops[eng].append(
                lambda E, fn=fn, h=s[0]: fn(E).then_inc(h, 16))
            return None
        key = ("sig", eng)
        s = self._sem(key)
        if eng in self.AUTOEDGE and s[1] > 0:
            self.ops[eng].append(lambda E, h=s[0], v=s[1]: E.wait_ge(h, v))
        s[1] += inck
        if inc is not None:
            lb = self.labels.setdefault(
                inc, {"kind": "sig", "key": key, "inck": inck, "ev": []})
            assert lb["kind"] == "sig" and lb["key"] == key, f"label {inc}"
            lb["ev"].append(s[1])
        self.ops[eng].append(
            lambda E, fn=fn, h=s[0], v=inck: fn(E).then_inc(h, v))
        return s[1]

    def run(self):
        with self.nc.Block() as block:
            @block.sync
            def _(E):
                for fn in self.ops["sync"]:
                    fn(E)

            @block.gpsimd
            def _(E):
                for fn in self.ops["gpsimd"]:
                    fn(E)

            @block.vector
            def _(E):
                for fn in self.ops["vector"]:
                    fn(E)

            @block.scalar
            def _(E):
                for fn in self.ops["scalar"]:
                    fn(E)

            @block.tensor
            def _(E):
                for fn in self.ops["tensor"]:
                    fn(E)


def build_nc(debug=False, phase_limit=99):
    nc = bass.Bass("TRN2", target_bir_lowering=False)
    pg = Prog(nc)
    mem = Mem(nc)

    x_win = nc.dram_tensor("x_win", [TCTX, D], F32, kind="ExternalInput").ap()
    wqT = nc.dram_tensor("wqT", [D, D], F32, kind="ExternalInput").ap()
    wkT = nc.dram_tensor("wkT", [D, DKV], F32, kind="ExternalInput").ap()
    wvT = nc.dram_tensor("wvT", [D, DKV], F32, kind="ExternalInput").ap()
    woT = nc.dram_tensor("woT", [D, D], F32, kind="ExternalInput").ap()
    gateT = nc.dram_tensor("gateT", [D, NE], F32, kind="ExternalInput").ap()
    w1_d = nc.dram_tensor("w1", [NE, D, HFF], F32, kind="ExternalInput").ap()
    w2_d = nc.dram_tensor("w2", [NE, HFF, D], F32, kind="ExternalInput").ap()
    sw1_d = nc.dram_tensor("sw1", [D, HFF], F32, kind="ExternalInput").ap()
    sw2_d = nc.dram_tensor("sw2", [HFF, D], F32, kind="ExternalInput").ap()
    qnw_d = nc.dram_tensor("qnw_c", [P, DC], F32, kind="ExternalInput").ap()
    knw_d = nc.dram_tensor("knw_c", [P, 8], F32, kind="ExternalInput").ap()
    ctxv_d = nc.dram_tensor("ctxv", [P, P], F32, kind="ExternalInput").ap()
    bcb_d = nc.dram_tensor("bcolbase", [P, NH * NKB], F32,
                           kind="ExternalInput").ap()
    crow_d = nc.dram_tensor("crow_all", [NH, 3, TOWN], BF16,
                            kind="ExternalInput").ap()
    e256_d = nc.dram_tensor("e256", [NE, 1], F32, kind="ExternalInput").ap()
    e832_d = nc.dram_tensor("e832", [NKV, NH], F32, kind="ExternalInput").ap()
    hsel_d = nc.dram_tensor("hsel", [P, DC * NH], BF16,
                            kind="ExternalInput").ap()
    ksel_d = nc.dram_tensor("ksel", [64, NKV * NKV], BF16,
                            kind="ExternalInput").ap()
    tok_d = nc.dram_tensor("tokid8", [P, NQT], F32, kind="ExternalInput").ap()
    oob_d = nc.dram_tensor("oobinit", [NSLOT + 1, 2], F32,
                           kind="ExternalInput").ap()

    xn_tm_dr = nc.dram_tensor("xn_tm_dr", [TOWN, D], BF16, kind="Internal").ap()
    xn_fm_dr = nc.dram_tensor("xn_fm_dr", [DC, P, TOWN], BF16,
                              kind="Internal").ap()
    s_dr = nc.dram_tensor("s_dr", [NQT, P, D], BF16, kind="Internal").ap()
    slotmap = nc.dram_tensor("slotmap", [NSLOT + 1, 2], F32, kind="Internal").ap()
    ab_dr = nc.dram_tensor("ab_dr", [2 * TOWN + 1, D], BF16, kind="Internal").ap()
    gpre_dr = nc.dram_tensor("gpre_dr", [NQT, P, D], F32, kind="Internal").ap()

    o_tm = nc.dram_tensor("o_tm", [TOWN, D], F32, kind="ExternalOutput").ap()
    dbg = {}
    if debug:
        for nm, shp, dt in [("d_xn_tm", [TOWN, D], BF16),
                            ("d_q", [DC, P, TOWN], BF16),
                            ("d_qraw", [DC, P, TOWN], BF16),
                            ("d_rrq", [1, TOWN], F32),
                            ("d_rstb", [P, TOWN], BF16),
                            ("d_psqs", [1, TOWN], F32),
                            ("d_k", [NKV, 67, TCTX], BF16),
                            ("d_att", [DC, P, TOWN], BF16),
                            ("d_s", [NQT, P, D], BF16),
                            ("d_cw", [P, NE * NQT], F32),
                            ("d_sidx", [P, 2 * NQT], F32),
                            ("d_cab", [P, 2 * NQT], F32),
                            ("d_gpre", [NQT, P, D], F32)]:
            dbg[nm] = nc.dram_tensor(nm, shp, dt, kind="ExternalOutput").ap()

    ones_f = mem.at("ones_f", [1, P], F32, 0, 9)
    ones_bf = mem.at("ones_bf", [1, P], BF16, 0, 9)
    tokrow = mem.at("tokrow", [P, 4], F32, 0, 9)
    tokrow_i = mem.at("tokrow_i", [P, 2], I32, 0, 9)
    e256_s = mem.at("e256_s", [NE, 1], F32, 0, 9)
    sc_i32 = mem.at("sc_i32", [P, 1], I32, 0, 9)
    ones_col = mem.at("ones_col", [P, 1], BF16, 0, 9)
    zrow = mem.at("zrow", [1, 1024], BF16, 0, 9)
    cab = mem.at("cab", [P, 2 * NQT], F32, 0, 8)
    ident = mem.at("ident", [P, P], BF16, 0, 7)
    ident_f = mem.at("ident_f", [P, P], F32, 0, 6)
    scrs = mem.at("scrs", [P, NE * NQT], F32, 0, 6)
    cw16 = mem.at("cw16", [P, NE * NQT], F32, 0, 6)
    sidx = mem.at("sidx", [P, 2 * NQT], F32, 0, 6)
    tokid_s = mem.at("tokid_s", [P, NQT], F32, 0, 6)
    xnown = mem.at("xnown", [P, DC * 1024], BF16, 1, 6)
    bcolbase = mem.at("bcolb", [P, NH * NKB], F32, 0, 5)
    mdg = mem.at("mdg", [P, P], F32, 0, 5)
    m01t = mem.at("m01t", [P, P], BF16, 0, 5)
    ctxv_s = mem.at("ctxv_s", [P, P], F32, 0, 4)
    gate_sb = mem.at("gate_sb", [P, DC * NE], BF16, 6, 6)
    xnctx = mem.at("xnctx", [P, DC * 1024], BF16, 1, 4)
    qfm = mem.at("qfm", [P, DC * 1024], BF16, 2, 5)
    knw = mem.at("knw", [P, 8], F32, 0, 3)
    kaug = [mem.at(f"kaug{v}", [67, TCTX], BF16, 3, 5) for v in range(NKV)]
    qnw = mem.at("qnw", [P, DC], F32, 0, 2)
    gp_t = [mem.at(f"gp_t{i}", [P, D], F32, 7, 7) for i in range(2)]
    gp8 = mem.at("gp8", [P, D], F32, 8, 8)
    wq_sb = mem.at("wq_sb", [P, DC * 1024], BF16, 1, 2)
    attfm = mem.at("attfm", [P, DC * 1024], BF16, 5, 6)
    vaug = mem.at("vaug", [P, NKB * (NKV * 65)], BF16, 4, 5)
    wk_sb = mem.at("wk_sb", [P, DC * 512], BF16, 2, 3)
    wv_sb = mem.at("wv_sb", [P, DC * 512], BF16, 3, 4)
    wo_sb = [mem.at(f"wo_sb{i}", [P, 8 * 1024], BF16, 5, 6) for i in range(2)]
    rstb = mem.at("rstb", [P, 1024], BF16, 2, 3)
    w1_sb = mem.at("w1_sb", [P, DC * HFF], BF16, 7, 7)
    w2_sb = mem.at("w2_sb", [P, HC * D], BF16, 7, 7)
    h1s = mem.at("h1s", [P, HC * 1024], BF16, 7, 7)
    wo_sb2 = [mem.at(f"wo_sb2_{i}", [P, 8 * 1024], BF16, 6, 6)
              for i in range(2)]
    ab_s = mem.at("ab_s", [P, 2 * D], F32, 8, 8)
    rr_k = mem.at("rr_k", [1, TCTX], F32, 3, 3)
    lnx0 = mem.at("lnx0", [P, D], F32, 1, 1)
    lnx1 = mem.at("lnx1", [P, D], F32, 1, 1)
    xngT = mem.at("xngT", [P, DC * CAP], BF16, 7, 7)
    xngT1 = mem.at("xngT1", [P, DC * CAP], BF16, 7, 7)
    x_rs = mem.at("x_rs", [P, D], F32, 8, 8)
    fin0 = mem.at("fin0", [P, D], F32, 8, 8)
    h1e = mem.at("h1e", [P, HC * CAP], BF16, 7, 7)
    rr_q = mem.at("rr_q", [1, TOWN], F32, 2, 2)
    lnxn0 = mem.at("lnxn0", [P, D], BF16, 1, 1)
    lnxn1 = mem.at("lnxn1", [P, D], BF16, 1, 1)
    s_tile = mem.at("s_tile", [P, D], BF16, 6, 6)
    rt_a = mem.at("rt_a", [NE, TOWN], F32, 6, 6)
    rt_b = mem.at("rt_b", [NE, TOWN], F32, 6, 6)
    rt_c = mem.at("rt_c", [NE, TOWN], F32, 6, 6)
    xng = [mem.at(f"xng{i}", [P, D], BF16, 7, 7) for i in range(2)]
    out2 = mem.at("out2", [P, D], BF16, 7, 7)
    s_rs = mem.at("s_rs", [P, D], BF16, 8, 8)
    fin1 = mem.at("fin1", [P, D], BF16, 8, 8)
    dbsnap = mem.at("dbsnap", [P, 1024], F32, 0, 9) if debug else None
    pt0 = mem.at("pt0", [P, 1152], BF16, 5, 5)
    pt1 = mem.at("pt1", [P, 1152], BF16, 5, 5)
    biascols = mem.at("biascols", [P, NH * NKB], F32, 5, 5)
    recip_row = mem.at("recip_row", [1, TOWN], BF16, 5, 5)
    sbR = mem.at("sbR", [64, 512], F32, 5, 5)
    qaug = mem.at("qaug", [67, TOWN], BF16, 5, 5)
    qaug1 = mem.at("qaug1", [67, TOWN], BF16, 5, 5)
    qsq_sc = mem.at("qsq_sc", [P, 1024], BF16, 2, 2)
    ksq_sc = mem.at("ksq_sc", [P, 1024], BF16, 3, 3)
    ksq_f = mem.at("ksq_f", [P, 1024], BF16, 3, 3)
    xnr = [mem.at(f"xnr{i}", [P, TOWN], BF16, 7, 7) for i in range(2)]
    qssm = mem.at("qssm", [NH, 1], F32, 5, 5)
    kssm = mem.at("kssm", [NKV, 1], F32, 5, 5)
    prod32 = mem.at("prod32", [NH, 1], F32, 5, 5)
    e832_s = mem.at("e832_s", [NKV, NH], F32, 0, 5)
    hsel_s = mem.at("hsel_s", [P, DC * NH], BF16, 0, 5)
    ksel_s = mem.at("ksel_s", [64, NKV * NKV], BF16, 0, 5)
    mcol = mem.at("mcol", [P, NH], F32, 5, 5)
    lnsc = mem.at("lnsc", [P, 16], F32, 1, 1)
    msk16 = mem.at("msk16", [P, NE], F32, 6, 6)
    cm16 = mem.at("cm16", [P, NE], F32, 6, 6)
    rk_a = mem.at("rk_a", [P, NE], F32, 6, 6)
    rk_b = mem.at("rk_b", [P, NE], F32, 6, 6)
    sel0 = mem.at("sel0", [P, NE], F32, 6, 6)
    sel1 = mem.at("sel1", [P, NE], F32, 6, 6)
    slot_tm = mem.at("slot_tm", [P, NE], F32, 6, 6)
    sltmp = mem.at("sltmp", [P, NE], F32, 6, 6)
    m8 = mem.at("m8", [P, 8], F32, 6, 6)
    pad8 = mem.at("pad8", [P, 8], F32, 6, 6)
    acpy = mem.at("acpy", [P, 8], F32, 6, 6)
    gsc = mem.at("gsc", [P, 4], F32, 6, 6)
    m32 = mem.at("m32", [NH, 1], BF16, 5, 5)
    mrow = mem.at("mrow", [1, NH], BF16, 5, 5)
    CVALS = [EPS_LN, EPS_RMS, -0.5, 1.0 / D, 1.0 / DKV, -1.0, 0.125,
             1.0, 0.0, 2.0, 1e-20, float(NSLOT), float(TOWN - 1),
             float(2 * TOWN), 0.5, -1e30, -1.01]
    constv_h = nc.alloc_sbuf_tensor_at("constv", [P, len(CVALS)], F32,
                                       offset=mem.base + mem.limit + 64)
    constv = constv_h.ap()
    for i, v in enumerate(CVALS):
        nc.const_aps.register(F32, float(v), constv[:, i:i + 1])

    def A(eng, fn, inc=None, inck=1):
        return pg.add(eng, fn, inc, inck)

    def W(eng, sem, val=None):
        pg.wait(eng, sem, val)

    def PHASE_GUARD(*labels):
        for e in (VE, SC, TE, GP, SY):
            for lb in labels:
                pg.wait(e, lb, None)

    SY, GP, VE, SC, TE = "sync", "gpsimd", "vector", "scalar", "tensor"

    for i, v in enumerate(CVALS):
        A(GP, lambda E, i=i, v=v: E.memset(constv[:, i:i + 1], float(v)))
    A(GP, lambda E: E.memset(ones_bf[:], 1.0))
    A(GP, lambda E: E.memset(ones_f[:], 1.0))
    A(GP, lambda E: E.memset(ones_col[:], 1.0))
    A(GP, lambda E: E.memset(zrow[:], 0.0))
    for idn in (ident, ident_f):
        A(GP, lambda E, idn=idn: E.memset(idn[:], 0.0), inc="cms")
        W(GP, "cms")
        A(GP, lambda E, idn=idn: E.affine_select(
            out=idn[:], in_=idn[:], compare_op=ALU.not_equal, fill=1.0,
            base=0, pattern=[[-1, P]], channel_multiplier=1))
    A(GP, lambda E: E.memset(m01t[:], 1.0), inc="cms")
    W(GP, "cms")
    A(GP, lambda E: E.affine_select(
        out=m01t[:], in_=m01t[:], compare_op=ALU.is_ge, fill=0.0,
        base=0, pattern=[[-1, P]], channel_multiplier=1))
    A(GP, lambda E: E.memset(mdg[:], 0.0), inc="cms")
    W(GP, "cms")
    A(GP, lambda E: E.affine_select(
        out=mdg[:], in_=mdg[:], compare_op=ALU.is_ge, fill=NEG,
        base=0, pattern=[[1, P]], channel_multiplier=-1), inc="cst")
    for src, dst in [(qnw_d, qnw), (knw_d, knw), (ctxv_d, ctxv_s),
                     (bcb_d, bcolbase), (e256_d, e256_s),
                     (e832_d, e832_s), (hsel_d, hsel_s), (ksel_d, ksel_s),
                     (tok_d, tokid_s)]:
        A(SY, lambda E, s=src, d=dst: E.dma_start(out=d[:], in_=s),
          inc="cld", inck=16)
    A(SY, lambda E: E.dma_start(out=slotmap, in_=oob_d), inc="cld", inck=16)
    for _e in (TE, VE, SC):
        W(_e, "cst", pg.cur("cst"))
        W(_e, "cld", pg.cur("cld"))

    def load_wq_half(half):
        if half == 1:
            W(GP, "qmm", 8)
        for c in range(DC):
            A(GP, lambda E, c=c, h=half: E.dma_start(
                out=wq_sb[:, c * 1024:(c + 1) * 1024],
                in_=wqT[c * P:(c + 1) * P, h * 1024:(h + 1) * 1024]),
              inc="wql", inck=16)

    load_wq_half(0)

    lnw_after = {}
    with nc.psum_tensor("ps_tp", [P, 1024], BF16) as ps_tp:
        for i in range(NKB):
            buf = lnx0 if i % 2 == 0 else lnx1
            xnb = lnxn0 if i % 2 == 0 else lnxn1
            pb = 8 * (i % 2)
            if i >= 2:
                W(SY, "lnc", i - 1)
            A(SY, lambda E, i=i, b=buf: E.dma_start(
                out=b[:], in_=x_win[i * P:(i + 1) * P, :]),
              inc="lnx", inck=16)
            W(VE, "lnx", 16 * (i + 1))
            A(VE, lambda E, b=buf, pb=pb: E.reduce_sum(
                lnsc[:, pb:pb + 1], b[:], axis=AX), inc="lns")
            W(SC, "lns", pg.cur("lns"))
            if i == 0:
                W(SC, "cst", pg.cur("cst"))
            if i >= 2:
                W(SC, "lnw", lnw_after.get(i - 2, 0))
                W(SC, "tpt", 2 * (i - 1))
            A(SC, lambda E, b=buf, xb=xnb, pb=pb: E.activation(
                xb[:], b[:], AF.Square, accum_out=lnsc[:, pb + 1:pb + 2]))
            A(SC, lambda E, pb=pb: E.mul(lnsc[:, pb + 2:pb + 3],
                                         lnsc[:, pb:pb + 1], 1.0 / D))
            A(SC, lambda E, pb=pb: E.mul(lnsc[:, pb + 3:pb + 4],
                                         lnsc[:, pb + 1:pb + 2], 1.0 / D),
              inc="lnm")
            W(VE, "lnm", pg.cur("lnm"))
            A(VE, lambda E, pb=pb: E.tensor_mul(
                lnsc[:, pb + 4:pb + 5], lnsc[:, pb + 2:pb + 3],
                lnsc[:, pb + 2:pb + 3]))
            A(VE, lambda E, pb=pb: E.tensor_sub(
                lnsc[:, pb + 5:pb + 6], lnsc[:, pb + 3:pb + 4],
                lnsc[:, pb + 4:pb + 5]), inc="lnv")
            W(SC, "lnv", pg.cur("lnv"))
            A(SC, lambda E, pb=pb: E.activation(
                lnsc[:, pb + 6:pb + 7], lnsc[:, pb + 5:pb + 6], AF.Ln,
                bias=EPS_LN))
            A(SC, lambda E, pb=pb: E.activation(
                lnsc[:, pb + 7:pb + 8], lnsc[:, pb + 6:pb + 7], AF.Exp,
                scale=-0.5), inc="lnr")
            W(VE, "lnr", pg.cur("lnr"))
            A(VE, lambda E, b=buf, xb=xnb, pb=pb: E.tensor_scalar(
                xb[:], b[:], lnsc[:, pb + 2:pb + 3], lnsc[:, pb + 7:pb + 8],
                ALU.subtract, ALU.mult), inc="lnc")
            if i >= 8:
                W(SY, "lnc", pg.cur("lnc"))
                A(SY, lambda E, i=i, xb=xnb: E.dma_start(
                    out=xn_tm_dr[(i - 8) * P:(i - 7) * P, :], in_=xb[:]),
                  inc="lnw", inck=16)
                if debug:
                    A(SY, lambda E, i=i, xb=xnb: E.dma_start(
                        out=dbg["d_xn_tm"][(i - 8) * P:(i - 7) * P, :],
                        in_=xb[:]), inc="lnw", inck=16)
            lnw_after[i] = pg.cur("lnw")
            for g in range(2):
                gidx = 2 * i + g
                W(TE, "lnc", pg.cur("lnc"))
                W(TE, "tpe", gidx)
                for c in range(8):
                    cc = g * 8 + c
                    A(TE, lambda E, xb=xnb, cc=cc, c=c: E.transpose(
                        ps_tp[:, c * 128:(c + 1) * 128],
                        xb[:, cc * 128:(cc + 1) * 128], ident[:]),
                      inc="tpt" if c == 7 else None)
                W(VE, "tpt", pg.cur("tpt"))
                dst = xnctx if i < 8 else xnown
                col = (i % 8) * 128
                A(VE, lambda E, g=g, d=dst, col=col: E.tensor_copy(
                    d[:].rearrange("p (c t) -> p c t", t=1024)
                       [:, g * 8:(g + 1) * 8, col:col + 128],
                    ps_tp[:].rearrange("p (c t) -> p c t", t=128)),
                  inc="tpe")
    W(SY, "tpe", pg.cur("tpe"))
    for c in range(DC):
        A(SY, lambda E, c=c: E.dma_start(
            out=xn_fm_dr[c], in_=xnown[:, c * 1024:(c + 1) * 1024]),
          inc="xfw", inck=16)

    if phase_limit <= 1:
        for _lbl in ("lnw", "dbq", "dbk", "dba", "dbc", "dbg1", "xfw",
                     "gwr", "ssp", "cld"):
            W(SY, _lbl, pg.cur(_lbl))
        mem.commit()
        with nc.allow_low_precision(reason="dev"):
            pg.run()
        return nc
    PHASE_GUARD("lnx", "lnw")
    with nc.psum_tensor("psQa", [P, 1024], F32) as psQa, \
         nc.psum_tensor("psQb", [P, 1024], F32) as psQb, \
         nc.psum_tensor("psQs", [1, 1024], F32) as psQs, \
         nc.psum_tensor("psQr", [P, 1024], F32) as psQr:
        W(TE, "tpe", pg.cur("tpe"))
        for half in range(2):
            if half == 1:
                load_wq_half(1)
            for oc in range(8):
                o = half * 8 + oc
                ps = psQa if o % 2 == 0 else psQb
                W(TE, "wql", 16 * DC * (half + 1))
                if o >= 2:
                    W(TE, "qsq", o - 1)
                    W(TE, "qev", o - 1)
                for c in range(DC):
                    for n2 in range(2):
                        A(TE, lambda E, c=c, oc=oc, n2=n2, ps=ps: E.matmul(
                            ps[:, n2 * 512:(n2 + 1) * 512],
                            wq_sb[:, c * 1024 + oc * 128:
                                  c * 1024 + (oc + 1) * 128],
                            xnown[:, c * 1024 + n2 * 512:
                                  c * 1024 + (n2 + 1) * 512],
                            start=(c == 0), stop=(c == DC - 1)),
                          inc="qmm" if (c == DC - 1 and n2 == 1) else None)
                W(VE, "qmm", pg.cur("qmm"))
                A(VE, lambda E, ps=ps, o=o: E.tensor_copy(
                    qfm[:, o * 1024:(o + 1) * 1024], ps[:]), inc="qev")
                A(VE, lambda E, o=o: E.tensor_mul(
                    qsq_sc[:], qfm[:, o * 1024:(o + 1) * 1024],
                    qfm[:, o * 1024:(o + 1) * 1024]), inc="qsq")
                W(TE, "qsq", pg.cur("qsq"))
                for n2 in range(2):
                    A(TE, lambda E, n2=n2, o=o: E.matmul(
                        psQs[:, n2 * 512:(n2 + 1) * 512], ones_col[:],
                        qsq_sc[:, n2 * 512:(n2 + 1) * 512],
                        start=(o == 0), stop=(o == 15),
                        skip_group_check=True),
                      inc="qss" if n2 == 1 else None)
        if debug:
            W(SY, "qev", pg.cur("qev"))
            for c in range(DC):
                A(SY, lambda E, c=c: E.dma_start(
                    out=dbg["d_qraw"][c], in_=qfm[:, c * 1024:(c + 1) * 1024]),
                  inc="dbqr", inck=16)
            W(SY, "dbqr", pg.cur("dbqr"))
        if phase_limit == 12:
            mem.commit()
            with nc.allow_low_precision(reason="dev"):
                pg.run()
            return nc
        if debug:
            W(SC, "qss", pg.cur("qss"))
            A(SC, lambda E: E.copy(dbsnap[0:1, 0:1024], psQs[:]), inc="dbsn")
            W(SY, "dbsn", pg.cur("dbsn"))
            A(SY, lambda E: E.dma_start(out=dbg["d_psqs"],
                                        in_=dbsnap[0:1, 0:1024]),
              inc="dbq", inck=16)
        W(SC, "qss", pg.cur("qss"))
        A(SC, lambda E: E.activation(rr_q[:], psQs[:], AF.Ln, scale=1.0 / D,
                                     bias=EPS_RMS))
        A(SC, lambda E: E.activation(rr_q[:], rr_q[:], AF.Exp, scale=-0.5),
          inc="qrs")
        W(TE, "qrs", pg.cur("qrs"))
        W(TE, "cst", pg.cur("cst"))
        for n2 in range(2):
            A(TE, lambda E, n2=n2: E.matmul(
                psQr[:, n2 * 512:(n2 + 1) * 512], ones_f[:],
                rr_q[:, n2 * 512:(n2 + 1) * 512], start=True, stop=True),
              inc="qrb" if n2 == 1 else None)
        W(SC, "qrb", pg.cur("qrb"))
        A(SC, lambda E: E.copy(rstb[:], psQr[:]), inc="qre")
        if phase_limit == 15:
            mem.commit()
            with nc.allow_low_precision(reason="dev"):
                pg.run()
            return nc
        if debug:
            W(SY, "qre", pg.cur("qre"))
            A(SY, lambda E: E.dma_start(out=dbg["d_rstb"], in_=rstb[:]),
              inc="dbq", inck=16)
            W(SY, "qrs", pg.cur("qrs"))
            A(SY, lambda E: E.dma_start(out=dbg["d_rrq"], in_=rr_q[:]),
              inc="dbq", inck=16)
        W(VE, "qre", pg.cur("qre"))
        W(VE, "qev", pg.cur("qev"))
        W(VE, "dbqr", pg.cur("dbqr"))
        for o in range(DC):
            A(VE, lambda E, o=o: E.tensor_mul(
                qfm[:, o * 1024:(o + 1) * 1024],
                qfm[:, o * 1024:(o + 1) * 1024], rstb[:]))
            A(VE, lambda E, o=o: E.tensor_scalar(
                qfm[:, o * 1024:(o + 1) * 1024],
                qfm[:, o * 1024:(o + 1) * 1024],
                qnw[:, o:o + 1], 0.125, ALU.mult, ALU.mult), inc="qsc")
    for c in range(DC):
        A(GP, lambda E, c=c: E.dma_start(
            out=wk_sb[:, c * 512:(c + 1) * 512],
            in_=wkT[c * P:(c + 1) * P, :]), inc="wkl", inck=16)
    if debug:
        W(SY, "qsc", pg.cur("qsc"))

        for c in range(DC):
            A(SY, lambda E, c=c: E.dma_start(
                out=dbg["d_q"][c], in_=qfm[:, c * 1024:(c + 1) * 1024]),
              inc="dbq", inck=16)

    if phase_limit <= 2:
        for _lbl in ("lnw", "dbq", "dbk", "dba", "dbc", "dbg1", "xfw",
                     "gwr", "ssp", "cld"):
            W(SY, _lbl, pg.cur(_lbl))
        mem.commit()
        with nc.allow_low_precision(reason="dev"):
            pg.run()
        return nc
    PHASE_GUARD("dbqr", "dbq")
    W(GP, "qsc", pg.cur("qsc"))
    for v in range(NKV):
        A(GP, lambda E, v=v: E.memset(kaug[v][64:67, :], 1.0),
          inc="kcst" if v == NKV - 1 else None)
    with nc.psum_tensor("psKa", [P, 1024], F32) as psKa, \
         nc.psum_tensor("psKb", [P, 1024], F32) as psKb, \
         nc.psum_tensor("psKs", [1, 1024], F32) as psKs, \
         nc.psum_tensor("psKr", [P, 1024], F32) as psKr:
        W(TE, "qev", pg.cur("qev"))
        W(TE, "qre", pg.cur("qre"))
        for th in range(2):
            xsrc = xnctx if th == 0 else xnown
            for oc in range(4):
                idx = th * 4 + oc
                ps = psKa if idx % 2 == 0 else psKb
                W(TE, "wkl", 16 * DC)
                if idx >= 2:
                    W(TE, "ksq", idx - 1)
                    W(TE, "kev", 2 * (idx - 1))
                for c in range(DC):
                    for n2 in range(2):
                        A(TE, lambda E, c=c, oc=oc, n2=n2, ps=ps, xs=xsrc:
                          E.matmul(
                            ps[:, n2 * 512:(n2 + 1) * 512],
                            wk_sb[:, c * 512 + oc * 128:
                                  c * 512 + (oc + 1) * 128],
                            xs[:, c * 1024 + n2 * 512:
                               c * 1024 + (n2 + 1) * 512],
                            start=(c == 0), stop=(c == DC - 1)),
                          inc="kmm" if (c == DC - 1 and n2 == 1) else None)
                W(VE, "kmm", pg.cur("kmm"))
                if idx == 0:
                    W(VE, "kcst", pg.cur("kcst"))
                for sub in range(2):
                    kv = oc * 2 + sub
                    A(VE, lambda E, ps=ps, kv=kv, sub=sub, th=th:
                      E.tensor_copy(
                        kaug[kv][0:64, th * 1024:(th + 1) * 1024],
                        ps[sub * 64:(sub + 1) * 64, :]), inc="kev")
                A(VE, lambda E, ps=ps: E.tensor_copy(ksq_f[:], ps[:]))
                A(VE, lambda E: E.tensor_mul(ksq_sc[:], ksq_f[:], ksq_f[:]),
                  inc="ksq")
                W(TE, "ksq", pg.cur("ksq"))
                for n2 in range(2):
                    A(TE, lambda E, n2=n2, oc=oc: E.matmul(
                        psKs[:, n2 * 512:(n2 + 1) * 512], ones_col[:],
                        ksq_sc[:, n2 * 512:(n2 + 1) * 512],
                        start=(oc == 0), stop=(oc == 3),
                        skip_group_check=True),
                      inc="kss" if n2 == 1 else None)
            W(SC, "kss", pg.cur("kss"))
            A(SC, lambda E, th=th: E.activation(
                rr_k[:, th * 1024:(th + 1) * 1024], psKs[:], AF.Ln,
                scale=1.0 / DKV, bias=EPS_RMS))
            A(SC, lambda E, th=th: E.activation(
                rr_k[:, th * 1024:(th + 1) * 1024],
                rr_k[:, th * 1024:(th + 1) * 1024], AF.Exp, scale=-0.5),
              inc="krs")
            W(TE, "krs", pg.cur("krs"))
            if th == 1:
                W(TE, "kre", 1)
            for n2 in range(2):
                A(TE, lambda E, th=th, n2=n2: E.matmul(
                    psKr[:, n2 * 512:(n2 + 1) * 512], ones_f[:],
                    rr_k[:, th * 1024 + n2 * 512:
                         th * 1024 + (n2 + 1) * 512],
                    start=True, stop=True, skip_group_check=True),
                  inc="krb" if n2 == 1 else None)
            W(SC, "krb", pg.cur("krb"))
            A(SC, lambda E: E.copy(rstb[:], psKr[:]), inc="kre")
            W(VE, "kre", pg.cur("kre"))
            W(VE, "kev", pg.cur("kev"))
            for kv in range(NKV):
                A(VE, lambda E, kv=kv, th=th: E.tensor_mul(
                    kaug[kv][0:64, th * 1024:(th + 1) * 1024],
                    kaug[kv][0:64, th * 1024:(th + 1) * 1024],
                    rstb[0:64, :]))
                A(VE, lambda E, kv=kv, th=th: E.tensor_scalar(
                    kaug[kv][0:64, th * 1024:(th + 1) * 1024],
                    kaug[kv][0:64, th * 1024:(th + 1) * 1024],
                    knw[0:64, kv:kv + 1],
                    None, ALU.mult), inc="ksc")
    W(GP, "qsc", pg.cur("qsc"))
    for c in range(DC):
        A(GP, lambda E, c=c: E.dma_start(
            out=wv_sb[:, c * 512:(c + 1) * 512],
            in_=wvT[c * P:(c + 1) * P, :]), inc="wvl", inck=16)
    if debug:
        W(SY, "ksc", pg.cur("ksc"))
        for kv in range(NKV):
            A(SY, lambda E, kv=kv: E.dma_start(
                out=dbg["d_k"][kv], in_=kaug[kv][:]), inc="dbk", inck=16)

    if phase_limit <= 3:
        for _lbl in ("lnw", "dbq", "dbk", "dba", "dbc", "dbg1", "xfw",
                     "gwr", "ssp", "cld"):
            W(SY, _lbl, pg.cur(_lbl))
        mem.commit()
        with nc.allow_low_precision(reason="dev"):
            pg.run()
        return nc
    PHASE_GUARD("dbk")
    with nc.psum_tensor("psVa", [P, 512], F32) as psVa, \
         nc.psum_tensor("psVb", [P, 512], F32) as psVb:
        W(TE, "kev", pg.cur("kev"))
        W(TE, "kre", pg.cur("kre"))
        W(VE, "cld")
        for st in range(NKB):
            ps = psVa if st % 2 == 0 else psVb
            W(TE, "wvl", 16 * DC)
            if st >= 2:
                W(TE, "vev", 2 * (st - 1))
            xsrc = xnctx if st < 8 else xnown
            col = (st % 8) * 128
            for c in range(DC):
                A(TE, lambda E, c=c, ps=ps, xs=xsrc, col=col: E.matmul(
                    ps[:], xs[:, c * 1024 + col: c * 1024 + col + 128],
                    wv_sb[:, c * 512:(c + 1) * 512],
                    start=(c == 0), stop=(c == DC - 1)),
                  inc="vmm" if c == DC - 1 else None)
            W(VE, "vmm", pg.cur("vmm"))
            A(VE, lambda E, ps=ps, st=st: E.tensor_copy(
                vaug[:, st * 520:(st + 1) * 520].rearrange(
                    "p (k c) -> p k c", c=65)[:, :, 0:64],
                ps[:].rearrange("p (k c) -> p k c", c=64)), inc="vev")
            A(VE, lambda E, st=st: E.tensor_copy(
                vaug[:, st * 520:(st + 1) * 520].rearrange(
                    "p (k c) -> p k c", c=65)[:, :, 64:65],
                ctxv_s[:, st * 8:(st + 1) * 8].rearrange(
                    "p (k c) -> p k c", c=1)), inc="vev")

    if phase_limit <= 4:
        for _lbl in ("lnw", "dbq", "dbk", "dba", "dbc", "dbg1", "xfw",
                     "gwr", "ssp", "cld"):
            W(SY, _lbl, pg.cur(_lbl))
        mem.commit()
        with nc.allow_low_precision(reason="dev"):
            pg.run()
        return nc
    with nc.psum_tensor("psQall", [NH, 1024], F32) as psQall, \
         nc.psum_tensor("psKall", [NKV, 2048], F32) as psKall:
        W(TE, "vev", pg.cur("vev"))
        W(TE, "qsc", pg.cur("qsc"))
        W(TE, "ksc", pg.cur("ksc"))
        W(VE, "ksc", pg.cur("ksc"))
        W(VE, "qsc", pg.cur("qsc"))
        W(VE, "vev", pg.cur("vev"))
        for kv in range(NKV):
            if kv >= 1:
                W(VE, "kbm", kv)
            A(VE, lambda E, kv=kv: E.tensor_mul(
                attfm[0:64, 0:TCTX], kaug[kv][0:64, :], kaug[kv][0:64, :]),
              inc="ks2")
            W(TE, "ks2", kv + 1)
            for i in range(4):
                A(TE, lambda E, kv=kv, i=i: E.matmul(
                    psKall[:, i * 512:(i + 1) * 512],
                    ksel_s[:, kv * NKV:(kv + 1) * NKV],
                    attfm[0:64, i * 512:(i + 1) * 512],
                    start=(kv == 0), stop=(kv == NKV - 1),
                    skip_group_check=True),
                  inc="kbm" if i == 3 else None)
        W(VE, "kbm", pg.cur("kbm"))
        A(VE, lambda E: E.reduce_max(kssm[:], psKall[:], axis=AX),
          inc="kmx")
        for c in range(DC):
            sq = pt0 if c % 2 == 0 else pt1
            if c >= 2:
                W(VE, "qbm", c - 1)
            A(VE, lambda E, c=c, sq=sq: E.tensor_mul(
                sq[:, 0:1024], qfm[:, c * 1024:(c + 1) * 1024],
                qfm[:, c * 1024:(c + 1) * 1024]), inc="qs2")
            W(TE, "qs2", c + 1)
            for n2 in range(2):
                A(TE, lambda E, c=c, n2=n2, sq=sq: E.matmul(
                    psQall[:, n2 * 512:(n2 + 1) * 512],
                    hsel_s[:, c * NH:(c + 1) * NH],
                    sq[:, n2 * 512:(n2 + 1) * 512],
                    start=(c == 0), stop=(c == DC - 1),
                    skip_group_check=True),
                  inc="qbm" if n2 == 1 else None)
        W(VE, "qbm", pg.cur("qbm"))
        A(VE, lambda E: E.reduce_max(qssm[:], psQall[:], axis=AX),
          inc="qmx")
    with nc.psum_tensor("psMB", [P, NH], F32) as psMB, \
         nc.psum_tensor("psMr", [1, NH], BF16) as psMr, \
         nc.psum_tensor("psE", [NH, 1], F32) as psE:
        W(TE, "kmx", 1)
        W(TE, "qmx", 1)
        A(TE, lambda E: E.matmul(psE[:], e832_s[:], kssm[:],
                                 start=True, stop=True,
                                 skip_group_check=True), inc="bex")
        W(VE, "bex", pg.cur("bex"))
        A(VE, lambda E: E.tensor_mul(prod32[:], qssm[:], psE[:]), inc="bpr")
        W(SC, "bpr", pg.cur("bpr"))
        A(SC, lambda E: E.activation(prod32[:], prod32[:], AF.Ln,
                                     bias=1e-20))
        A(SC, lambda E: E.activation(prod32[:], prod32[:], AF.Exp,
                                     scale=0.5))
        A(SC, lambda E: E.activation(m32[:], prod32[:], AF.Copy,
                                     scale=-1.01, bias=-0.5), inc="am3")
        W(TE, "am3", pg.cur("am3"))
        A(TE, lambda E: E.transpose(psMr[:], m32[:], ident[0:NH, 0:NH]),
          inc="amr")
        W(SC, "amr", pg.cur("amr"))
        A(SC, lambda E: E.copy(mrow[:], psMr[:]), inc="amw")
        W(TE, "amw", pg.cur("amw"))
        A(TE, lambda E: E.matmul(psMB[:], ones_bf[:], mrow[:],
                                 start=True, stop=True,
                                 skip_group_check=True), inc="am4")
        W(SC, "am4", pg.cur("am4"))
        A(SC, lambda E: E.copy(mcol[:], psMB[:]), inc="am5")
        W(VE, "am5", pg.cur("am5"))
        for h in range(NH):
            A(VE, lambda E, h=h: E.tensor_scalar(
                biascols[:, h * NKB:(h + 1) * NKB],
                bcolbase[:, h * NKB:(h + 1) * NKB],
                mcol[:, h:h + 1], None, ALU.add), inc="am6")

    W(GP, "vmm", pg.cur("vmm"))
    W(GP, "ksc", pg.cur("ksc"))
    for c in range(8):
        for i in range(2):
            A(GP, lambda E, c=c, i=i: E.dma_start(
                out=wo_sb[i][:, c * 1024:(c + 1) * 1024],
                in_=woT[c * P:(c + 1) * P, i * 1024:(i + 1) * 1024]),
              inc="wol", inck=16)

    pieces = []
    for b in range(NKB):
        j0, j1 = max(0, b - 8), min(NQT - 1, b)
        wdt = (j1 - j0 + 1) * 128
        cuts = list(range(0, wdt, 512)) + [wdt]
        pieces.append((j0, j1,
                       [(cuts[i], cuts[i + 1])
                        for i in range(len(cuts) - 1)]))
    with nc.psum_tensor("psB0", [P, 512], F32) as psB0, \
         nc.psum_tensor("psB1", [P, 512], F32) as psB1, \
         nc.psum_tensor("psB2", [P, 512], F32) as psB2, \
         nc.psum_tensor("psO0", [65, 1024], F32) as psO0, \
         nc.psum_tensor("psO1", [65, 1024], F32) as psO1, \
         nc.psum_tensor("psRc", [64, 512], F32) as psRc:
        W(TE, "amx", pg.cur("amx"))
        W(TE, "am6", pg.cur("am6"))
        W(SC, "am6", pg.cur("am6"))
        pcount = 0
        qauX = (qaug, qaug1)
        bmm_after = {}

        def stage_qaug(hh):
            qa = qauX[hh % 2]
            g = bmm_after.get(hh - 2)
            if g:
                W(VE, "bmm", g)
                W(SY, "bmm", g)
            A(VE, lambda E, hh=hh, qa=qa: E.tensor_copy(
                qa[0:64, :], qfm[64 * (hh % 2):64 * (hh % 2) + 64,
                                 (hh // 2) * 1024:(hh // 2) * 1024 + 1024]),
              inc="qau")
            A(SY, lambda E, hh=hh, qa=qa: E.dma_start(out=qa[64:67, :],
                                                      in_=crow_d[hh]),
              inc="qauD", inck=16)

        def emit_tail(hh):
            psOt = psO0 if hh % 2 == 0 else psO1
            qrowt = 64 * (hh % 2)
            qcolt = (hh // 2) * 1024
            W(TE, "pvr", hh + 1)
            for n2 in range(2):
                if 2 * hh + n2 >= 1:
                    W(TE, "pvc", 2 * hh + n2)
                A(TE, lambda E, n2=n2: E.matmul(
                    psRc[:], ones_bf[:, 0:64],
                    recip_row[:, n2 * 512:(n2 + 1) * 512],
                    start=True, stop=True, skip_group_check=True),
                  inc="pvb")
                W(SC, "pvb", pg.cur("pvb"))
                if 2 * hh + n2 >= 1:
                    W(SC, "pvd2", 2 * hh + n2)
                A(SC, lambda E, n2=n2: E.copy(sbR[:], psRc[:]), inc="pvc")
                W(VE, "pvc", pg.cur("pvc"))
                A(VE, lambda E, qrowt=qrowt, qcolt=qcolt, n2=n2, psOt=psOt:
                  E.tensor_mul(
                    attfm[qrowt:qrowt + 64,
                          qcolt + n2 * 512: qcolt + (n2 + 1) * 512],
                    psOt[0:64, n2 * 512:(n2 + 1) * 512],
                    sbR[:]), inc="pvd2")

        W(SY, "vev", pg.cur("vev"))
        W(SY, "vmm", pg.cur("vmm"))
        W(VE, "qsc", pg.cur("qsc"))
        stage_qaug(0)
        stage_qaug(1)
        psBs = (psB0, psB1, psB2)
        for h in range(NH):
            kv = h // 4
            qrow = 64 * (h % 2)
            qcol = (h // 2) * 1024
            qaug_h = qauX[h % 2]
            psO = psO0 if h % 2 == 0 else psO1
            W(TE, "qau", h + 1)
            W(TE, "qauD", 16 * (h + 1))
            if h >= 2:
                W(TE, "pvd2", 2 * (h - 1))
            for n2 in range(2):
                A(TE, lambda E, n2=n2, psO=psO: E.matmul(
                    psO[:, n2 * 512:(n2 + 1) * 512], ones_bf[0:1, 0:65],
                    zrow[0:1, n2 * 512:(n2 + 1) * 512],
                    start=True, stop=False, skip_group_check=True))
            pending_pv = None
            for b in range(NKB):
                if b == 6 and h >= 1:
                    emit_tail(h - 1)
                j0, j1, pcs = pieces[b]
                pt = pt0 if b % 2 == 0 else pt1
                for (p0, p1) in pcs:
                    ps = psBs[pcount % 3]
                    if pcount >= 3:
                        W(TE, "exb", pcount - 2)
                    A(TE, lambda E, kv=kv, b=b, j0=j0, p0=p0, p1=p1, ps=ps,
                      qa=qaug_h:
                      E.matmul(ps[:, 0:p1 - p0],
                               kaug[kv][:, b * 128:(b + 1) * 128],
                               qa[:, j0 * 128 + p0: j0 * 128 + p1],
                               start=True, stop=True,
                               skip_group_check=True), inc="bmm")
                    has_diag = (b >= 8 and p0 == 0)
                    if has_diag:
                        W(VE, "bmm", pg.cur("bmm"))
                        A(VE, lambda E, ps=ps: E.tensor_add(
                            ps[:, 0:128], ps[:, 0:128], mdg[:]), inc="bmk")
                    W(SC, "bmm", pg.cur("bmm"))
                    if has_diag:
                        W(SC, "bmk", pg.cur("bmk"))
                    A(SC, lambda E, ps=ps, pt=pt, h=h, b=b, p0=p0, p1=p1:
                      E.activation(pt[:, p0:p1], ps[:, 0:p1 - p0], AF.Exp,
                                   bias=biascols[:, h * NKB + b:
                                                 h * NKB + b + 1]),
                      inc="exb")
                    pcount += 1
                if b <= 7:
                    off = (b - j0) * 128
                    W(VE, "exb", pg.cur("exb"))
                    A(VE, lambda E, pt=pt, off=off: E.tensor_mul(
                        pt[:, off:off + 128], pt[:, off:off + 128],
                        m01t[:]), inc="bmt")

                def make_pv(b=b, j0=j0, j1=j1, kv=kv, pt=pt, psO=psO,
                            exbv=None, bmtv=None):
                    exbv = pg.cur("exb")
                    bmtv = pg.cur("bmt")

                    def emit_pv():
                        W(TE, "exb", exbv)
                        if b <= 7:
                            W(TE, "bmt", bmtv)
                        vs0 = b * 520 + kv * 65
                        wdt = (j1 - j0 + 1) * 128
                        if b <= 7:
                            ow = (b - j0) * 128
                            for q0 in range(0, ow, 512):
                                qw = min(512, ow - q0)
                                A(TE, lambda E, vs0=vs0, j0=j0, q0=q0, qw=qw,
                                  pt=pt, psO=psO:
                                  E.matmul(
                                    psO[:, j0 * 128 + q0:
                                        j0 * 128 + q0 + qw],
                                    vaug[:, vs0:vs0 + 65], pt[:, q0:q0 + qw],
                                    start=False, stop=False,
                                    skip_group_check=True))
                            A(TE, lambda E, vs0=vs0, b=b, pt=pt, ow=ow,
                              psO=psO:
                              E.matmul(
                                psO[:, b * 128:(b + 1) * 128],
                                vaug[:, vs0:vs0 + 65], pt[:, ow:ow + 128],
                                start=False, stop=False,
                                skip_group_check=True),
                              inc="pvm")
                        else:
                            a0, a1 = j0 * 128, j0 * 128 + wdt
                            cuts = ([a0] + [c for c in range(512, 2048, 512)
                                            if a0 < c < a1] + [a1])
                            for ci in range(len(cuts) - 1):
                                c0, c1 = cuts[ci], cuts[ci + 1]
                                last = (c1 == a1)
                                A(TE, lambda E, vs0=vs0, c0=c0, c1=c1, a0=a0,
                                  pt=pt, psO=psO,
                                  st_=(b == NKB - 1 and last): E.matmul(
                                    psO[:, c0:c1],
                                    vaug[:, vs0:vs0 + 65],
                                    pt[:, c0 - a0:c1 - a0],
                                    start=False, stop=st_,
                                    skip_group_check=True),
                                  inc="pvm" if last else None)
                    return emit_pv
                if pending_pv is not None:
                    pending_pv()
                pending_pv = make_pv()
            pending_pv()
            bmm_after[h] = pg.cur("bmm")
            if h + 2 < NH:
                stage_qaug(h + 2)
            W(VE, "pvm", pg.cur("pvm"))
            if h >= 1:
                W(VE, "pvb", 2 * h)
            A(VE, lambda E, psO=psO: E.reciprocal(recip_row[:],
                                                  psO[64:65, :]),
              inc="pvr")
        emit_tail(NH - 1)

    if debug:
        W(SY, "pvd2", pg.cur("pvd2"))
        for c in range(DC):
            A(SY, lambda E, c=c: E.dma_start(
                out=dbg["d_att"][c], in_=attfm[:, c * 1024:(c + 1) * 1024]),
              inc="dba", inck=16)

    if phase_limit <= 5:
        for _lbl in ("lnw", "dbq", "dbk", "dba", "dbc", "dbg1", "xfw",
                     "gwr", "ssp", "cld"):
            W(SY, _lbl, pg.cur(_lbl))
        mem.commit()
        with nc.allow_low_precision(reason="dev"):
            pg.run()
        return nc
    PHASE_GUARD("dba")
    W(GP, "pvd2", 2 * NH)
    W(GP, "exb", pg.cur("exb"))
    for c in range(8):
        for i in range(2):
            A(GP, lambda E, c=c, i=i: E.dma_start(
                out=wo_sb2[i][:, c * 1024:(c + 1) * 1024],
                in_=woT[(8 + c) * P:(9 + c) * P, i * 1024:(i + 1) * 1024]),
              inc="wol2", inck=16)
    for c in range(DC):
        A(GP, lambda E, c=c: E.dma_start(
            out=gate_sb[:, c * NE:(c + 1) * NE],
            in_=gateT[c * P:(c + 1) * P, :]), inc="wgl", inck=16)
    with nc.psum_tensor("psW0", [P, 512], F32) as psW0, \
         nc.psum_tensor("psW1", [P, 512], F32) as psW1, \
         nc.psum_tensor("psW2", [P, 512], F32) as psW2, \
         nc.psum_tensor("psW3", [P, 512], F32) as psW3:
        psWs = [psW0, psW1, psW2, psW3]
        W(TE, "pvd2", pg.cur("pvd2"))
        W(TE, "pvc", pg.cur("pvc"))
        W(TE, "exb", pg.cur("exb"))
        for tt in range(NQT):
            for og in range(4):
                ps = psWs[og]
                W(TE, "wol", 16 * 16)
                if tt == 0:
                    W(TE, "wol2", 16 * 16)
                if tt >= 1:
                    W(TE, "wev", (tt - 1) * 4 + og + 1)
                for c in range(DC):
                    wsb = (wo_sb if c < 8 else wo_sb2)[og // 2]
                    cc = c % 8
                    A(TE, lambda E, tt=tt, og=og, c=c, cc=cc, ps=ps, wsb=wsb:
                      E.matmul(
                        ps[:],
                        attfm[:, c * 1024 + tt * 128:
                              c * 1024 + (tt + 1) * 128],
                        wsb[:, cc * 1024 + (og % 2) * 512:
                            cc * 1024 + (og % 2) * 512 + 512],
                        start=(c == 0), stop=(c == DC - 1)),
                      inc="wmm" if c == DC - 1 else None)
                W(VE, "wmm", pg.cur("wmm"))
                if og == 0 and tt >= 1:
                    W(VE, "ssp", pg.cur("ssp"))
                A(VE, lambda E, ps=ps, og=og: E.tensor_copy(
                    s_tile[:, og * 512:(og + 1) * 512], ps[:]), inc="wev")
            W(SY, "wev", pg.cur("wev"))
            A(SY, lambda E, tt=tt: E.dma_start(out=s_dr[tt], in_=s_tile[:]),
              inc="ssp", inck=16)
            if debug:
                A(SY, lambda E, tt=tt: E.dma_start(
                    out=dbg["d_s"][tt], in_=s_tile[:]), inc="ssp", inck=16)

    with nc.psum_tensor("psG", [P, NE], F32) as psG, \
         nc.psum_tensor("psT1", [NE, P], F32) as psT1, \
         nc.psum_tensor("psT2", [P, NE], F32) as psT2:
        W(TE, "wev", pg.cur("wev"))
        W(TE, "wgl", 16 * DC)
        A(VE, lambda E: E.memset(pad8[:, 4:8], -1e30), inc="gpad")
        for tt in range(NQT):
            if tt >= 1:
                W(TE, "gsg", tt)
            for c in range(DC):
                A(TE, lambda E, c=c, tt=tt: E.matmul(
                    psG[:],
                    xnown[:, c * 1024 + tt * 128: c * 1024 + (tt + 1) * 128],
                    gate_sb[:, c * NE:(c + 1) * NE],
                    start=(c == 0), stop=(c == DC - 1)),
                  inc="gmm" if c == DC - 1 else None)
            W(SC, "gmm", pg.cur("gmm"))
            A(SC, lambda E, tt=tt: E.activation(
                scrs[:, tt * NE:(tt + 1) * NE], psG[:], AF.Sigmoid),
              inc="gsg")
            W(VE, "gsg", pg.cur("gsg"))
            for g in range(4):
                A(VE, lambda E, g=g, tt=tt: E.tensor_copy(
                    pad8[:, 0:4],
                    scrs[:, tt * NE + 4 * g: tt * NE + 4 * g + 4]))
                A(VE, lambda E: E.max(out=m8[:], in_=pad8[:]))
                A(VE, lambda E, g=g: E.tensor_add(
                    gsc[:, g:g + 1], m8[:, 0:1], m8[:, 1:2]))
            A(VE, lambda E: E.tensor_copy(pad8[:, 0:4], gsc[:]))
            A(VE, lambda E: E.max(out=m8[:], in_=pad8[:]), inc="gm8")
            W(SC, "gm8", pg.cur("gm8"))
            A(SC, lambda E: E.copy(acpy[:, 0:1], m8[:, 1:2]), inc="gth")
            W(VE, "gth", pg.cur("gth"))
            A(VE, lambda E: E.tensor_scalar(
                sltmp[:, 0:4] if False else gsc[:], gsc[:], acpy[:, 0:1],
                None, ALU.is_ge), inc="gmk")
            W(SC, "gmk", pg.cur("gmk"))
            A(SC, lambda E: E.copy(acpy[:, 4:8], gsc[:]), inc="gmkA")
            W(VE, "gmkA", pg.cur("gmkA"))
            for g in range(4):
                A(VE, lambda E, g=g, tt=tt: E.tensor_scalar(
                    msk16[:, 4 * g:4 * g + 4],
                    scrs[:, tt * NE + 4 * g: tt * NE + 4 * g + 4],
                    acpy[:, 4 + g:5 + g], None, ALU.mult))
            A(VE, lambda E: E.max(out=m8[:], in_=msk16[:]), inc="gt2")
            W(SC, "gt2", pg.cur("gt2"))
            A(SC, lambda E: E.copy(acpy[:, 1:3], m8[:, 0:2]), inc="gt2A")
            W(VE, "gt2A", pg.cur("gt2A"))
            A(VE, lambda E: E.tensor_add(sltmp[:, 4:5], m8[:, 0:1],
                                         m8[:, 1:2]))
            A(VE, lambda E: E.tensor_scalar(
                sltmp[:, 4:5], sltmp[:, 4:5], 1e-20, None, ALU.add))
            A(VE, lambda E: E.reciprocal(sltmp[:, 5:6], sltmp[:, 4:5]),
              inc="grc")
            W(SC, "grc", pg.cur("grc"))
            A(SC, lambda E: E.copy(acpy[:, 3:4], sltmp[:, 5:6]), inc="grcA")
            W(VE, "grcA", pg.cur("grcA"))
            A(VE, lambda E: E.tensor_scalar(
                cm16[:], msk16[:], acpy[:, 2:3], None, ALU.is_ge))
            A(VE, lambda E, tt=tt: E.tensor_mul(
                cw16[:, tt * NE:(tt + 1) * NE], msk16[:], cm16[:]))
            A(VE, lambda E, tt=tt: E.tensor_scalar(
                cw16[:, tt * NE:(tt + 1) * NE],
                cw16[:, tt * NE:(tt + 1) * NE],
                acpy[:, 3:4], None, ALU.mult), inc="gcw")
            W(TE, "gcw", pg.cur("gcw"))
            if tt >= 1:
                W(TE, "gte", tt)
            A(TE, lambda E: E.transpose(psT1[:], cm16[:], ident_f[:]),
              inc="gtp")
            W(VE, "gtp", pg.cur("gtp"))
            A(VE, lambda E, tt=tt: E.tensor_copy(
                rt_a[:, tt * 128:(tt + 1) * 128], psT1[:]), inc="gte")
        W(VE, "gte", pg.cur("gte"))
        A(VE, lambda E: E.tensor_copy(rt_c[:], rt_a[:]))
        src, dst = rt_a, rt_b
        for sh in [1, 2, 4, 8, 16, 32, 64, 128, 256, 512]:
            A(VE, lambda E, s=src, d=dst, sh=sh: E.tensor_copy(
                d[:, 0:sh], s[:, 0:sh]))
            A(VE, lambda E, s=src, d=dst, sh=sh: E.tensor_add(
                d[:, sh:TOWN], s[:, sh:TOWN], s[:, 0:TOWN - sh]))
            src, dst = dst, src
        A(VE, lambda E, s=src: E.tensor_sub(s[:], s[:], rt_c[:]))
        A(VE, lambda E, s=src, d=dst: E.tensor_scalar(
            d[:], s[:], CAP - 0.5, None, ALU.is_ge))
        A(VE, lambda E, s=src: E.tensor_scalar(
            s[:], s[:], e256_s[:, 0:1], None, ALU.add))
        A(VE, lambda E, s=src, d=dst: E.scalar_tensor_tensor(
            s[:], d[:], BIG, s[:], ALU.mult, ALU.add), inc="gps")
        slotT = src
        for tt in range(NQT):
            W(TE, "gps", pg.cur("gps"))
            if tt >= 1:
                W(TE, "gsl", tt)
            A(TE, lambda E, tt=tt, s=slotT: E.transpose(
                psT2[:], s[:, tt * 128:(tt + 1) * 128],
                ident_f[0:NE, 0:NE]), inc="gtq")
            W(VE, "gtq", pg.cur("gtq"))
            A(VE, lambda E: E.tensor_copy(slot_tm[:], psT2[:]), inc="gsl")
            A(VE, lambda E, tt=tt: E.tensor_scalar(
                cm16[:], cw16[:, tt * NE:(tt + 1) * NE], 0.0, None,
                ALU.is_gt))
            A(VE, lambda E: E.tensor_copy(rk_a[:], cm16[:]))
            ra, rb = rk_a, rk_b
            for sh in (1, 2, 4, 8):
                A(VE, lambda E, a=ra, b2=rb, sh=sh: E.tensor_copy(
                    b2[:, 0:sh], a[:, 0:sh]))
                A(VE, lambda E, a=ra, b2=rb, sh=sh: E.tensor_add(
                    b2[:, sh:NE], a[:, sh:NE], a[:, 0:NE - sh]))
                ra, rb = rb, ra
            A(VE, lambda E, r=ra: E.tensor_sub(r[:], r[:], cm16[:]))
            A(VE, lambda E, r=ra: E.tensor_mul(sel1[:], cm16[:], r[:]))
            A(VE, lambda E: E.tensor_sub(sel0[:], cm16[:], sel1[:]))
            for k, sl in ((0, sel0), (1, sel1)):
                A(VE, lambda E, sl=sl: E.tensor_mul(sltmp[:], slot_tm[:],
                                                    sl[:]))
                A(VE, lambda E, tt=tt, k=k: E.reduce_sum(
                    sidx[:, 2 * tt + k:2 * tt + k + 1], sltmp[:], axis=AX))
                A(VE, lambda E, sl=sl, tt=tt: E.tensor_mul(
                    sltmp[:], cw16[:, tt * NE:(tt + 1) * NE], sl[:]))
                A(VE, lambda E, tt=tt, k=k: E.reduce_sum(
                    cab[:, 2 * tt + k:2 * tt + k + 1], sltmp[:], axis=AX),
                  inc="gsx")
        for tt in range(NQT):
            for k in range(2):
                n = tt * 2 + k
                if n >= 1:
                    W(VE, "gsct", 16 * n)
                A(VE, lambda E, tt=tt: E.tensor_copy(
                    tokrow[:, 0:1], tokid_s[:, tt:tt + 1]))
                A(VE, lambda E, tt=tt, k=k: E.tensor_scalar(
                    tokrow[:, 1:2], tokid_s[:, tt:tt + 1], 2.0, float(k),
                    ALU.mult, ALU.add))
                A(VE, lambda E, tt=tt, k=k: E.tensor_scalar(
                    sc_i32[:], sidx[:, 2 * tt + k:2 * tt + k + 1],
                    float(NSLOT), None, ALU.min), inc="gsc1")
                W(GP, "gsc1", pg.cur("gsc1"))
                W(GP, "cld")
                A(GP, lambda E: E.indirect_dma_start(
                    out=slotmap,
                    out_offset=IndirectOffsetOnAxis(sc_i32[:], 0),
                    in_=tokrow[:, 0:2], in_offset=None),
                  inc="gsct", inck=16)
    if debug:
        W(SY, "gsx", pg.cur("gsx"))
        A(SY, lambda E: E.dma_start(out=dbg["d_cw"], in_=cw16[:]),
          inc="dbc", inck=16)
        A(SY, lambda E: E.dma_start(out=dbg["d_sidx"], in_=sidx[:]),
          inc="dbc", inck=16)
        A(SY, lambda E: E.dma_start(out=dbg["d_cab"], in_=cab[:]),
          inc="dbc", inck=16)

    if phase_limit <= 6:
        for _lbl in ("lnw", "dbq", "dbk", "dba", "dbc", "dbg1", "xfw",
                     "gwr", "ssp", "cld"):
            W(SY, _lbl, pg.cur(_lbl))
        mem.commit()
        with nc.allow_low_precision(reason="dev"):
            pg.run()
        return nc
    PHASE_GUARD("ssp", "dbc")
    def load_w1(e, wait_sem, wait_val):
        src = sw1_d if e < 0 else w1_d[e]
        if wait_val:
            W(GP, wait_sem, wait_val)
        for c in range(DC):
            A(GP, lambda E, c=c, src=src: E.dma_start(
                out=w1_sb[:, c * HFF:(c + 1) * HFF],
                in_=src[c * P:(c + 1) * P, :]), inc="mw1", inck=16)

    def load_w2(e, wait_sem, wait_val):
        src = sw2_d if e < 0 else w2_d[e]
        if wait_val:
            W(GP, wait_sem, wait_val)
        for c in range(HC):
            A(GP, lambda E, c=c, src=src: E.dma_start(
                out=w2_sb[:, c * D:(c + 1) * D],
                in_=src[c * P:(c + 1) * P, :]), inc="w2c", inck=16)

    W(GP, "wev", pg.cur("wev"))
    W(GP, "gsx", pg.cur("gsx"))
    W(GP, "ssp", pg.cur("ssp"))
    load_w1(-1, None, 0)
    with nc.psum_tensor("psH0", [P, 1024], F32) as psH0, \
         nc.psum_tensor("psH1", [P, 1024], F32) as psH1, \
         nc.psum_tensor("psP0", [P, 512], F32) as psP0, \
         nc.psum_tensor("psP1", [P, 512], F32) as psP1, \
         nc.psum_tensor("psXa", [P, 1024], BF16) as psXa, \
         nc.psum_tensor("psXb", [P, 1024], BF16) as psXb:
        W(SY, "xfw", pg.cur("xfw"))
        W(SY, "gsx", pg.cur("gsx"))
        W(SY, "gsct", pg.cur("gsct"))
        W(TE, "gsl", NQT)
        W(TE, "gte", NQT)
        W(TE, "gsg", NQT)
        for hc in range(HC):
            ps = psH0 if hc % 2 == 0 else psH1
            W(TE, "mw1", 16 * DC)
            if hc >= 2:
                W(TE, "shev", hc - 1)
            for c in range(DC):
                n = hc * DC + c
                if n >= 2:
                    W(SY, "shmm", n - 1)
                A(SY, lambda E, c=c: E.dma_start(
                    out=xnr[c % 2][:], in_=xn_fm_dr[c]), inc="xnl", inck=16)
                W(TE, "xnl", 16 * (n + 1))
                for n2 in range(2):
                    A(TE, lambda E, c=c, hc=hc, n2=n2, ps=ps: E.matmul(
                        ps[:, n2 * 512:(n2 + 1) * 512],
                        w1_sb[:, c * HFF + hc * 128:
                              c * HFF + (hc + 1) * 128],
                        xnr[c % 2][:, n2 * 512:(n2 + 1) * 512],
                        start=(c == 0), stop=(c == DC - 1)),
                      inc="shmm" if n2 == 1 else None)
            W(SC, "shmm", pg.cur("shmm"))
            A(SC, lambda E, ps=ps, hc=hc: E.activation(
                h1s[:, hc * 1024:(hc + 1) * 1024], ps[:], AF.Silu),
              inc="shev")
        load_w2(-1, None, 0)
        W(TE, "shev", pg.cur("shev"))
        for tt in range(NQT):
            for og in range(4):
                idx = tt * 4 + og
                ps = psP0 if idx % 2 == 0 else psP1
                W(TE, "w2c", 16 * HC)
                if idx >= 2:
                    W(TE, "shp2", idx - 1)
                for c in range(HC):
                    A(TE, lambda E, c=c, tt=tt, og=og, ps=ps: E.matmul(
                        ps[:],
                        h1s[:, c * 1024 + tt * 128:
                            c * 1024 + (tt + 1) * 128],
                        w2_sb[:, c * D + og * 512: c * D + (og + 1) * 512],
                        start=(c == 0), stop=(c == HC - 1)),
                      inc="shm2" if c == HC - 1 else None)
                W(SC, "shm2", pg.cur("shm2"))
                if tt >= 2 and og == 0:
                    W(SC, "gwr", 16 * (tt - 1))
                A(SC, lambda E, ps=ps, tt=tt, og=og: E.copy(
                    gp_t[tt % 2][:, og * 512:(og + 1) * 512],
                    ps[:]), inc="shp2")
            W(SY, "shp2", pg.cur("shp2"))
            A(SY, lambda E, tt=tt: E.dma_start(
                out=gpre_dr[tt], in_=gp_t[tt % 2][:]), inc="gwr", inck=16)
        load_w1(0, "shmm", pg.cur("shmm"))

        xngTs = (xngT, xngT1)
        psXs2 = (psXa, psXb)

        def fetch(e):
            xT = xngTs[e % 2]
            psX = psXs2[e % 2]
            for st in range(2):
                n = 2 * e + st
                W(SY, "gsct", pg.cur("gsct"))
                if n >= 1:
                    W(SY, "mti", n)
                W(GP, "lnw" if n == 0 else "mti", 16 * 8 if n == 0 else 0)
                A(SY, lambda E, e=e, st=st: E.dma_start(
                    out=tokrow[:, 0:2],
                    in_=slotmap[e * CAP + st * 128:
                                e * CAP + (st + 1) * 128, :]),
                  inc="mtr", inck=16)
                W(VE, "mtr", 16 * (n + 1))
                if n >= 1:
                    W(VE, "mgth", 16 * n)
                A(VE, lambda E: E.tensor_scalar(
                    tokrow_i[:, 0:1], tokrow[:, 0:1], float(TOWN - 1),
                    None, ALU.min), inc="mti")
                W(GP, "mti", pg.cur("mti"))
                if e >= 1:
                    W(GP, "mxt", 4 * e)
                A(GP, lambda E, st=st: E.indirect_dma_start(
                    out=xng[st][:], out_offset=None, in_=xn_tm_dr,
                    in_offset=IndirectOffsetOnAxis(tokrow_i[:, 0:1], 0)),
                  inc="mgth", inck=16)
            first_evac = True
            for st in range(2):
                for g in range(2):
                    gi = 4 * e + 2 * st + g
                    W(TE, "mgth", 16 * (2 * e + st + 1))
                    if 2 * st + g == 0:
                        if e >= 2:
                            W(TE, "mxe", 4 * (e - 1))
                    else:
                        W(TE, "mxe", gi)
                    for c in range(8):
                        cc = g * 8 + c
                        A(TE, lambda E, st=st, cc=cc, c=c, ps=psX:
                          E.transpose(
                            ps[:, c * 128:(c + 1) * 128],
                            xng[st][:, cc * 128:(cc + 1) * 128],
                            ident[:]), inc="mxt" if c == 7 else None)
                    W(VE, "mxt", pg.cur("mxt"))
                    if first_evac and e >= 2:
                        W(VE, "memm", HC * (e - 1))
                        first_evac = False
                    A(VE, lambda E, st=st, g=g, ps=psX, xT=xT:
                      E.tensor_copy(
                        xT[:].rearrange("p (c s) -> p c s", s=CAP)
                            [:, g * 8:(g + 1) * 8,
                             st * 128:(st + 1) * 128],
                        ps[:].rearrange("p (c s) -> p c s", s=128)),
                      inc="mxe")

        fetch(0)
        for e in range(NE):
            xngT_e = xngTs[e % 2]
            for hc in range(HC):
                ps = psP0 if hc % 2 == 0 else psP1
                W(TE, "mw1", 16 * DC * (e + 2))
                W(TE, "mxe", 4 * (e + 1))
                if hc == 0:
                    if e == 0:
                        W(TE, "shp2", pg.cur("shp2"))
                    else:
                        W(TE, "meo2", 8 * e)
                if e * HC + hc >= 2:
                    W(TE, "meev", e * HC + hc - 1)
                for c in range(DC):
                    A(TE, lambda E, c=c, hc=hc, ps=ps, xT=xngT_e: E.matmul(
                        ps[:, 0:CAP],
                        w1_sb[:, c * HFF + hc * 128:
                              c * HFF + (hc + 1) * 128],
                        xT[:, c * CAP:(c + 1) * CAP],
                        start=(c == 0), stop=(c == DC - 1)),
                      inc="memm" if c == DC - 1 else None)
                W(SC, "memm", pg.cur("memm"))
                A(SC, lambda E, ps=ps, hc=hc: E.activation(
                    h1e[:, hc * CAP:(hc + 1) * CAP], ps[:, 0:CAP],
                    AF.Silu), inc="meev")
            if e + 1 < NE:
                fetch(e + 1)
            load_w2(e, "shm2" if e == 0 else "mem2",
                    pg.cur("shm2") if e == 0 else pg.cur("mem2"))
            if e + 1 < NE:
                load_w1(e + 1, "memm", pg.cur("memm"))
            for st in range(2):
                for og in range(4):
                    idx = st * 4 + og
                    ps = psP0 if idx % 2 == 0 else psP1
                    W(TE, "w2c", 16 * HC * (e + 2))
                    W(TE, "meev", (e + 1) * HC)
                    if 8 * e + idx >= 2:
                        W(TE, "meo2", 8 * e + idx - 1)
                    for c in range(HC):
                        A(TE, lambda E, c=c, st=st, og=og, ps=ps: E.matmul(
                            ps[:],
                            h1e[:, c * CAP + st * 128:
                                c * CAP + st * 128 + 128],
                            w2_sb[:, c * D + og * 512:
                                  c * D + (og + 1) * 512],
                            start=(c == 0), stop=(c == HC - 1)),
                          inc="mem2" if c == HC - 1 else None)
                    W(VE, "mem2", pg.cur("mem2"))
                    if 2 * e + st >= 1:
                        W(VE, "msct", 16 * (2 * e + st))
                    A(VE, lambda E, ps=ps, og=og: E.tensor_copy(
                        out2[:, og * 512:(og + 1) * 512], ps[:]),
                      inc="meo2")
                n2v = 2 * e + st
                W(SY, "mti2", n2v)
                A(SY, lambda E, e=e, st=st: E.dma_start(
                    out=tokrow[:, 2:4],
                    in_=slotmap[e * CAP + st * 128:
                                e * CAP + (st + 1) * 128, :]),
                  inc="mtr2", inck=16)
                W(VE, "mtr2", 16 * (n2v + 1))
                A(VE, lambda E: E.tensor_scalar(
                    tokrow_i[:, 1:2], tokrow[:, 3:4], float(2 * TOWN),
                    None, ALU.min), inc="mti2")
                W(GP, "mti2", pg.cur("mti2"))
                W(GP, "meo2", pg.cur("meo2"))
                A(GP, lambda E: E.indirect_dma_start(
                    out=ab_dr,
                    out_offset=IndirectOffsetOnAxis(tokrow_i[:, 1:2], 0),
                    in_=out2[:], in_offset=None),
                  inc="msct", inck=16)
    if debug:
        W(SY, "gwr", pg.cur("gwr"))
        for tt in range(NQT):
            A(SY, lambda E, tt=tt: E.dma_start(
                out=dbg["d_gpre"][tt], in_=gpre_dr[tt]),
              inc="dbg1", inck=16)

    if phase_limit <= 7:
        for _lbl in ("lnw", "dbq", "dbk", "dba", "dbc", "dbg1", "xfw",
                     "gwr", "ssp", "cld"):
            W(SY, _lbl, pg.cur(_lbl))
        mem.commit()
        with nc.allow_low_precision(reason="dev"):
            pg.run()
        return nc
    PHASE_GUARD("msct", "gwr", "dbg1", "mtr", "mtr2", "mgth")
    W(SY, "msct", pg.cur("msct"))
    W(SY, "gwr", pg.cur("gwr"))
    W(SY, "ssp", pg.cur("ssp"))
    W(GP, "msct", pg.cur("msct"))
    for tt in range(NQT):
        if tt >= 1:
            W(SY, "fo", tt)
            W(GP, "fo", tt)
        A(GP, lambda E, tt=tt: E.dma_start(
            out=ab_s[:, 0:D],
            in_=ab_dr[0:2 * TOWN, :].rearrange("(t k) d -> t k d", k=2)
                [tt * 128:(tt + 1) * 128, 0, :]), inc="flg", inck=16)
        A(GP, lambda E, tt=tt: E.dma_start(
            out=ab_s[:, D:2 * D],
            in_=ab_dr[0:2 * TOWN, :].rearrange("(t k) d -> t k d", k=2)
                [tt * 128:(tt + 1) * 128, 1, :]), inc="flg", inck=16)
        A(SY, lambda E, tt=tt: E.dma_start(out=s_rs[:], in_=s_dr[tt]),
          inc="fl", inck=16)
        A(SY, lambda E, tt=tt: E.dma_start(
            out=x_rs[:], in_=x_win[(8 + tt) * P:(9 + tt) * P, :]),
          inc="fl", inck=16)
        A(SY, lambda E, tt=tt: E.dma_start(out=gp8[:], in_=gpre_dr[tt]),
          inc="fl", inck=16)
        W(VE, "fl", pg.cur("fl"))
        W(VE, "flg", pg.cur("flg"))
        if tt >= 1:
            W(VE, "fst", 16 * tt)
        A(VE, lambda E, tt=tt: E.scalar_tensor_tensor(
            fin0[:], ab_s[:, 0:D], cab[:, 2 * tt:2 * tt + 1],
            gp8[:], ALU.mult, ALU.add))
        A(VE, lambda E, tt=tt: E.scalar_tensor_tensor(
            fin0[:], ab_s[:, D:2 * D], cab[:, 2 * tt + 1:2 * tt + 2],
            fin0[:], ALU.mult, ALU.add), inc="fg")
        W(SC, "fg", pg.cur("fg"))
        A(SC, lambda E: E.activation(fin1[:], fin0[:], AF.Silu), inc="fs")
        W(VE, "fs", pg.cur("fs"))
        A(VE, lambda E: E.tensor_mul(fin0[:], s_rs[:], fin1[:]))
        A(VE, lambda E: E.tensor_add(fin0[:], fin0[:], x_rs[:]), inc="fo")
        W(SY, "fo", pg.cur("fo"))
        A(SY, lambda E, tt=tt: E.dma_start(
            out=o_tm[tt * 128:(tt + 1) * 128, :], in_=fin0[:]),
          inc="fst", inck=16)
    W(SY, "fst", pg.cur("fst"))
    if debug:
        for s in ("dbq", "dbk", "dba", "dbc", "dbg1", "lnw"):
            W(SY, s, pg.cur(s))

    mem.commit()
    with nc.allow_low_precision(reason="bf16 matmul operands with fp32 accum; "
                                "tolerance spec allows bf16 matmuls"):
        pg.run()
    return nc


_NC_CACHE = {}


def kernel(L0_attn_x, L0_attn_wq, L0_attn_wk, L0_attn_wv, L0_attn_wo,
           L0_attn_q_norm_w, L0_attn_k_norm_w, L0_ffn_gate_w, L0_ffn_w1,
           L0_ffn_w2, L0_ffn_sw1, L0_ffn_sw2, _debug=False):
    x = np.asarray(L0_attn_x, np.float32)
    B, T, _ = x.shape
    xr = np.ascontiguousarray(x.reshape(B * T, D))

    wqT = np.ascontiguousarray(np.asarray(L0_attn_wq, np.float32).T)
    wkT = np.ascontiguousarray(np.asarray(L0_attn_wk, np.float32).T)
    wvT = np.ascontiguousarray(np.asarray(L0_attn_wv, np.float32).T)
    woT = np.ascontiguousarray(np.asarray(L0_attn_wo, np.float32).T)
    gateT = np.ascontiguousarray(np.asarray(L0_ffn_gate_w, np.float32).T)
    w1 = np.ascontiguousarray(np.asarray(L0_ffn_w1, np.float32))
    w2 = np.ascontiguousarray(np.asarray(L0_ffn_w2, np.float32))
    sw1 = np.ascontiguousarray(np.asarray(L0_ffn_sw1, np.float32)[0])
    sw2 = np.ascontiguousarray(np.asarray(L0_ffn_sw2, np.float32)[0])
    qnw_c = np.ascontiguousarray(
        np.asarray(L0_attn_q_norm_w, np.float32).reshape(DC, P).T)
    knw_c = np.zeros((P, 8), np.float32)
    knw_c[0:64, :] = np.asarray(L0_attn_k_norm_w, np.float32).reshape(8, 64).T

    key = "dbg" if _debug else "main"
    if key not in _NC_CACHE:
        _NC_CACHE[key] = build_nc(debug=_debug)
    nc = _NC_CACHE[key]

    in_maps = []
    for c in range(8):
        t0 = c * TOWN
        xw = np.zeros((TCTX, D), np.float32)
        if c % 4 == 0:
            xw[1024:] = xr[t0:t0 + TOWN]
        else:
            xw[:] = xr[t0 - 1024:t0 + TOWN]
        valid = np.zeros((P, P), np.float32)
        for st in range(NKB):
            v = 1.0 if (st >= 8 or c % 4 != 0) else 0.0
            valid[:, st * 8:(st + 1) * 8] = v
        in_maps.append({
            "x_win": xw, "wqT": wqT, "wkT": wkT, "wvT": wvT, "woT": woT,
            "gateT": gateT, "w1": w1, "w2": w2, "sw1": sw1, "sw2": sw2,
            "qnw_c": qnw_c, "knw_c": knw_c, "ctxv": valid,
            "bcolbase": _CONSTS["bcolbase"], "crow_all": _CONSTS["crow_all"],
            "e256": _CONSTS["e256"], "e832": _CONSTS["e832"],
            "hsel": _CONSTS["hsel"], "ksel": _CONSTS["ksel"],
            "tokid8": _CONSTS["tokid8"], "oobinit": _CONSTS["oobinit"],
        })

    res = run_bass_kernel_spmd(nc, in_maps, core_ids=list(range(8)))
    out = np.concatenate([np.asarray(res.results[c]["o_tm"])
                          for c in range(8)], axis=0)
    if _debug:
        return out.reshape(B, T, D), res
    return out.reshape(B, T, D)
